# version 16
# baseline (speedup 1.0000x reference)
import contextlib
import numpy as np
import ml_dtypes
import concourse.bass as bass
import concourse.mybir as mybir
from concourse.bass_utils import run_bass_kernel_spmd

F32 = mybir.dt.float32
BF16 = mybir.dt.bfloat16
F16 = mybir.dt.float16
AF = mybir.ActivationFunctionType
ALU = mybir.AluOpType
AX = mybir.AxisListType

S = 2048
D = 1024
TB = S // 128
DB = D // 128
LN_EPS = 1e-5
RMS_EPS = 1.1920928955078125e-07
ALIBI_SLOPES = [2.0 ** (-8.0 / n) for n in range(16, 0, -1)]

C_SUM, C_SSQ, C_NMU, C_R, C_T4, C_T5, C_T6, C_T7, C_R1 = (i * 16 for i in range(9))


class Ctr:
    def __init__(self, sem):
        self.sem = sem
        self.n = 0
        self.base = 0

    def inc(self, instr, k=1):
        self.n += k
        return instr.then_inc(self.sem, k)


class Prog:
    def __init__(self, nc, stack):
        self.nc = nc
        self.stack = stack
        ent = stack.enter_context
        self.ld = Ctr(ent(nc.semaphore("c_ld")))
        self.lw = Ctr(ent(nc.semaphore("c_lw")))
        self.pe = Ctr(ent(nc.semaphore("c_pe")))
        self.ev = Ctr(ent(nc.semaphore("c_ev")))
        self.es = Ctr(ent(nc.semaphore("c_es")))
        self.eg = Ctr(ent(nc.semaphore("c_eg")))
        self.st = Ctr(ent(nc.semaphore("c_st")))
        self.se = Ctr(ent(nc.semaphore("c_se")))
        self.so = Ctr(ent(nc.semaphore("c_so")))
        self.e2 = Ctr(ent(nc.semaphore("c_e2")))
        self.s2 = Ctr(ent(nc.semaphore("c_s2")))
        self.le = Ctr(ent(nc.semaphore("c_le")))
        self.lo = Ctr(ent(nc.semaphore("c_lo")))
        self.lw2 = Ctr(ent(nc.semaphore("c_lw2")))
        self.lw3 = Ctr(ent(nc.semaphore("c_lw3")))
        self.ps4 = ent(nc.psum_tensor("ps4", [128, 4, 512], F32))
        self.psq = ent(nc.psum_tensor("psq", [128, 4, 512], F32))
        self.stats = ent(nc.sbuf_tensor("g_stats", [128, 160], F32))
        self.sqscr = ent(nc.sbuf_tensor("g_sqscr", [128, 2, D], F32))
        self.cvals = [LN_EPS, RMS_EPS, -0.5, -1.0, 0.125, 2.0, 1.0, 0.0, 1e-6,
                      0.5, -2.0, 1.0 / D, -1.0 / D]
        self.cpool = ent(nc.sbuf_tensor("cpool", [128, len(self.cvals)], F32))

    def cval(self, v):
        i = self.cvals.index(v)
        return self.cpool[:, i:i + 1]

    def emit_const_init(self):
        with self.block() as blk:
            @blk.gpsimd
            def _(g):
                for i, v in enumerate(self.cvals):
                    g.memset(self.cpool[:, i:i + 1], float(v))

    def dump(self, name, ap):
        shape = list(ap.shape)
        d = self.nc.dram_tensor("dbg_" + name, shape, ap.dtype, kind="ExternalOutput").ap()
        with self.block() as blk:
            @blk.sync
            def _(sy):
                self.st.inc(sy.dma_start(out=d, in_=ap), 16)
                sy.wait_ge(self.st.sem, self.st.n)

    def sb(self, name, shape, dtype):
        return self.stack.enter_context(self.nc.sbuf_tensor(name, shape, dtype))

    def alloc(self, name, shape, dtype, side="left"):
        g = self.nc.sbuf_tensor(name, shape, dtype, side=side)
        t = g.__enter__()
        b = _Buf(t, g, self, side)
        if not hasattr(self, "_live"):
            self._live = {"left": [], "right": []}
        self._live[side].append(b)
        return b

    def free_live(self):
        for side in ("right", "left"):
            for b in reversed(self._live.get(side, []) if hasattr(self, "_live") else []):
                b._g.__exit__(None, None, None)
            if hasattr(self, "_live"):
                self._live[side] = []

    @contextlib.contextmanager
    def block(self):
        with self.nc.Block() as blk:
            yield blk


class _Buf:
    def __init__(self, t, g, p, side):
        self.t = t
        self._g = g
        self._p = p
        self._side = side

    def __getitem__(self, k):
        return self.t[k]

    def free(self):
        self._g.__exit__(None, None, None)
        self._p._live[self._side].remove(self)



def make_consts():
    c = {}
    c["c_ident"] = np.eye(128, dtype=ml_dtypes.bfloat16)
    pp = np.arange(128)[:, None]
    ff = np.arange(512)[None, :]
    offs = np.arange(-15, 13) * 128
    c["c_alibi"] = np.stack([np.abs(o + ff - pp) for o in offs]).astype(np.float16)
    c["c_causal"] = np.stack([((ff - pp - 128 * k) >= 0) for k in range(4)]).astype(ml_dtypes.bfloat16)
    HEAD_DIM = 64
    inv_freq = 1.0 / 10000.0 ** (np.arange(0, HEAD_DIM, 2, dtype=np.float32) / HEAD_DIM)
    freqs = np.arange(S, dtype=np.float32)[:, None] * inv_freq[None, :]
    emb = np.concatenate([freqs, freqs], axis=-1)
    c["c_cos"] = np.cos(emb).astype(np.float32)
    sin = np.sin(emb)
    sin_s = sin.copy()
    sin_s[:, :32] *= -1.0
    c["c_sin"] = sin_s.astype(np.float32)
    sel = np.zeros((64, 64, 128), dtype=ml_dtypes.bfloat16)
    for r in range(64):
        sel[r, r, :] = 1
    c["c_sel64"] = sel.reshape(64, 64 * 128)
    return c


CONST_DT = {np.dtype(np.float32): F32, np.dtype(np.float16): F16,
            np.dtype(ml_dtypes.bfloat16): BF16}



def emit_loadw(p, g, dst, src):
    s3 = src.rearrange("(a p) n -> p a n", p=128)
    for a in range(s3.shape[1]):
        p.lw.inc(g.dma_start(out=dst[:, a, :], in_=s3[:, a, :]), 16)


def emit_loadw2(p, g, ctr, dst, src):
    s3 = src.rearrange("(a p) n -> p a n", p=128)
    for a in range(s3.shape[1]):
        ctr.inc(g.dma_start(out=dst[:, a, :], in_=s3[:, a, :]), 16)


def emit_loadcol(p, g, dst, src):
    p.lw.inc(g.dma_start(out=dst[:], in_=src.rearrange("(b p) -> p b", p=128),
                         allow_slow_non_contiguous=True), 16)


def emit_transposes(p, blk, src3, dst3, ntb, ndb, ident, src_off=0):
    pe0, ev0, es0 = p.pe.n, p.ev.n, p.es.n
    tiles = [(db, tb) for db in range(ndb) for tb in range(ntb)]
    pairs = [tiles[i:i + 2] for i in range(0, len(tiles), 2)]
    bview = p.ps4[:].bitcast(BF16)

    @blk.tensor
    def _(t):
        for pi, pr in enumerate(pairs):
            if pi >= 2:
                who, cnt = divmod(pi - 2, 2)
                if (pi - 2) % 2 == 0:
                    t.wait_ge(p.ev.sem, ev0 + (pi - 2) // 2 + 1)
                else:
                    t.wait_ge(p.es.sem, es0 + (pi - 2) // 2 + 1)
            for j, (db, tb) in enumerate(pr):
                ins = t.transpose(bview[:, (2 * pi + j) % 4, 0:128],
                                  src3[:, tb, src_off + db * 128: src_off + (db + 1) * 128],
                                  ident[:])
            p.pe.inc(ins)

    @blk.vector
    def _(v):
        for pi, pr in enumerate(pairs):
            if pi % 2 != 0:
                continue
            v.wait_ge(p.pe.sem, pe0 + pi + 1)
            db0, tb0 = pr[0]
            half = (2 * pi) % 4
            p.ev.inc(v.tensor_copy(
                dst3[:, db0, tb0 * 128:(tb0 + 2) * 128].rearrange("q (a f) -> q a f", f=128),
                bview[:, half:half + 2, 0:128]))

    @blk.scalar
    def _(s_):
        for pi, pr in enumerate(pairs):
            if pi % 2 != 1:
                continue
            s_.wait_ge(p.pe.sem, pe0 + pi + 1)
            db0, tb0 = pr[0]
            half = (2 * pi) % 4
            p.es.inc(s_.copy(
                dst3[:, db0, tb0 * 128:(tb0 + 2) * 128].rearrange("q (a f) -> q a f", f=128),
                bview[:, half:half + 2, 0:128]))


def emit_seq(p, blk, steps):
    cnt = {"vector": p.ev, "scalar": p.es, "gpsimd": p.eg}
    base = {k: c.n for k, c in cnt.items()}
    counts = dict(base)
    pres = []
    for en, fn in steps:
        pres.append(dict(counts))
        counts[en] += 1

    for en in ("vector", "scalar", "gpsimd"):
        if not any(e == en for e, _ in steps):
            continue

        def body(eng, en=en):
            for i, (sen, fn) in enumerate(steps):
                if sen != en:
                    continue
                for oen in ("vector", "scalar", "gpsimd"):
                    if pres[i][oen] > base[oen]:
                        eng.wait_ge(cnt[oen].sem, pres[i][oen])
                cnt[en].inc(fn(eng))
        getattr(blk, en)(body)


def emit_stats(p, blk, src3, n_tiles, sq_scr, stats):
    es0, ev0 = p.es.n, p.ev.n

    @blk.scalar
    def _(s_):
        for tb in range(n_tiles):
            if tb >= 2:
                s_.wait_ge(p.ev.sem, ev0 + tb - 1)
            p.es.inc(s_.activation(sq_scr[:, tb % 2, :], src3[:, tb, :], AF.Square))

    @blk.vector
    def _(v):
        for tb in range(n_tiles):
            v.reduce_sum(stats[:, C_SUM + tb:C_SUM + tb + 1], src3[:, tb, :], AX.X)
            v.wait_ge(p.es.sem, es0 + tb + 1)
            p.ev.inc(v.reduce_sum(stats[:, C_SSQ + tb:C_SSQ + tb + 1], sq_scr[:, tb % 2, :], AX.X))


def emit_ln_algebra(p, blk, stats, eps_list, nt=TB, store_r1=False, rms=False):
    sl = lambda c: stats[:, c:c + nt]
    if rms:
        steps = [
            ("vector", lambda v: v.tensor_scalar(sl(C_NMU), sl(C_SUM), 0.0, None, ALU.mult)),
            ("vector", lambda v: v.tensor_scalar(sl(C_T4), sl(C_SSQ), 1.0 / D, None, ALU.mult)),
        ]
    else:
        steps = [
            ("vector", lambda v: v.tensor_scalar(sl(C_NMU), sl(C_SUM), -1.0 / D, None, ALU.mult)),
            ("vector", lambda v: v.tensor_scalar(sl(C_T4), sl(C_SSQ), 1.0 / D, None, ALU.mult)),
            ("vector", lambda v: v.tensor_tensor(sl(C_T5), sl(C_NMU), sl(C_NMU), ALU.mult)),
            ("vector", lambda v: v.tensor_tensor(sl(C_T4), sl(C_T4), sl(C_T5), ALU.subtract)),
        ]
    for i, eps in enumerate(eps_list):
        tgt = C_R if i == 0 else C_T6
        steps.append(("scalar", lambda s_, eps=eps: s_.activation(sl(C_T5), sl(C_T4), AF.Ln, bias=p.cval(eps))))
        steps.append(("scalar", lambda s_, tgt=tgt: s_.activation(sl(tgt), sl(C_T5), AF.Exp, scale=p.cval(-0.5))))
        if i == 0 and store_r1:
            steps.append(("scalar", lambda s_: s_.copy(sl(C_R1), sl(C_R))))
        if i > 0:
            steps.append(("vector", lambda v: v.tensor_tensor(sl(C_R), sl(C_R), sl(C_T6), ALU.mult)))
        if i < len(eps_list) - 1:
            steps.append(("scalar", lambda s_: s_.activation(sl(C_T7), sl(C_T5), AF.Exp, scale=p.cval(-1.0))))
            steps.append(("vector", lambda v: v.tensor_tensor(sl(C_T4), sl(C_T4), sl(C_T7), ALU.mult)))
    emit_seq(p, blk, steps)


def emit_epilogue(p, blk, TA, stats, xin_t, xout_t, xfull, otile, scr, out_stats=False):
    ld0, ev0, es0 = p.ld.n, p.ev.n, p.es.n
    se0, so0 = p.se.n, p.so.n
    VP = 4 if out_stats else 2

    @blk.sync
    def _(sy):
        for tb in range(TB):
            p.ld.inc(sy.dma_start(out=xfull[:, tb, :], in_=xin_t[:, tb, :]), 16)
        for tb in range(TB):
            sy.wait_ge(p.ev.sem, ev0 + VP * tb + 2)
            ctr = p.se if tb % 2 == 0 else p.so
            ctr.inc(sy.dma_start(out=xout_t[:, tb, :], in_=otile[:, tb % 2, :]), 16)
        sy.wait_ge(p.se.sem, p.se.n)
        sy.wait_ge(p.so.sem, p.so.n)

    if out_stats:
        @blk.scalar
        def _(s_):
            for tb in range(TB):
                s_.wait_ge(p.ev.sem, ev0 + VP * tb + 2)
                p.es.inc(s_.activation(scr[:, tb % 2, :], otile[:, tb % 2, :], AF.Square))

    @blk.vector
    def _(v):
        v.wait_ge(p.ld.sem, ld0 + 16 * TB)
        for tb in range(TB):
            if tb >= 2:
                ctr0, base = (p.se, se0) if tb % 2 == 0 else (p.so, so0)
                v.wait_ge(ctr0.sem, base + 16 * (tb // 2))
            if out_stats and tb >= 2:
                v.wait_ge(p.es.sem, es0 + tb - 1)
            p.ev.inc(v.tensor_scalar(scr[:, tb % 2, :], TA[:, tb, :],
                                     stats[:, C_NMU + tb:C_NMU + tb + 1],
                                     stats[:, C_R + tb:C_R + tb + 1], ALU.add, ALU.mult))
            NOEDGE or v.wait_ge(p.ev.sem, p.ev.n)
            p.ev.inc(v.tensor_tensor(otile[:, tb % 2, :], xfull[:, tb, :],
                                     scr[:, tb % 2, :], ALU.add))
            if out_stats:
                NOEDGE or v.wait_ge(p.ev.sem, p.ev.n)
                p.ev.inc(v.reduce_sum(stats[:, C_SUM + tb:C_SUM + tb + 1],
                                      otile[:, tb % 2, :], AX.X))
                v.wait_ge(p.es.sem, es0 + tb + 1)
                p.ev.inc(v.reduce_sum(stats[:, C_SSQ + tb:C_SSQ + tb + 1],
                                      scr[:, tb % 2, :], AX.X))


def emit_mm_grid(p, blk, jobs, evac, psum=None, nbank=4, lag=None):
    psum = psum or p.ps4
    nb = nbank
    pe0, ev0 = p.pe.n, p.ev.n

    @blk.tensor
    def _(t):
        for ji, job in enumerate(jobs):
            if ji >= nb:
                t.wait_ge(p.ev.sem, ev0 + ji - nb + 1)
            p.pe.inc(job(t, psum[:, ji % nb, :]))

    @blk.vector
    def _(v):
        for ji in range(len(jobs)):
            v.wait_ge(p.pe.sem, pe0 + ji + 1)
            p.ev.inc(evac(v, ji, psum[:, ji % nb, :]))



def emit_ln_stream(p, xin_t, stats, sq_scr, XR, eps_list, dsts):
    with p.block() as blk:
        ld0 = p.ld.n

        @blk.sync
        def _(sy):
            for tb in range(TB):
                p.ld.inc(sy.dma_start(out=XR[:, tb, :], in_=xin_t[:, tb, :]), 16)

        es0, ev0 = p.es.n, p.ev.n

        @blk.scalar
        def _(s_):
            s_.wait_ge(p.ld.sem, ld0 + 16 * TB)
            for tb in range(TB):
                if tb >= 2:
                    s_.wait_ge(p.ev.sem, ev0 + tb - 1)
                p.es.inc(s_.activation(sq_scr[:, tb % 2, :], XR[:, tb, :], AF.Square))

        @blk.vector
        def _(v):
            v.wait_ge(p.ld.sem, ld0 + 16 * TB)
            for tb in range(TB):
                v.reduce_sum(stats[:, C_SUM + tb:C_SUM + tb + 1], XR[:, tb, :], AX.X)
                v.wait_ge(p.es.sem, es0 + tb + 1)
                p.ev.inc(v.reduce_sum(stats[:, C_SSQ + tb:C_SSQ + tb + 1],
                                      sq_scr[:, tb % 2, :], AX.X))

    with p.block() as blk:
        emit_ln_algebra(p, blk, stats, eps_list)

    with p.block() as blk:
        @blk.vector
        def _(v):
            for dst in dsts:
                for tb in range(TB):
                    p.ev.inc(v.tensor_scalar(dst[:, tb, :], XR[:, tb, :],
                                             stats[:, C_NMU + tb:C_NMU + tb + 1],
                                             stats[:, C_R + tb:C_R + tb + 1],
                                             ALU.add, ALU.mult))


import os
P1_STOP = int(os.environ.get("P1_STOP", "99"))
DBG = int(os.environ.get("P1_DBG", "0"))
NOEDGE = int(os.environ.get("NOEDGE", "0"))
SCANEDGE_OFF = int(os.environ.get("SCANEDGE_OFF", "1"))


def emit_p1(p, xin, xout, W, csb):
    nc = p.nc
    xin_t = xin.rearrange("(tb p) d -> p tb d", p=128)
    xout_t = xout.rearrange("(tb p) d -> p tb d", p=128)
    ident = csb["ident"]

    al = p.alloc
    gcol = al("p1_gcol", [128, DB], F32)
    bcol = al("p1_bcol", [128, DB], F32)
    bcol_bf = al("p1_bcolb", [128, DB], BF16)
    bWq = al("p1_bWq", [128, DB], F32)
    bWk = al("p1_bWk", [128, DB], F32)
    stats = p.stats
    sq_scr = p.sqscr
    rowscr = sq_scr[:].bitcast(BF16)[0:1, 0, 0:D]
    bWv_b = sq_scr[:].bitcast(BF16)[:, 1, 0:D]

    wq = al("p1_wq", [128, DB, D], BF16, side="right")
    wk = al("p1_wk", [128, DB, D], BF16, side="right")
    wv = al("p1_wv", [128, DB, D], BF16, side="right")
    xn_fm = al("p1_xn_fm", [128, DB, S], BF16, side="right")
    xn_tm = al("p1_xn_tm", [128, TB, D], BF16, side="right")
    XR = al("p1_XR", [128, TB, D], F32, side="right")

    with p.block() as blk:
        @blk.gpsimd
        def _(g):
            emit_loadw(p, g, wq, W["Wq"])
            emit_loadw(p, g, wk, W["Wk"])
            emit_loadw(p, g, wv, W["Wv"])
            emit_loadcol(p, g, gcol, W["gamma"])
            emit_loadcol(p, g, bcol, W["beta"])
            g.wait_ge(p.lw.sem, p.lw.n)

    emit_ln_stream(p, xin_t, stats, sq_scr, XR, [LN_EPS, LN_EPS, LN_EPS], [xn_tm])
    XR.free()

    if P1_STOP <= 1:
        return p.free_live()
    with p.block() as blk:
        emit_transposes(p, blk, xn_tm, xn_fm, TB, DB, ident)

        @blk.scalar
        def _(s_):
            s_.copy(bcol_bf[:], bcol[:])
    xn_tm.free()

    if DBG:
        p.dump("xn_tm", xn_tm[:])
    if P1_STOP <= 2:
        return p.free_live()
    with p.block() as blk:
        pe0, ev0 = p.pe.n, p.ev.n

        @blk.tensor
        def _(t):
            ji = 0
            for w in (wq, wk):
                for db in range(DB):
                    bank = p.ps4[:, ji % 4, :]
                    if ji >= 4:
                        t.wait_ge(p.ev.sem, ev0 + ji - 3)
                    for kb in range(DB):
                        ins = t.matmul(bank[:, 0:1], w[:, kb, db * 128:(db + 1) * 128],
                                       bcol_bf[:, kb:kb + 1], start=(kb == 0), stop=(kb == DB - 1))
                    p.pe.inc(ins)
                    ji += 1
            for ncx in range(2):
                bank = p.ps4[:, ji % 4, :]
                t.wait_ge(p.ev.sem, ev0 + ji - 3)
                for kb in range(DB):
                    ins = t.matmul(bank[0:1, :], bcol_bf[:, kb:kb + 1],
                                   wv[:, kb, ncx * 512:(ncx + 1) * 512],
                                   start=(kb == 0), stop=(kb == DB - 1))
                p.pe.inc(ins)
                ji += 1

        @blk.vector
        def _(v):
            ji = 0
            for dst in (bWq, bWk):
                for db in range(DB):
                    v.wait_ge(p.pe.sem, pe0 + ji + 1)
                    p.ev.inc(v.tensor_copy(dst[:, db:db + 1], p.ps4[:, ji % 4, 0:1]))
                    ji += 1
            for ncx in range(2):
                v.wait_ge(p.pe.sem, pe0 + ji + 1)
                p.ev.inc(v.tensor_copy(rowscr[:, ncx * 512:(ncx + 1) * 512],
                                       p.ps4[0:1, ji % 4, :]))
                ji += 1

    with p.block() as blk:
        pe0, ev0 = p.pe.n, p.ev.n

        @blk.vector
        def _(v):
            for w in (wq, wk, wv):
                for kb in range(DB):
                    v.tensor_scalar(w[:, kb, :], w[:, kb, :], gcol[:, kb:kb + 1],
                                    None, ALU.mult)
            for ncx in range(2):
                v.wait_ge(p.pe.sem, pe0 + ncx + 1)
                p.ev.inc(v.tensor_copy(bWv_b[:, ncx * 512:(ncx + 1) * 512],
                                       p.ps4[:, ncx, :]))

        @blk.tensor
        def _(t):
            for ncx in range(2):
                p.pe.inc(t.matmul(p.ps4[:, ncx, :], csb["ones1x128"][:],
                                  rowscr[:, ncx * 512:(ncx + 1) * 512],
                                  start=True, stop=True))

    if DBG:
        p.dump("xn_fm", xn_fm[:])
        p.dump("stats", stats[:])
    if P1_STOP <= 3:
        return p.free_live()
    qf = al("p1_qf", [128, DB, S], BF16)
    kf = al("p1_kf", [128, DB, S], BF16)
    vt = al("p1_vt", [128, TB, 16, 65], BF16)
    with p.block() as blk:
        pe0, ev0, es0 = p.pe.n, p.ev.n, p.es.n
        jobs = ([("q", db, tc) for db in range(DB) for tc in range(4)]
                + [("k", db, tc) for db in range(DB) for tc in range(4)]
                + [("v", tb, ncx) for tb in range(TB) for ncx in range(2)])

        @blk.gpsimd
        def _(g):
            g.memset(vt[:, :, :, 64], 1.0)

        @blk.tensor
        def _(t):
            for ji, job in enumerate(jobs):
                bank = p.ps4[:, ji % 4, :]
                if ji >= 4:
                    prev = ji - 4
                    if prev < 64:
                        t.wait_ge(p.es.sem, es0 + prev + 1)
                    else:
                        t.wait_ge(p.ev.sem, ev0 + prev - 64 + 1)
                kind = job[0]
                if kind in ("q", "k"):
                    _, db, tc = job
                    w = wq if kind == "q" else wk
                    for kb in range(DB):
                        ins = t.matmul(bank, w[:, kb, db * 128:(db + 1) * 128],
                                       xn_fm[:, kb, tc * 512:(tc + 1) * 512],
                                       start=(kb == 0), stop=(kb == DB - 1))
                else:
                    _, tb, ncx = job
                    for kb in range(DB):
                        ins = t.matmul(bank, xn_fm[:, kb, tb * 128:(tb + 1) * 128],
                                       wv[:, kb, ncx * 512:(ncx + 1) * 512],
                                       start=(kb == 0), stop=(kb == DB - 1))
                p.pe.inc(ins)

        @blk.scalar
        def _(s_):
            for ji, job in enumerate(jobs[:64]):
                _, db, tc = job
                s_.wait_ge(p.pe.sem, pe0 + ji + 1)
                dst = qf if job[0] == "q" else kf
                bias = bWq if job[0] == "q" else bWk
                p.es.inc(s_.activation(dst[:, db, tc * 512:(tc + 1) * 512],
                                       p.ps4[:, ji % 4, :], AF.Identity, bias=bias[:, db:db + 1]))

        @blk.vector
        def _(v):
            for ji, job in enumerate(jobs):
                if job[0] != "v":
                    continue
                _, tb, ncx = job
                v.wait_ge(p.pe.sem, pe0 + ji + 1)
                dst = vt[:, tb, ncx * 8:(ncx + 1) * 8, 0:64]
                src = p.ps4[:, ji % 4, :].rearrange("p (h e) -> p h e", h=8)
                bsl = bWv_b[:, ncx * 512:(ncx + 1) * 512].rearrange("p (h e) -> p h e", h=8)
                p.ev.inc(v.tensor_tensor(dst, src, bsl, ALU.add))

    xn_fm.free()
    wv.free()
    wk.free()
    wq.free()

    if DBG:
        p.dump("qf", qf[:])
        p.dump("kf", kf[:])
        p.dump("vt", vt[:])
        p.dump("bWq", bWq[:])
        p.dump("bWv_b", bWv_b[:])
    if P1_STOP <= 4:
        return p.free_live()
    attn_tm = al("p1_attn_tm", [128, TB, D], BF16, side="right")
    PT = al("p1_PT", [128, 2, TB, 512], BF16, side="right")
    sT = al("p1_sT", [128, 4, 512], F32, side="right")
    rzq = al("p1_rzq", [128, 4], F32, side="right")
    alibi = al("p1_alibi", [128, 28, 512], F16, side="right")
    with p.block() as blk:
        @blk.sync
        def _(sy):
            asrc = W["c_alibi"].rearrange("a p f -> p a f")
            for a in range(0, 28, 4):
                p.ld.inc(sy.dma_start(out=alibi[:, a:a + 4, :], in_=asrc[:, a:a + 4, :]), 16)
            sy.wait_ge(p.ld.sem, p.ld.n)
    with p.block() as blk:
        pe0, ev0, es0 = p.pe.n, p.ev.n, p.es.n
        NH = 16
        NIT = NH * 4
        pe_cnt = pe0
        pe_sc = {}
        pe_pv = {}
        for it in range(NIT + 1):
            if it <= NIT - 1 and it >= 0:
                pass
        pe_order = [("S", 0)]
        for it in range(NIT):
            if it + 1 < NIT:
                pe_order.append(("S", it + 1))
            pe_order.append(("P", it))
        for kind, it in pe_order:
            if kind == "S":
                for skb in range(TB):
                    pe_cnt += 1
                    pe_sc[(it, skb)] = pe_cnt
            else:
                for sqb in range(4):
                    pe_cnt += 1
                    pe_pv[(it, sqb)] = pe_cnt
        v_order = [("T", 0)]
        for it in range(NIT):
            if it + 1 < NIT:
                v_order.append(("T", it + 1))
            v_order.append(("E", it))
        ev_cnt = ev0
        ev_stt_d = {}
        ev_evac_d = {}
        for kind, it in v_order:
            if kind == "T":
                for skb in range(TB):
                    ev_cnt += 1
                    ev_stt_d[it * 16 + skb] = ev_cnt
            else:
                for sqb in range(4):
                    ev_cnt += 1
                    ev_evac_d[it * 4 + sqb] = ev_cnt
        S_PER = 8

        def es_exp(it, skb):
            return es0 + S_PER * it + skb // 2 + 1

        def ev_stt(g):
            return ev_stt_d[g]

        def ev_evac(gq):
            return ev_evac_d[gq]

        @blk.tensor
        def _(t):
            for kind, it in pe_order:
                if kind == "S":
                    h, sqc = it // 4, it % 4
                    dbh, off = h // 2, 64 * (h % 2)
                    for skb in range(TB):
                        g = it * 16 + skb
                        if g >= 4:
                            t.wait_ge(p.ev.sem, ev_stt(g - 4))
                        t.matmul(p.ps4[:, g % 4, :],
                                 kf[off:off + 64, dbh, skb * 128:(skb + 1) * 128],
                                 qf[off:off + 64, dbh, sqc * 512:(sqc + 1) * 512],
                                 start=True, stop=True).then_inc(p.pe.sem, 1)
                else:
                    for sqb in range(4):
                        gq = it * 4 + sqb
                        if gq >= 4:
                            t.wait_ge(p.ev.sem, ev_evac(gq - 4))
                        for skb in range(TB):
                            t.wait_ge(p.es.sem, es_exp(it, skb))
                            ins = t.matmul(p.psq[:, gq % 4, 0:65],
                                           PT[:, it % 2, skb, sqb * 128:(sqb + 1) * 128],
                                           vt[:, skb, it // 4, :],
                                           start=(skb == 0), stop=(skb == TB - 1))
                        ins.then_inc(p.pe.sem, 1)
        p.pe.n = pe_cnt

        @blk.vector
        def _(v):
            for kind, it in v_order:
                h, sqc = it // 4, it % 4
                if kind == "T":
                    for skb in range(TB):
                        g = it * 16 + skb
                        v.wait_ge(p.pe.sem, pe_sc[(it, skb)])
                        if g >= 4:
                            v.wait_ge(p.es.sem, es_exp((g - 4) // 16, (g - 4) % 16))
                        dix = 4 * sqc - skb + 15
                        v.scalar_tensor_tensor(
                            sT[:, g % 4, :], alibi[:, dix, :], -8.0 * ALIBI_SLOPES[h],
                            p.ps4[:, g % 4, :], ALU.mult, ALU.add).then_inc(p.ev.sem, 1)
                else:
                    for sqb in range(4):
                        gq = it * 4 + sqb
                        tb = sqc * 4 + sqb
                        v.wait_ge(p.pe.sem, pe_pv[(it, sqb)])
                        p.e2.inc(v.reciprocal(rzq[:, sqb:sqb + 1], p.psq[:, gq % 4, 64:65]))
                        NOEDGE or v.wait_ge(p.e2.sem, p.e2.n)
                        v.tensor_scalar(attn_tm[:, tb, h * 64:(h + 1) * 64],
                                        p.psq[:, gq % 4, 0:64], rzq[:, sqb:sqb + 1],
                                        None, ALU.mult).then_inc(p.ev.sem, 1)
        p.ev.n = ev_cnt

        @blk.scalar
        def _(s_):
            for it in range(NIT):
                if it >= 2:
                    s_.wait_ge(p.pe.sem, pe_pv[(it - 2, 3)])
                for sk2 in range(8):
                    g = it * 16 + 2 * sk2
                    s_.wait_ge(p.ev.sem, ev_stt(g + 1))
                    s_.activation(PT[:, it % 2, 2 * sk2, :].rearrange("q f -> q () f")
                                  if False else PT[:, it % 2, 2 * sk2:2 * sk2 + 2, :],
                                  sT[:, g % 4:g % 4 + 2, :],
                                  AF.Exp, scale=p.cval(0.125)).then_inc(p.es.sem, 1)
        p.es.n = es0 + S_PER * NIT

    alibi.free()
    rzq.free()
    sT.free()
    PT.free()
    vt.free()
    kf.free()

    if DBG:
        p.dump("attn_tm", attn_tm[:])
        p.dump("PT", PT[:])
        p.dump("rzq", rzq[:])
    if P1_STOP <= 5:
        return p.free_live()
    attn_fm = qf
    wo = al("p1_wo", [128, DB, D], BF16)
    with p.block() as blk:
        @blk.gpsimd
        def _(g):
            emit_loadw(p, g, wo, W["Wo"])
            g.wait_ge(p.lw.sem, p.lw.n)
    with p.block() as blk:
        emit_transposes(p, blk, attn_tm, attn_fm, TB, DB, ident)
    attn_tm.free()

    TA = al("p1_TA", [128, TB, D], F32)
    with p.block() as blk:
        pe0, ev0 = p.pe.n, p.ev.n

        @blk.tensor
        def _(t):
            ji = 0
            for tb in range(TB):
                for ncx in range(2):
                    bank = p.ps4[:, ji % 4, :]
                    if ji >= 4:
                        t.wait_ge(p.ev.sem, ev0 + ji - 3)
                    for kb in range(DB):
                        ins = t.matmul(bank, attn_fm[:, kb, tb * 128:(tb + 1) * 128],
                                       wo[:, kb, ncx * 512:(ncx + 1) * 512],
                                       start=(kb == 0), stop=(kb == DB - 1))
                    p.pe.inc(ins)
                    ji += 1

        @blk.vector
        def _(v):
            ji = 0
            for tb in range(TB):
                for ncx in range(2):
                    v.wait_ge(p.pe.sem, pe0 + ji + 1)
                    p.ev.inc(v.tensor_copy(TA[:, tb, ncx * 512:(ncx + 1) * 512],
                                           p.ps4[:, ji % 4, :]))
                    ji += 1

    with p.block() as blk:
        emit_stats(p, blk, TA, TB, sq_scr, stats)
    with p.block() as blk:
        emit_ln_algebra(p, blk, stats, [LN_EPS])

    if DBG:
        p.dump("TA", TA[:])
        p.dump("stats2", stats[:])
    xfull = al("p1_xt", [128, TB, D], F32)
    otile = al("p1_ot", [128, 2, D], F32)
    with p.block() as blk:
        emit_epilogue(p, blk, TA, stats, xin_t, xout_t, xfull, otile, sq_scr,
                      out_stats=True)

    for b in (otile, xfull, TA, wo, qf, bWk, bWq,
              bcol_bf, bcol, gcol):
        b.free()



def emit_bcast_f32(p, csb, row, dst, n):
    with p.block() as blk:
        pe0 = p.pe.n
        nmm = ceil_div2(n, 512)

        @blk.tensor
        def _(t):
            for i in range(nmm):
                w = min(512, n - 512 * i)
                p.pe.inc(t.matmul(p.ps4[:, i % 4, 0:w], csb["onesf32"][:],
                                  row[0:1, 512 * i:512 * i + w], start=True, stop=True))

        @blk.vector
        def _(v):
            for i in range(nmm):
                w = min(512, n - 512 * i)
                v.wait_ge(p.pe.sem, pe0 + i + 1)
                p.ev.inc(v.tensor_copy(dst[:, 512 * i:512 * i + w], p.ps4[:, i % 4, 0:w]))


def ceil_div2(a, b):
    return -(-a // b)


def emit_ln_stats_stream(p, blk, xin_t, stats, sq_scr, xtile):
    ev0, es0 = p.ev.n, p.es.n
    le0, lo0 = p.le.n, p.lo.n

    def ldval(tb):
        return (le0 if tb % 2 == 0 else lo0) + 16 * (tb // 2 + 1)

    @blk.sync
    def _(sy):
        for tb in range(TB):
            if tb >= 2:
                sy.wait_ge(p.ev.sem, ev0 + tb - 1)
            ctr = p.le if tb % 2 == 0 else p.lo
            ctr.inc(sy.dma_start(out=xtile[:, tb % 2, :], in_=xin_t[:, tb, :]), 16)

    @blk.scalar
    def _(s_):
        for tb in range(TB):
            s_.wait_ge((p.le if tb % 2 == 0 else p.lo).sem, ldval(tb))
            if tb >= 2:
                s_.wait_ge(p.ev.sem, ev0 + tb - 1)
            p.es.inc(s_.activation(sq_scr[:, tb % 2, :], xtile[:, tb % 2, :], AF.Square))

    @blk.vector
    def _(v):
        for tb in range(TB):
            v.wait_ge((p.le if tb % 2 == 0 else p.lo).sem, ldval(tb))
            v.reduce_sum(stats[:, C_SUM + tb:C_SUM + tb + 1], xtile[:, tb % 2, :], AX.X)
            v.wait_ge(p.es.sem, es0 + tb + 1)
            p.ev.inc(v.reduce_sum(stats[:, C_SSQ + tb:C_SSQ + tb + 1],
                                  sq_scr[:, tb % 2, :], AX.X))


def emit_apply_stream(p, blk, xin_t, xtile, fn):
    ev0 = p.ev.n
    le0, lo0 = p.le.n, p.lo.n

    def ldval(tb):
        return (le0 if tb % 2 == 0 else lo0) + 16 * (tb // 2 + 1)

    @blk.sync
    def _(sy):
        for tb in range(TB):
            if tb >= 2:
                sy.wait_ge(p.ev.sem, ev0 + tb - 1)
            ctr = p.le if tb % 2 == 0 else p.lo
            ctr.inc(sy.dma_start(out=xtile[:, tb % 2, :], in_=xin_t[:, tb, :]), 16)

    @blk.vector
    def _(v):
        for tb in range(TB):
            v.wait_ge((p.le if tb % 2 == 0 else p.lo).sem, ldval(tb))
            fn(v, tb, xtile[:, tb % 2, :])



def emit_p2(p, xin, xout, W, csb, standalone=False):
    nc = p.nc
    xin_t = xin.rearrange("(tb p) d -> p tb d", p=128)
    xout_t = xout.rearrange("(tb p) d -> p tb d", p=128)
    ident = csb["ident"]
    al = p.alloc
    NKV = 4
    KVD = 256

    stats = p.stats
    sq_scr = p.sqscr
    TA = al("p2_TA", [128, TB, D], F32)
    comb = al("p2_comb", [128, TB, 4], F32)
    g1b = al("p2_g1b", [128, D], F32)
    g2b = al("p2_g2b", [128, D], F32)

    g1r = al("p2_g1r", [1, D], BF16)
    g2r = al("p2_g2r", [1, D], BF16)

    qk_tm = al("p2_qktm", [128, TB, D + KVD], BF16, side="right")
    xn1_fm = al("p2_xn1fm", [128, DB, S], BF16, side="right")
    xn1_tm = al("p2_xn1tm", [128, TB, D], BF16, side="right")
    xtile = al("p2_xtile", [128, 2, D], F32, side="right")

    with p.block() as blk:
        @blk.gpsimd
        def _(g):
            p.lw.inc(g.dma_start(out=g1r[:], in_=W["f_g1"].rearrange("(a d) -> a d", a=1)), 16)
            p.lw.inc(g.dma_start(out=g2r[:], in_=W["f_g2"].rearrange("(a d) -> a d", a=1)), 16)
            g.wait_ge(p.lw.sem, p.lw.n)



    with p.block() as blk:
        pe0 = p.pe.n

        @blk.tensor
        def _(t):
            for gi, grow in enumerate((g1r, g2r)):
                for ncx in range(2):
                    p.pe.inc(t.matmul(p.ps4[:, (2 * gi + ncx) % 4, :], csb["ones1x128"][:],
                                      grow[0:1, ncx * 512:(ncx + 1) * 512], start=True, stop=True))

        @blk.vector
        def _(v):
            for gi, gb in enumerate((g1b, g2b)):
                for ncx in range(2):
                    v.wait_ge(p.pe.sem, pe0 + 2 * gi + ncx + 1)
                    p.ev.inc(v.tensor_copy(gb[:, ncx * 512:(ncx + 1) * 512],
                                           p.ps4[:, (2 * gi + ncx) % 4, :]))

    if standalone:
        with p.block() as blk:
            emit_ln_stats_stream(p, blk, xin_t, stats, sq_scr, xtile)
    with p.block() as blk:
        emit_ln_algebra(p, blk, stats, [LN_EPS, RMS_EPS], store_r1=True)

    with p.block() as blk:
        def apply1(v, tb, xs):
            p.e2.inc(v.tensor_scalar(TA[:, tb, :], xs,
                                     stats[:, C_NMU + tb:C_NMU + tb + 1],
                                     stats[:, C_R1 + tb:C_R1 + tb + 1], ALU.add, ALU.mult))
            p.e2.inc(v.tensor_scalar(sq_scr[:, tb % 2, :], xs,
                                     stats[:, C_NMU + tb:C_NMU + tb + 1],
                                     stats[:, C_R + tb:C_R + tb + 1], ALU.add, ALU.mult))
            NOEDGE or v.wait_ge(p.e2.sem, p.e2.n)
            p.ev.inc(v.tensor_tensor(xn1_tm[:, tb, :], sq_scr[:, tb % 2, :], g1b[:], ALU.mult))
        emit_apply_stream(p, blk, xin_t, xtile, apply1)
    xtile.free()

    with p.block() as blk:
        emit_transposes(p, blk, xn1_tm, xn1_fm, TB, DB, ident)
    xn1_tm.free()
    wq2 = al("p2_wq", [128, DB, D], BF16, side="right")
    wk2 = al("p2_wk", [128, DB, KVD], BF16, side="right")
    wv2 = al("p2_wv", [128, DB, KVD], BF16, side="right")
    with p.block() as blk:
        @blk.gpsimd
        def _(g):
            emit_loadw(p, g, wq2, W["f_wq"])
            emit_loadw(p, g, wk2, W["f_wk"])
            emit_loadw(p, g, wv2, W["f_wv"])
            g.wait_ge(p.lw.sem, p.lw.n)

    cosb = al("p2_cos", [128, TB, 64], F32)
    sinb = al("p2_sin", [128, TB, 64], F32)
    vt = al("p2_vt", [128, TB, NKV, 65], BF16)
    with p.block() as blk:
        @blk.gpsimd
        def _(g):
            g.memset(vt[:, :, :, 64], 1.0)

        @blk.sync
        def _(sy):
            p.ld.inc(sy.dma_start(out=cosb[:], in_=W["c_cos"].rearrange("(tb p) e -> p tb e", p=128)), 16)
            p.ld.inc(sy.dma_start(out=sinb[:], in_=W["c_sin"].rearrange("(tb p) e -> p tb e", p=128)), 16)
            sy.wait_ge(p.ld.sem, p.ld.n)
    with p.block() as blk:
        pe0, ev0 = p.pe.n, p.ev.n
        jobs = []
        for tb in range(TB):
            jobs.append(("q", tb, 0))
            jobs.append(("q", tb, 1))
            jobs.append(("k", tb, 0))
            jobs.append(("v", tb, 0))

        @blk.tensor
        def _(t):
            for ji, (kind, tb, ncx) in enumerate(jobs):
                bank = p.ps4[:, ji % 4, :]
                if ji >= 4:
                    t.wait_ge(p.ev.sem, ev0 + ji - 3)
                w = {"q": wq2, "k": wk2, "v": wv2}[kind]
                nn = 512 if kind == "q" else KVD
                for kb in range(DB):
                    ins = t.matmul(bank[:, 0:nn], xn1_fm[:, kb, tb * 128:(tb + 1) * 128],
                                   w[:, kb, ncx * 512:ncx * 512 + nn],
                                   start=(kb == 0), stop=(kb == DB - 1))
                p.pe.inc(ins)

        @blk.vector
        def _(v):
            for ji, (kind, tb, ncx) in enumerate(jobs):
                v.wait_ge(p.pe.sem, pe0 + ji + 1)
                if ji >= 1:
                    v.wait_ge(p.ev.sem, ev0 + ji)
                bank = p.ps4[:, ji % 4, :]
                if kind == "v":
                    dst = vt[:, tb, :, 0:64]
                    src = bank[:, 0:KVD].rearrange("p (h e) -> p h e", h=NKV)
                    p.ev.inc(v.tensor_copy(dst, src))
                    continue
                nh = 8 if kind == "q" else NKV
                v3 = bank[:, 0:64 * nh].rearrange("p (h e) -> p h e", h=nh)
                cs = cosb[:, tb, :].unsqueeze(1).broadcast_to([128, nh, 64])
                sn_lo = sinb[:, tb, 0:32].unsqueeze(1).broadcast_to([128, nh, 32])
                sn_hi = sinb[:, tb, 32:64].unsqueeze(1).broadcast_to([128, nh, 32])
                tmp = sq_scr[:, 0, 0:64 * nh].rearrange("p (h e) -> p h e", h=nh)
                qc = sq_scr[:, 1, 0:64 * nh].rearrange("p (h e) -> p h e", h=nh)
                p.e2.inc(v.tensor_tensor(tmp[:, :, 0:32], v3[:, :, 32:64], sn_lo, ALU.mult))
                p.e2.inc(v.tensor_tensor(tmp[:, :, 32:64], v3[:, :, 0:32], sn_hi, ALU.mult))
                p.e2.inc(v.tensor_tensor(qc, v3, cs, ALU.mult))
                NOEDGE or v.wait_ge(p.e2.sem, p.e2.n)
                if kind == "q":
                    dst = qk_tm[:, tb, ncx * 512:(ncx + 1) * 512].rearrange(
                        "p (a s e) -> p s a e", a=4, s=2)
                    srcq = qc.rearrange("p (s a) e -> p s a e", s=2)
                    srct = tmp.rearrange("p (s a) e -> p s a e", s=2)
                    p.ev.inc(v.tensor_tensor(dst, srcq, srct, ALU.add))
                else:
                    dst = qk_tm[:, tb, D:D + KVD].rearrange("p (h e) -> p h e", h=nh)
                    p.ev.inc(v.tensor_tensor(dst, qc, tmp, ALU.add))

    wv2.free()
    wk2.free()
    wq2.free()
    xn1_fm.free()

    kfm = al("p2_kfm", [128, 2, S], BF16)
    qfm = al("p2_qfm", [128, DB, S], BF16)
    with p.block() as blk:
        emit_transposes(p, blk, qk_tm, qfm, TB, DB, ident)
    with p.block() as blk:
        emit_transposes(p, blk, qk_tm, kfm, TB, 2, ident, src_off=D)
    qk_tm.free()

    causal = al("p2_causal", [128, 4, 512], BF16)
    with p.block() as blk:
        @blk.sync
        def _(sy):
            p.ld.inc(sy.dma_start(out=causal[:], in_=W["c_causal"].rearrange("a p f -> p a f")), 16)
            sy.wait_ge(p.ld.sem, p.ld.n)
    attn_tm = al("p2_attn_tm", [128, TB, D], BF16, side="right")
    PT = al("p2_PT", [128, 1, TB, 512], BF16, side="right")
    rzq = al("p2_rzq", [128, 4], F32, side="right")
    with p.block() as blk:
        pe0, ev0, es0, e20 = p.pe.n, p.ev.n, p.es.n, p.e2.n
        NH = 16
        its = [(h, sqc) for h in range(NH) for sqc in range(4)]
        nsk_l = [4 * sqc + 4 for (h, sqc) in its]
        pe_sc = []
        pe_pv = []
        es_xp = []
        ev_msk = []
        ev_evc = []
        cpe = pe0
        ces = es0
        cev = ev0
        for i, (h, sqc) in enumerate(its):
            nsk = nsk_l[i]
            pe_sc.append([cpe + j + 1 for j in range(nsk)])
            cpe += nsk
            pe_pv.append([cpe + j + 1 for j in range(4)])
            cpe += 4
            es_xp.append([ces + j + 1 for j in range(nsk)])
            ces += nsk
            ev_msk.append([cev + j + 1 for j in range(4)])
            cev += 4
            ev_evc.append([cev + j + 1 for j in range(4)])
            cev += 4
        sc_flat = [(i, j) for i, (h, sqc) in enumerate(its) for j in range(nsk_l[i])]

        @blk.tensor
        def _(t):
            gsc = 0
            for i, (h, sqc) in enumerate(its):
                kh = h // 4
                dbh, off = kh // 2, 64 * (kh % 2)
                nsk = nsk_l[i]
                for skb in range(nsk):
                    if gsc >= 4:
                        pi, pj = sc_flat[gsc - 4]
                        t.wait_ge(p.es.sem, es_xp[pi][pj])
                    qdb = (h % 4) + 4 * (h // 8)
                    p.pe.inc(t.matmul(p.ps4[:, gsc % 4, :],
                                      kfm[off:off + 64, dbh, skb * 128:(skb + 1) * 128],
                                      qfm[off:off + 64, qdb, sqc * 512:(sqc + 1) * 512],
                                      start=True, stop=True))
                    gsc += 1
                for sqb in range(4):
                    gq = i * 4 + sqb
                    if gq >= 2:
                        pi, pj = (gq - 2) // 4, (gq - 2) % 4
                        t.wait_ge(p.ev.sem, ev_evc[pi][pj])
                    t.wait_ge(p.ev.sem, ev_msk[i][3])
                    for skb in range(nsk):
                        ins = t.matmul(p.psq[:, gq % 2, 0:65],
                                       PT[:, 0, skb, sqb * 128:(sqb + 1) * 128],
                                       vt[:, skb, kh, :],
                                       start=(skb == 0), stop=(skb == nsk - 1))
                    p.pe.inc(ins)

        @blk.scalar
        def _(s_):
            for i, (h, sqc) in enumerate(its):
                nsk = nsk_l[i]
                if i >= 1:
                    s_.wait_ge(p.pe.sem, pe_pv[i - 1][3])
                for skb in range(nsk):
                    s_.wait_ge(p.pe.sem, pe_sc[i][skb])
                    p.es.inc(s_.activation(PT[:, 0, skb, :], p.ps4[:, (sum(nsk_l[:i]) + skb) % 4, :],
                                           AF.Exp, scale=p.cval(0.125)))

        @blk.vector
        def _(v):
            for i, (h, sqc) in enumerate(its):
                nsk = nsk_l[i]
                kh = h // 4
                for j in range(4):
                    skb = 4 * sqc + j
                    v.wait_ge(p.es.sem, es_xp[i][skb])
                    p.ev.inc(v.tensor_tensor(PT[:, 0, skb, :], PT[:, 0, skb, :],
                                             causal[:, j, :], ALU.mult))
                for sqb in range(4):
                    gq = i * 4 + sqb
                    tb = sqc * 4 + sqb
                    v.wait_ge(p.pe.sem, pe_pv[i][sqb])
                    p.e2.inc(v.reciprocal(rzq[:, sqb:sqb + 1], p.psq[:, gq % 2, 64:65]))
                    NOEDGE or v.wait_ge(p.e2.sem, p.e2.n)
                    p.ev.inc(v.tensor_scalar(attn_tm[:, tb, h * 64:(h + 1) * 64],
                                             p.psq[:, gq % 2, 0:64], rzq[:, sqb:sqb + 1],
                                             None, ALU.mult))

    rzq.free()
    PT.free()

    attn_fm = qfm
    wo2 = al("p2_wo", [128, DB, D], BF16, side="right")
    with p.block() as blk:
        @blk.gpsimd
        def _(g):
            emit_loadw(p, g, wo2, W["f_wo"])
            g.wait_ge(p.lw.sem, p.lw.n)
    with p.block() as blk:
        emit_transposes(p, blk, attn_tm, attn_fm, TB, DB, ident)

    with p.block() as blk:
        pe0, ev0 = p.pe.n, p.ev.n

        @blk.tensor
        def _(t):
            ji = 0
            for tb in range(TB):
                for ncx in range(2):
                    bank = p.ps4[:, ji % 4, :]
                    if ji >= 4:
                        t.wait_ge(p.ev.sem, ev0 + ji - 3)
                    for kb in range(DB):
                        ins = t.matmul(bank, attn_fm[:, kb, tb * 128:(tb + 1) * 128],
                                       wo2[:, kb, ncx * 512:(ncx + 1) * 512],
                                       start=(kb == 0), stop=(kb == DB - 1))
                    p.pe.inc(ins)
                    ji += 1

        @blk.vector
        def _(v):
            ji = 0
            for tb in range(TB):
                for ncx in range(2):
                    v.wait_ge(p.pe.sem, pe0 + ji + 1)
                    sl_ = TA[:, tb, ncx * 512:(ncx + 1) * 512]
                    p.ev.inc(v.tensor_tensor(sl_, sl_, p.ps4[:, ji % 4, :], ALU.add))
                    ji += 1
    wo2.free()
    attn_tm.free()
    causal.free()
    qfm.free()
    kfm.free()
    vt.free()
    sinb.free()
    cosb.free()

    xn2_fm = al("p2_xn2fm", [128, DB, S], BF16, side="right")
    xn2_tm = al("p2_xn2tm", [128, TB, D], BF16, side="right")
    with p.block() as blk:
        emit_stats(p, blk, TA, TB, sq_scr, stats)
    with p.block() as blk:
        emit_ln_algebra(p, blk, stats, [RMS_EPS], rms=True)
    with p.block() as blk:
        @blk.vector
        def _(v):
            for tb in range(TB):
                p.e2.inc(v.tensor_scalar(sq_scr[:, tb % 2, :], TA[:, tb, :],
                                         stats[:, C_NMU + tb:C_NMU + tb + 1],
                                         stats[:, C_R + tb:C_R + tb + 1], ALU.add, ALU.mult))
                NOEDGE or v.wait_ge(p.e2.sem, p.e2.n)
                p.e2.inc(v.tensor_tensor(xn2_tm[:, tb, :], sq_scr[:, tb % 2, :], g2b[:], ALU.mult))
                NOEDGE or v.wait_ge(p.e2.sem, p.e2.n)
    with p.block() as blk:
        emit_transposes(p, blk, xn2_tm, xn2_fm, TB, DB, ident)
    xn2_tm.free()

    gate_sb = al("p2_gate", [128, DB, 4], BF16, side="right")
    logits = al("p2_logits", [128, TB, 4], F32, side="right")
    with p.block() as blk:
        @blk.gpsimd
        def _(g):
            p.lw.inc(g.dma_start(out=gate_sb[:], in_=W["f_gate"].rearrange("(a p) n -> p a n", p=128)), 16)
            g.wait_ge(p.lw.sem, p.lw.n)
    with p.block() as blk:
        pe0, ev0g = p.pe.n, p.ev.n

        @blk.tensor
        def _(t):
            for tb in range(TB):
                bank = p.ps4[:, tb % 2, :]
                if tb >= 2:
                    t.wait_ge(p.ev.sem, ev0g + tb - 1)
                for kb in range(DB):
                    ins = t.matmul(bank[:, 0:4], xn2_fm[:, kb, tb * 128:(tb + 1) * 128],
                                   gate_sb[:, kb, :], start=(kb == 0), stop=(kb == DB - 1))
                p.pe.inc(ins)

        @blk.vector
        def _(v):
            for tb in range(TB):
                v.wait_ge(p.pe.sem, pe0 + tb + 1)
                p.ev.inc(v.tensor_copy(logits[:, tb, :], p.ps4[:, tb % 2, 0:4]))
    mscr = al("p2_mscr", [128, 6, TB, 4], F32, side="right")
    with p.block() as blk:
        m1 = stats[:, C_T4:C_T4 + TB]
        em2 = stats[:, C_T5:C_T5 + TB]
        den = stats[:, C_T6:C_T6 + TB]
        e_ = mscr[:, 0]
        d0 = mscr[:, 1]
        msk = mscr[:, 2]
        e2m = mscr[:, 3]
        ge2 = mscr[:, 4]
        em1 = stats[:, C_T7:C_T7 + TB]

        def b3(ap2d):
            return ap2d.unsqueeze(2).broadcast_to([128, TB, 4])

        steps = [
            ("vector", lambda v: v.tensor_reduce(m1.unsqueeze(2), logits[:], AX.X, ALU.max)),
            ("vector", lambda v: v.tensor_tensor(d0, logits[:], b3(m1), ALU.subtract)),
            ("scalar", lambda s_: s_.activation(e_, d0, AF.Exp)),
            ("vector", lambda v: v.tensor_single_scalar(msk, d0, 0.0, ALU.is_ge)),
            ("vector", lambda v: v.scalar_tensor_tensor(e2m, msk, -2.0, e_, ALU.mult, ALU.add)),
            ("vector", lambda v: v.tensor_reduce(em2.unsqueeze(2), e2m, AX.X, ALU.max)),
            ("vector", lambda v: v.tensor_reduce(em1.unsqueeze(2), e_, AX.X, ALU.max)),
            ("vector", lambda v: v.tensor_tensor(ge2, e_, b3(em2), ALU.is_ge)),
            ("vector", lambda v: v.tensor_tensor(den, em1, em2, ALU.add)),
            ("vector", lambda v: v.reciprocal(den, den)),
            ("vector", lambda v: v.tensor_tensor(e_, e_, ge2, ALU.mult)),
            ("vector", lambda v: v.tensor_tensor(comb[:], e_, b3(den), ALU.mult)),
        ]
        emit_seq(p, blk, steps)
    mscr.free()
    logits.free()
    gate_sb.free()

    g2r.free()
    g1r.free()
    g2b.free()
    g1b.free()
    HID = 3072
    HC = 1024
    NHC = HID // HC
    chunks = [(e, hc) for e in range(4) for hc in range(NHC)]
    h_fm = al("p2_hfm", [128, HC // 128, S], BF16, side="right")
    w1p = al("p2_w1p", [128, DB, HC], BF16, side="right")
    w3p = al("p2_w3p", [128, DB, HC], BF16, side="right")
    w2p = al("p2_w2p", [128, HC // 128, D], BF16, side="right")
    ch_ctr = lambda ci: p.lw if ci % 2 == 0 else p.lw2
    ch_target = {}
    w2_target = {}
    with p.block() as blk:
        @blk.gpsimd
        def _(g):
            e0, h0 = chunks[0]
            emit_loadw2(p, g, ch_ctr(0), w1p, W["f_w1"][e0, :, h0 * HC:(h0 + 1) * HC])
            emit_loadw2(p, g, ch_ctr(0), w3p, W["f_w3"][e0, :, h0 * HC:(h0 + 1) * HC])
            ch_target[0] = ch_ctr(0).n

    combf = comb[:].rearrange("p a b -> p (a b)")
    for ci, (e, hc) in enumerate(chunks):
        w1, w3 = w1p, w3p
        w1n, w3n = w1p, w3p
        hc_ctr = ch_ctr(ci)
        with p.block() as blk:
            pe0, ev0, es0 = p.pe.n, p.ev.n, p.es.n
            hjobs = [(hb, tc) for hb in range(HC // 128) for tc in range(4)]
            yjobs = [(tb, ncx) for tb in range(TB) for ncx in range(2)]
            bank_free = {0: None, 1: None, 2: None, 3: None}
            bank_plan = []
            cur = 0
            ev_cnt = ev0
            for _ in hjobs:
                ba, bb = cur % 4, (cur + 1) % 4
                bank_plan.append((ba, bb, [bank_free[ba], bank_free[bb]]))
                ev_cnt += 1
                bank_free[ba] = ev_cnt
                bank_free[bb] = ev_cnt
                cur += 2
            for _ in yjobs:
                b = cur % 4
                bank_plan.append((b, None, [bank_free[b]]))
                ev_cnt += 1
                bank_free[b] = ev_cnt
                cur += 1

            @blk.gpsimd
            def _(g):
                emit_loadw2(p, g, p.lw3, w2p, W["f_w2"][e, hc * HC:(hc + 1) * HC, :])
                w2_target[ci] = p.lw3.n
                if ci + 1 < len(chunks):
                    g.wait_ge(p.pe.sem, pe0 + len(hjobs))
                    en, hn = chunks[ci + 1]
                    nctr = ch_ctr(ci + 1)
                    emit_loadw2(p, g, nctr, w1n, W["f_w1"][en, :, hn * HC:(hn + 1) * HC])
                    emit_loadw2(p, g, nctr, w3n, W["f_w3"][en, :, hn * HC:(hn + 1) * HC])
                    ch_target[ci + 1] = nctr.n

            @blk.tensor
            def _(t):
                t.wait_ge(hc_ctr.sem, ch_target[ci])
                for ji, (hb, tc) in enumerate(hjobs):
                    ba, bb, waits = bank_plan[ji]
                    for wv in waits:
                        if wv is not None:
                            t.wait_ge(p.ev.sem, wv)
                    for kb in range(DB):
                        t.matmul(p.ps4[:, ba, :], w1[:, kb, hb * 128:(hb + 1) * 128],
                                 xn2_fm[:, kb, tc * 512:(tc + 1) * 512],
                                 start=(kb == 0), stop=(kb == DB - 1))
                    for kb in range(DB):
                        ins = t.matmul(p.ps4[:, bb, :], w3[:, kb, hb * 128:(hb + 1) * 128],
                                       xn2_fm[:, kb, tc * 512:(tc + 1) * 512],
                                       start=(kb == 0), stop=(kb == DB - 1))
                    p.pe.inc(ins)
                t.wait_ge(p.lw3.sem, w2_target[ci])
                for yi, (tb, ncx) in enumerate(yjobs):
                    ji = len(hjobs) + yi
                    b, _, waits = bank_plan[ji]
                    for wv in waits:
                        if wv is not None:
                            t.wait_ge(p.ev.sem, wv)
                    for kb in range(HC // 128):
                        ins = t.matmul(p.ps4[:, b, :], h_fm[:, kb, tb * 128:(tb + 1) * 128],
                                       w2p[:, kb, ncx * 512:(ncx + 1) * 512],
                                       start=(kb == 0), stop=(kb == HC // 128 - 1))
                    p.pe.inc(ins)

            @blk.scalar
            def _(s_):
                for ji, (hb, tc) in enumerate(hjobs):
                    ba = bank_plan[ji][0]
                    s_.wait_ge(p.pe.sem, pe0 + ji + 1)
                    if ji >= 1:
                        s_.wait_ge(p.ev.sem, ev0 + ji)
                    p.es.inc(s_.activation(sq_scr[:, ji % 2, 0:512], p.ps4[:, ba, :], AF.Silu))

            @blk.vector
            def _(v):
                for ji, (hb, tc) in enumerate(hjobs):
                    bb = bank_plan[ji][1]
                    v.wait_ge(p.es.sem, es0 + ji + 1)
                    p.ev.inc(v.tensor_tensor(h_fm[:, hb, tc * 512:(tc + 1) * 512],
                                             sq_scr[:, ji % 2, 0:512],
                                             p.ps4[:, bb, :], ALU.mult))
                for yi, (tb, ncx) in enumerate(yjobs):
                    ji = len(hjobs) + yi
                    b = bank_plan[ji][0]
                    v.wait_ge(p.pe.sem, pe0 + ji + 1)
                    sl_ = TA[:, tb, ncx * 512:(ncx + 1) * 512]
                    p.ev.inc(v.scalar_tensor_tensor(sl_, p.ps4[:, b, :],
                                                    combf[:, 4 * tb + e:4 * tb + e + 1], sl_,
                                                    ALU.mult, ALU.add))

    w2p.free()
    w3p.free()
    w1p.free()
    h_fm.free()
    xn2_fm.free()

    with p.block() as blk:
        emit_stats(p, blk, TA, TB, sq_scr, stats)
    with p.block() as blk:
        emit_ln_algebra(p, blk, stats, [LN_EPS])
    xfull = al("p2_xt", [128, TB, D], F32, side="right")
    otile = al("p2_ot", [128, 2, D], F32, side="right")
    with p.block() as blk:
        emit_epilogue(p, blk, TA, stats, xin_t, xout_t, xfull, otile, sq_scr,
                      out_stats=True)
    otile.free()
    xfull.free()

    for b in (comb, TA):
        b.free()


def build_p2_program():
    nc = bass.Bass("TRN2", target_bir_lowering=False)
    xin = nc.dram_tensor("x", [S, D], F32, kind="ExternalInput").ap()
    xout = nc.dram_tensor("y", [S, D], F32, kind="ExternalOutput").ap()
    W = {}
    for k, shp in (("f_wq", [D, D]), ("f_wk", [D, 256]), ("f_wv", [D, 256]),
                   ("f_wo", [D, D]), ("f_g1", [D]), ("f_g2", [D]), ("f_gate", [D, 4]),
                   ("f_w1", [4, D, 3072]), ("f_w2", [4, 3072, D]), ("f_w3", [4, D, 3072])):
        W[k] = nc.dram_tensor(k, shp, F32, kind="ExternalInput").ap()
    Cnp = make_consts()
    W.update(declare_consts(nc, Cnp))
    with contextlib.ExitStack() as stack:
        p = Prog(nc, stack)
        p.emit_const_init()
        csb = load_consts_block(p, W)
        emit_p2(p, xin, xout, W, csb, standalone=True)
    return nc, Cnp




def emit_p3(p, xin, xout, W, csb, standalone=False):
    nc = p.nc
    xin_t = xin.rearrange("(tb p) d -> p tb d", p=128)
    xout_t = xout.rearrange("(tb p) d -> p tb d", p=128)
    ident = csb["ident"]
    al = p.alloc
    NS = 16
    RR = 64

    xin_d = nc.dram_tensor("p3_xin_s", [D, S], BF16, kind="Internal").ap()
    zsil_d = nc.dram_tensor("p3_zsil_s", [D, S], BF16, kind="Internal").ap()
    dt_d = nc.dram_tensor("p3_dt_s", [D, S], F32, kind="Internal").ap()
    gx_d = nc.dram_tensor("p3_gx_s", [D, S], F32, kind="Internal").ap()
    xp_bc_d = nc.dram_tensor("p3_xpbc_s", [64, S], BF16, kind="Internal").ap()

    stats = p.stats
    sq_scr = p.sqscr
    Ab = al("p3_Ab", [128, NS], F32)
    xp_bf = al("p3_xpbf", [128, S], BF16)
    wxT = al("p3_wxT", [128, DB, 128], BF16)
    wdtT = al("p3_wdtT", [128, DB, 128], BF16)
    bdt = al("p3_bdt", [128, DB], F32)
    tfm = al("p3_tfm", [128, DB, S], BF16)

    t_tm = al("p3_ttm", [128, TB, D], BF16, side="right")
    xtile = al("p3_xtile", [128, 2, D], F32, side="right")
    if standalone:
        with p.block() as blk:
            emit_ln_stats_stream(p, blk, xin_t, stats, sq_scr, xtile)
    with p.block() as blk:
        emit_ln_algebra(p, blk, stats, [LN_EPS])
    with p.block() as blk:
        def applyA(v, tb, xs):
            p.ev.inc(v.tensor_scalar(t_tm[:, tb, :], xs,
                                     stats[:, C_NMU + tb:C_NMU + tb + 1],
                                     stats[:, C_R + tb:C_R + tb + 1], ALU.add, ALU.mult))
        emit_apply_stream(p, blk, xin_t, xtile, applyA)
    xtile.free()
    with p.block() as blk:
        emit_transposes(p, blk, t_tm, tfm, TB, DB, ident)
    t_tm.free()

    winT = al("p3_winT", [128, DB, 2048], BF16, side="right")
    win_nat = al("p3_winn", [128, 16, D], BF16, side="right")
    wx_nat = al("p3_wxn", [128, 1, D], BF16, side="right")
    wdt_nat = al("p3_wdtn", [128, DB, 64], BF16, side="right")
    with p.block() as blk:
        @blk.gpsimd
        def _(g):
            emit_loadw(p, g, win_nat, W["s_Win"])
            emit_loadw(p, g, wx_nat, W["s_Wx"])
            emit_loadw(p, g, wdt_nat, W["s_Wdt"])
            emit_loadcol(p, g, bdt, W["s_bdt"])
            g.wait_ge(p.lw.sem, p.lw.n)

        @blk.sync
        def _(sy):
            p.ld.inc(sy.dma_start(out=Ab[0:1, :], in_=W["s_Alog"].rearrange("(a d) -> a d", a=1)), 16)
            sy.wait_ge(p.ld.sem, p.ld.n)
    with p.block() as blk:
        emit_seq(p, blk, [
            ("scalar", lambda s_: s_.activation(Ab[0:1, :], Ab[0:1, :], AF.Exp)),
            ("vector", lambda v: v.tensor_scalar(Ab[0:1, :], Ab[0:1, :], -1.0, None, ALU.mult)),
        ])
    emit_bcast_f32(p, csb, Ab[0:1, :], Ab[:], NS)
    with p.block() as blk:
        emit_transposes(p, blk, win_nat, winT, 16, DB, ident)
    with p.block() as blk:
        pe0, ev0 = p.pe.n, p.ev.n
        tjobs = [("wx", kb) for kb in range(DB)] + [("wdt", kb) for kb in range(DB)]

        @blk.tensor
        def _(t):
            for ti, (kind, kb) in enumerate(tjobs):
                if ti >= 4:
                    t.wait_ge(p.ev.sem, ev0 + ti - 3)
                bank = p.ps4[:, ti % 4, :].bitcast(BF16)[:, 0:128]
                if kind == "wx":
                    p.pe.inc(t.transpose(bank, wx_nat[:, 0, kb * 128:(kb + 1) * 128], ident[:]))
                else:
                    p.pe.inc(t.transpose(bank[0:64, :], wdt_nat[:, kb, :], ident[:]))

        @blk.vector
        def _(v):
            for ti, (kind, kb) in enumerate(tjobs):
                v.wait_ge(p.pe.sem, pe0 + ti + 1)
                bank = p.ps4[:, ti % 4, :].bitcast(BF16)[:, 0:128]
                if kind == "wx":
                    p.ev.inc(v.tensor_copy(wxT[:, kb, :], bank))
                else:
                    p.ev.inc(v.tensor_copy(wdtT[0:64, kb, :], bank[0:64, :]))
    wdt_nat.free()
    wx_nat.free()
    win_nat.free()

    stg = al("p3_stg", [128, 4, 512], BF16, side="right")
    with p.block() as blk:
        pe0, ev0, es0, st0 = p.pe.n, p.ev.n, p.es.n, p.st.n
        jobs = [(ob, tc) for ob in range(16) for tc in range(4)]

        @blk.tensor
        def _(t):
            for ji, (ob, tc) in enumerate(jobs):
                bank = p.ps4[:, ji % 4, :]
                if ji >= 4:
                    t.wait_ge(p.es.sem, es0 + ji - 3)
                for kb in range(DB):
                    ins = t.matmul(bank, winT[:, kb, ob * 128:(ob + 1) * 128],
                                   tfm[:, kb, tc * 512:(tc + 1) * 512],
                                   start=(kb == 0), stop=(kb == DB - 1))
                p.pe.inc(ins)

        se0, so0 = p.se.n, p.so.n

        @blk.scalar
        def _(s_):
            for ji, (ob, tc) in enumerate(jobs):
                s_.wait_ge(p.pe.sem, pe0 + ji + 1)
                if ji >= 4:
                    base = se0 if ji % 2 == 0 else so0
                    ctr = p.se if ji % 2 == 0 else p.so
                    s_.wait_ge(ctr.sem, base + 16 * (ji // 2 - 1))
                p.es.inc(s_.activation(stg[:, ji % 4, :], p.ps4[:, ji % 4, :], AF.Silu))

        @blk.sync
        def _(sy):
            for ji, (ob, tc) in enumerate(jobs):
                sy.wait_ge(p.es.sem, es0 + ji + 1)
                ctr = p.se if ji % 2 == 0 else p.so
                base = se0 if ji % 2 == 0 else so0
                if ji >= 2:
                    ctr.wait = None
                    sy.wait_ge(ctr.sem, base + 16 * (ji // 2))
                dst = xin_d if ob < 8 else zsil_d
                o2 = ob if ob < 8 else ob - 8
                ctr.inc(sy.dma_start(
                    out=dst.rearrange("(a q) s -> q a s", q=128)[:, o2, tc * 512:(tc + 1) * 512],
                    in_=stg[:, ji % 4, :]), 16)
            sy.wait_ge(p.se.sem, p.se.n)
            sy.wait_ge(p.so.sem, p.so.n)
    stg.free()
    winT.free()

    xk = al("p3_xk", [128, 2, 2048], BF16, side="right")
    with p.block() as blk:
        pe0, ev0, es0, ld0 = p.pe.n, p.ev.n, p.es.n, p.ld.n
        le0, lo0 = p.le.n, p.lo.n
        xin3 = xin_d.rearrange("(a q) s -> q a s", q=128)

        @blk.sync
        def _(sy):
            for kb in range(DB):
                if kb >= 2:
                    sy.wait_ge(p.pe.sem, pe0 + 4 * (kb - 1))
                ctr = p.le if kb % 2 == 0 else p.lo
                ctr.inc(sy.dma_start(out=xk[:, kb % 2, :], in_=xin3[:, kb, :]), 16)

        @blk.tensor
        def _(t):
            for kb in range(DB):
                t.wait_ge((p.le if kb % 2 == 0 else p.lo).sem,
                          (le0 if kb % 2 == 0 else lo0) + 16 * (kb // 2 + 1))
                for tc in range(4):
                    p.pe.inc(t.matmul(p.ps4[:, tc, :], wxT[:, kb, :],
                                      xk[:, kb % 2, tc * 512:(tc + 1) * 512],
                                      start=(kb == 0), stop=(kb == DB - 1)))

        @blk.vector
        def _(v):
            for tc in range(4):
                v.wait_ge(p.pe.sem, pe0 + 4 * DB)
                p.ev.inc(v.tensor_copy(xp_bf[:, tc * 512:(tc + 1) * 512], p.ps4[:, tc, :]))
    with p.block() as blk:
        @blk.sync
        def _(sy):
            p.st.inc(sy.dma_start(out=xp_bc_d, in_=xp_bf[64:128, :]), 16)
            sy.wait_ge(p.st.sem, p.st.n)

    dtb = al("p3_dtb", [128, 2048], F32, side="right")
    scr1 = al("p3_scr1", [128, 2048], F32, side="right")
    scr2 = al("p3_scr2", [128, 2048], F32, side="right")
    with p.block() as blk:
        pe0, ev0, es0, s20, e20, st0 = p.pe.n, p.ev.n, p.es.n, p.s2.n, p.e2.n, p.st.n
        le0, lo0 = p.le.n, p.lo.n
        xin3 = xin_d.rearrange("(a q) s -> q a s", q=128)

        @blk.sync
        def _(sy):
            for db in range(DB):
                if db >= 2:
                    sy.wait_ge(p.e2.sem, e20 + 2 * (db - 1))
                ctr = p.le if db % 2 == 0 else p.lo
                ctr.inc(sy.dma_start(out=xk[:, db % 2, :], in_=xin3[:, db, :]), 16)
                sy.wait_ge(p.e2.sem, e20 + 2 * db + 2)
                d3 = dt_d.rearrange("(a q) s -> q a s", q=128)
                g3 = gx_d.rearrange("(a q) s -> q a s", q=128)
                p.st.inc(sy.dma_start(out=d3[:, db, :], in_=dtb[:]), 16)
                p.st.inc(sy.dma_start(out=g3[:, db, :], in_=scr1[:]), 16)
            sy.wait_ge(p.st.sem, p.st.n)

        @blk.tensor
        def _(t):
            for db in range(DB):
                for tc in range(4):
                    if db >= 1:
                        t.wait_ge(p.es.sem, es0 + 4 * db)
                    p.pe.inc(t.matmul(p.ps4[:, tc, :], wdtT[0:64, db, :],
                                      xp_bf[0:64, tc * 512:(tc + 1) * 512],
                                      start=True, stop=True))

        @blk.scalar
        def _(s_):
            for db in range(DB):
                for tc in range(4):
                    ji = 4 * db + tc
                    s_.wait_ge(p.pe.sem, pe0 + ji + 1)
                    if db >= 1:
                        s_.wait_ge(p.e2.sem, e20 + 2 * db)
                        s_.wait_ge(p.st.sem, st0 + 32 * db)
                    sl1 = scr2[:, tc * 512:(tc + 1) * 512]
                    sld = dtb[:, tc * 512:(tc + 1) * 512]
                    p.s2.inc(s_.activation(sl1, p.ps4[:, tc, :], AF.Abs, bias=bdt[:, db:db + 1]))
                    NOEDGE or s_.wait_ge(p.s2.sem, p.s2.n)
                    p.s2.inc(s_.activation(sl1, sl1, AF.Exp, scale=p.cval(-1.0)))
                    NOEDGE or s_.wait_ge(p.s2.sem, p.s2.n)
                    p.s2.inc(s_.activation(sl1, sl1, AF.Ln, bias=p.cval(1.0)))
                    NOEDGE or s_.wait_ge(p.s2.sem, p.s2.n)
                    p.es.inc(s_.activation(sld, p.ps4[:, tc, :], AF.Relu, bias=bdt[:, db:db + 1]))

        @blk.vector
        def _(v):
            for db in range(DB):
                v.wait_ge(p.es.sem, es0 + 4 * (db + 1))
                v.wait_ge((p.le if db % 2 == 0 else p.lo).sem,
                          (le0 if db % 2 == 0 else lo0) + 16 * (db // 2 + 1))
                if db >= 1:
                    v.wait_ge(p.st.sem, st0 + 32 * db)
                p.e2.inc(v.tensor_tensor(dtb[:], dtb[:], scr2[:], ALU.add))
                NOEDGE or v.wait_ge(p.e2.sem, p.e2.n)
                p.e2.inc(v.tensor_tensor(scr1[:], dtb[:], xk[:, db % 2, :], ALU.mult))
                NOEDGE or v.wait_ge(p.e2.sem, p.e2.n)
    scr2.free()
    scr1.free()
    dtb.free()
    xk.free()
    tfm.free()
    bdt.free()
    wdtT.free()
    wxT.free()
    xp_bf.free()

    u4 = al("p3_u4", [128, 2, 4, S], BF16, side="right")
    p4 = al("p3_p4", [128, 4, S], BF16, side="right")
    alpha4 = al("p3_alpha", [128, 2, 4, S], BF16, side="right")
    BCC = al("p3_bcc", [128, 2, 8, S], BF16, side="right")
    gx2 = al("p3_gx2", [128, S], F32, side="right")
    gx2b = al("p3_gx2b", [128, S], BF16, side="right")
    dt2 = al("p3_dt2", [128, S], F32, side="right")
    zs2 = al("p3_zs2", [128, S], BF16, side="right")
    y2stg = al("p3_y2s", [128, S], BF16, side="right")
    brow = sq_scr[:].bitcast(BF16)[0:1, :, :]
    ybf_d = nc.dram_tensor("p3_ybf_s", [D, S], BF16, kind="Internal").ap()

    with p.block() as blk:
        NG = 64
        e20, eg0, es0, s20, pe0, ld0 = p.e2.n, p.eg.n, p.es.n, p.s2.n, p.pe.n, p.ld.n
        le0, lo0 = p.le.n, p.lo.n
        st0, se0 = p.st.n, p.se.n
        d3 = dt_d.rearrange("(a q) s -> q a s", q=128)
        g3 = gx_d.rearrange("(a q) s -> q a s", q=128)
        z3 = zsil_d.rearrange("(a q) s -> q a s", q=128)
        y3 = ybf_d.rearrange("(a q) s -> q a s", q=128)
        yccp = p.psq[:].rearrange("p b f -> p (b f)")

        def adds_seq(gg):
            out = []
            if gg % 8 == 0:
                out.append(("add01", gg))
                cr = (2, 3)
            else:
                cr = (0, 1, 2, 3)
            for c in cr:
                out.append(("add", gg, c))
            if gg % 8 == 7:
                out.append(("y2", gg // 8))
            return out

        vseq = [("memset",)]
        for g in range(NG):
            vseq.append(("scan", g))
            if g >= 1:
                vseq += adds_seq(g - 1)
            for c in range(4):
                vseq.append(("prod", g, c))
        vseq += adds_seq(NG - 1)

        ev_scan, ev_prodlast, ev_y2 = {}, {}, {}
        for i, op in enumerate(vseq):
            n = e20 + i + 1
            if op[0] == "scan":
                ev_scan[op[1]] = n
            elif op[0] == "prod" and op[2] == 3:
                ev_prodlast[op[1]] = n
            elif op[0] == "y2":
                ev_y2[op[1]] = n

        eg_u = {g: eg0 + 4 * (g + 1) for g in range(NG)}
        es_evac = {g: es0 + 16 * (g + 1) for g in range(NG)}
        s2_alpha, s2_copy = {}, {}
        cs = s20
        for g in range(NG):
            if g % 8 == 0:
                cs += 1
                s2_copy[g // 8] = cs
            cs += 2
            s2_alpha[g] = cs

        def rowlist(g):
            ng = g % 8
            return ([(4 * ng + j, False) for j in range(4)] +
                    [(32 + 4 * ng + j, j % 2 == 1) for j in range(4)])

        @blk.sync
        def _(sy):
            for db in range(DB):
                if db >= 1:
                    sy.wait_ge(p.s2.sem, s2_alpha[8 * db - 1])
                p.ld.inc(sy.dma_start(out=dt2[:], in_=d3[:, db, :]), 16)
                p.ld.inc(sy.dma_start(out=gx2[:], in_=g3[:, db, :]), 16)
                for ng in range(8):
                    g = 8 * db + ng
                    if ng == 3:
                        if db >= 1:
                            sy.wait_ge(p.e2.sem, ev_y2[db - 1])
                        p.se.inc(sy.dma_start(out=zs2[:], in_=z3[:, db, :]), 16)
                    for r in range(8):
                        q = 8 * g + r
                        if q >= 2:
                            sy.wait_ge(p.pe.sem, pe0 + 2 * q - 2)
                        rabs, _neg = rowlist(g)[r]
                        ctr = p.le if q % 2 == 0 else p.lo
                        ctr.inc(sy.dma_start(out=brow[0:1, q % 2, :],
                                             in_=xp_bc_d[rabs:rabs + 1, :]), 16)
            sy.wait_ge(p.st.sem, st0 + 16 * 8)

        @blk.tensor
        def _(t):
            for g in range(NG):
                for r in range(8):
                    q = 8 * g + r
                    t.wait_ge((p.le if q % 2 == 0 else p.lo).sem,
                              (le0 if q % 2 == 0 else lo0) + 16 * (q // 2 + 1))
                    for h in range(2):
                        m = 2 * q + h
                        if m >= 2:
                            t.wait_ge(p.es.sem, es0 + m - 1)
                        for tc in (2 * h, 2 * h + 1):
                            ins = t.matmul(p.ps4[:, 2 * (m % 2) + tc % 2, :],
                                           csb["ones1x128"][:],
                                           brow[0:1, q % 2, tc * 512:(tc + 1) * 512],
                                           start=True, stop=True)
                        p.pe.inc(ins)

        @blk.scalar
        def _(s_):
            for g in range(NG):
                db, ng = g // 8, g % 8
                if ng == 0:
                    s_.wait_ge(p.ld.sem, ld0 + 32 * db + 32)
                    if db >= 1:
                        s_.wait_ge(p.eg.sem, eg_u[8 * db - 1])
                    p.s2.inc(s_.copy(gx2b[:], gx2[:]))
                for r in range(8):
                    for h in range(2):
                        m = 2 * (8 * g + r) + h
                        s_.wait_ge(p.pe.sem, pe0 + m + 1)
                        if r == 0 and h == 0 and g >= 2:
                            s_.wait_ge(p.e2.sem, ev_prodlast[g - 2])
                        src = p.ps4[:, 2 * (m % 2):2 * (m % 2) + 2, :]
                        src = src.rearrange("p b f -> p (b f)")
                        _rabs, neg = rowlist(g)[r]
                        dst = BCC[:, g % 2, r, h * 1024:(h + 1) * 1024]
                        if neg:
                            p.es.inc(s_.activation(dst, src, AF.Copy, scale=-1.0))
                        else:
                            p.es.inc(s_.activation(dst, src, AF.Copy))
                if g >= 2:
                    s_.wait_ge(p.e2.sem, ev_scan[g - 2])
                for j in range(2):
                    n = 2 * ng + j
                    p.s2.inc(s_.activation(
                        alpha4[:, g % 2, 2 * j:2 * j + 2, 1:S],
                        dt2[:, 1:S].unsqueeze(1).broadcast_to([128, 2, S - 1]),
                        AF.Exp, scale=Ab[:, n:n + 1]))
                if ng == 0 and db >= 1:
                    s_.wait_ge(p.e2.sem, ev_y2[db - 1])
                    p.st.inc(s_.dma_start(out=y3[:, db - 1, :], in_=y2stg[:]), 16)
            s_.wait_ge(p.e2.sem, ev_y2[7])
            p.st.inc(s_.dma_start(out=y3[:, 7, :], in_=y2stg[:]), 16)

        @blk.gpsimd
        def _(g_):
            for g in range(NG):
                db = g // 8
                g_.wait_ge(p.es.sem, es0 + 16 * g + 8)
                g_.wait_ge(p.s2.sem, s2_copy[db])
                if g >= 2:
                    g_.wait_ge(p.e2.sem, ev_prodlast[g - 2])
                for c in range(4):
                    p.eg.inc(g_.tensor_tensor(u4[:, g % 2, c, :], gx2b[:],
                                              BCC[:, g % 2, c, :], ALU.mult))

        @blk.vector
        def _(v):
            for op in vseq:
                if op[0] == "memset":
                    p.e2.inc(v.memset(alpha4[:, :, :, 0:1], 0.0))
                elif op[0] == "scan":
                    g = op[1]
                    u4f = u4[:, g % 2, :, :].rearrange("p c s -> p (c s)")
                    alph_f = alpha4[:, g % 2, :, :].rearrange("p c s -> p (c s)")
                    v.wait_ge(p.s2.sem, s2_alpha[g])
                    v.wait_ge(p.eg.sem, eg_u[g])
                    p.e2.inc(v.tensor_tensor_scan(u4f, alph_f, u4f, 0.0,
                                                  ALU.mult, ALU.add))
                elif op[0] == "prod":
                    g, c = op[1], op[2]
                    p.e2.inc(v.tensor_tensor(p4[:, c, :], u4[:, g % 2, c, :],
                                             BCC[:, g % 2, 4 + c, :], ALU.mult))
                elif op[0] == "add01":
                    p.e2.inc(v.tensor_tensor(yccp, p4[:, 0, :], p4[:, 1, :],
                                             ALU.add))
                elif op[0] == "add":
                    c = op[2]
                    p.e2.inc(v.tensor_tensor(yccp, yccp, p4[:, c, :], ALU.add))
                elif op[0] == "y2":
                    db = op[1]
                    v.wait_ge(p.se.sem, se0 + 16 * (db + 1))
                    if db >= 1:
                        v.wait_ge(p.st.sem, st0 + 16 * db)
                    p.e2.inc(v.tensor_tensor(y2stg[:], yccp, zs2[:], ALU.mult))

        assert p.e2.n == ev_y2[7], (p.e2.n, ev_y2[7])
        assert p.eg.n == eg_u[NG - 1], (p.eg.n, eg_u[NG - 1])
    y2stg.free()
    zs2.free()
    dt2.free()
    gx2b.free()
    gx2.free()
    BCC.free()
    alpha4.free()
    p4.free()
    u4.free()

    ybf = al("p3_ybf", [128, DB, S], BF16, side="right")
    woT = al("p3_woT", [128, DB, D], BF16, side="right")
    wo_nat = al("p3_won", [128, DB, D], BF16, side="right")
    with p.block() as blk:
        @blk.gpsimd
        def _(g):
            emit_loadw(p, g, wo_nat, W["s_Wout"])
            g.wait_ge(p.lw.sem, p.lw.n)

        @blk.sync
        def _(sy):
            y3 = ybf_d.rearrange("(a q) s -> q a s", q=128)
            for kb in range(DB):
                p.ld.inc(sy.dma_start(out=ybf[:, kb, :], in_=y3[:, kb, :]), 16)
            sy.wait_ge(p.ld.sem, p.ld.n)
    with p.block() as blk:
        emit_transposes(p, blk, wo_nat, woT, DB, DB, ident)
    wo_nat.free()

    TA = al("p3_TA", [128, TB, D], F32, side="right")
    bob = al("p3_bob", [128, D], F32, side="right")
    with p.block() as blk:
        @blk.sync
        def _(sy):
            p.ld.inc(sy.dma_start(out=bob[0:1, :], in_=W["s_bout"].rearrange("(a d) -> a d", a=1)), 16)
            sy.wait_ge(p.ld.sem, p.ld.n)
    emit_bcast_f32(p, csb, bob[0:1, :], bob[:], D)
    with p.block() as blk:
        pe0, ev0 = p.pe.n, p.ev.n

        @blk.tensor
        def _(t):
            ji = 0
            for tb in range(TB):
                for ncx in range(2):
                    bank = p.ps4[:, ji % 4, :]
                    if ji >= 4:
                        t.wait_ge(p.ev.sem, ev0 + ji - 3)
                    for kb in range(DB):
                        ins = t.matmul(bank, ybf[:, kb, tb * 128:(tb + 1) * 128],
                                       woT[:, kb, ncx * 512:(ncx + 1) * 512],
                                       start=(kb == 0), stop=(kb == DB - 1))
                    p.pe.inc(ins)
                    ji += 1

        @blk.vector
        def _(v):
            ji = 0
            for tb in range(TB):
                for ncx in range(2):
                    v.wait_ge(p.pe.sem, pe0 + ji + 1)
                    p.ev.inc(v.tensor_tensor(TA[:, tb, ncx * 512:(ncx + 1) * 512],
                                             p.ps4[:, ji % 4, :],
                                             bob[:, ncx * 512:(ncx + 1) * 512], ALU.add))
                    ji += 1

    with p.block() as blk:
        emit_stats(p, blk, TA, TB, sq_scr, stats)
    with p.block() as blk:
        emit_ln_algebra(p, blk, stats, [LN_EPS])
    xfull = al("p3_xt", [128, TB, D], F32, side="right")
    otile = al("p3_ot", [128, 2, D], F32, side="right")
    with p.block() as blk:
        emit_epilogue(p, blk, TA, stats, xin_t, xout_t, xfull, otile, sq_scr,
                      out_stats=True)
    otile.free()
    xfull.free()
    bob.free()
    TA.free()
    woT.free()
    ybf.free()

    Ab.free()


def build_p3_program():
    nc = bass.Bass("TRN2", target_bir_lowering=False)
    xin = nc.dram_tensor("x", [S, D], F32, kind="ExternalInput").ap()
    xout = nc.dram_tensor("y", [S, D], F32, kind="ExternalOutput").ap()
    W = {}
    for k, shp in (("s_Win", [2048, D]), ("s_Wx", [128, D]), ("s_Wdt", [D, 64]),
                   ("s_bdt", [D]), ("s_Wout", [D, D]), ("s_bout", [D]), ("s_Alog", [16])):
        W[k] = nc.dram_tensor(k, shp, F32, kind="ExternalInput").ap()
    Cnp = make_consts()
    W.update(declare_consts(nc, Cnp))
    with contextlib.ExitStack() as stack:
        p = Prog(nc, stack)
        p.emit_const_init()
        csb = load_consts_block(p, W)
        emit_p3(p, xin, xout, W, csb, standalone=True)
    return nc, Cnp




def emit_p4(p, xin, xout, W, csb, standalone=False):
    nc = p.nc
    xin_t = xin.rearrange("(tb p) d -> p tb d", p=128)
    xout_t = xout.rearrange("(tb p) d -> p tb d", p=128)
    ident = csb["ident"]
    al = p.alloc
    E4 = 8
    HID = 2048
    HH = 1024

    b2s_d = nc.dram_tensor("p4_b2s", [D], F32, kind="Internal").ap()

    stats = p.stats
    sq_scr = p.sqscr
    TA = al("p4_TA", [128, TB, D], F32)
    ngc = al("p4_ng", [128, TB], F32)
    gwT = al("p4_gwT", [128, DB, E4], BF16)
    gbb = al("p4_gbb", [128, E4], F32)
    rwb = al("p4_rwb", [128, E4], F32)
    b2sb = al("p4_b2sb", [128, D], F32)
    b1c = al("p4_b1c", [128, 16], F32)
    b1c2 = al("p4_b1c2", [128, 16], F32)
    xn_fm = al("p4_xnfm", [128, DB, S], BF16)

    xn_tm = al("p4_xntm", [128, TB, D], BF16, side="right")
    xtile = al("p4_xtile", [128, 2, D], F32, side="right")
    if standalone:
        with p.block() as blk:
            emit_ln_stats_stream(p, blk, xin_t, stats, sq_scr, xtile)
    with p.block() as blk:
        emit_ln_algebra(p, blk, stats, [LN_EPS])
    with p.block() as blk:
        def applyA(v, tb, xs):
            p.ev.inc(v.tensor_scalar(xn_tm[:, tb, :], xs,
                                     stats[:, C_NMU + tb:C_NMU + tb + 1],
                                     stats[:, C_R + tb:C_R + tb + 1], ALU.add, ALU.mult))
        emit_apply_stream(p, blk, xin_t, xtile, applyA)
    xtile.free()
    with p.block() as blk:
        emit_transposes(p, blk, xn_tm, xn_fm, TB, DB, ident)
    xn_tm.free()

    b2row = al("p4_b2row", [1, E4, D], F32, side="right")
    b2srow = al("p4_b2srow", [1, D], F32, side="right")
    with p.block() as blk:
        @blk.gpsimd
        def _(g):
            gsrc = W["l_gw"].rearrange("e (kb q) -> q kb e", q=128)
            for kb in range(DB):
                p.lw.inc(g.dma_start(out=gwT[:, kb, :], in_=gsrc[:, kb, :],
                                     allow_slow_non_contiguous=True), 16)
            g.wait_ge(p.lw.sem, p.lw.n)

        @blk.sync
        def _(sy):
            p.ld.inc(sy.dma_start(out=gbb[0:1, :], in_=W["l_gb"].rearrange("(a d) -> a d", a=1)), 16)
            p.ld.inc(sy.dma_start(out=rwb[0:1, :], in_=W["l_rw"].rearrange("(a d) -> a d", a=1)), 16)
            p.ld.inc(sy.dma_start(out=b2row[:], in_=W["l_b2"].unsqueeze(0),
                                  allow_slow_non_contiguous=True), 16)
            sy.wait_ge(p.ld.sem, p.ld.n)
    emit_bcast_f32(p, csb, gbb[0:1, :], gbb[:], E4)
    emit_bcast_f32(p, csb, rwb[0:1, :], rwb[:], E4)
    with p.block() as blk:
        emit_seq(p, blk, [
            ("vector", lambda v: v.tensor_reduce(
                b2srow[0:1, :].unsqueeze(2),
                b2row[0:1].transpose([0, 2, 1]), AX.X, ALU.add)),
        ])
    emit_bcast_f32(p, csb, b2srow[0:1, :], b2sb[:], D)
    b2srow.free()
    b2row.free()

    logits = al("p4_logits", [128, TB, E4], F32, side="right")
    mscr = al("p4_mscr", [128, 3, TB, E4], F32, side="right")
    with p.block() as blk:
        pe0, ev0 = p.pe.n, p.ev.n

        @blk.tensor
        def _(t):
            for tb in range(TB):
                bank = p.ps4[:, tb % 2, :]
                if tb >= 2:
                    t.wait_ge(p.ev.sem, ev0 + tb - 1)
                for kb in range(DB):
                    ins = t.matmul(bank[:, 0:E4], xn_fm[:, kb, tb * 128:(tb + 1) * 128],
                                   gwT[:, kb, :], start=(kb == 0), stop=(kb == DB - 1))
                p.pe.inc(ins)

        @blk.vector
        def _(v):
            gb3 = gbb[:].unsqueeze(1)
            for tb in range(TB):
                v.wait_ge(p.pe.sem, pe0 + tb + 1)
                p.ev.inc(v.tensor_tensor(logits[:, tb:tb + 1, :], p.ps4[:, tb % 2, 0:E4].unsqueeze(1),
                                         gb3, ALU.add))

    with p.block() as blk:
        ms = stats[:, C_T4:C_T4 + TB]
        m1 = stats[:, C_T5:C_T5 + TB]
        em1 = stats[:, C_T6:C_T6 + TB]
        ssum = stats[:, C_T7:C_T7 + TB]
        sq3 = mscr[:, 0]
        l2 = mscr[:, 1]
        e_ = mscr[:, 2]

        def b3(ap2d):
            return ap2d.unsqueeze(2).broadcast_to([128, TB, E4])

        steps = [
            ("scalar", lambda s_: s_.activation(sq3, logits[:], AF.Square)),
            ("vector", lambda v: v.tensor_reduce(ms.unsqueeze(2), sq3, AX.X, ALU.add)),
            ("vector", lambda v: v.tensor_scalar(ms, ms, 1.0 / E4, None, ALU.mult)),
            ("scalar", lambda s_: s_.activation(ms, ms, AF.Ln, bias=p.cval(RMS_EPS))),
            ("scalar", lambda s_: s_.activation(ms, ms, AF.Exp, scale=p.cval(-0.5))),
            ("vector", lambda v: v.tensor_tensor(l2, logits[:], b3(ms), ALU.mult)),
            ("vector", lambda v: v.tensor_tensor(l2, l2, rwb[:].unsqueeze(1).broadcast_to([128, TB, E4]), ALU.mult)),
            ("vector", lambda v: v.tensor_reduce(m1.unsqueeze(2), l2, AX.X, ALU.max)),
            ("vector", lambda v: v.tensor_tensor(l2, l2, b3(m1), ALU.subtract)),
            ("scalar", lambda s_: s_.activation(e_, l2, AF.Exp, scale=p.cval(2.0))),
            ("vector", lambda v: v.tensor_reduce(em1.unsqueeze(2), e_, AX.X, ALU.max)),
            ("vector", lambda v: v.tensor_reduce(ssum.unsqueeze(2), e_, AX.X, ALU.add)),
            ("vector", lambda v: v.reciprocal(ssum, ssum)),
            ("vector", lambda v: v.tensor_tensor(em1, em1, ssum, ALU.mult)),
            ("vector", lambda v: v.tensor_scalar(ssum, em1, 1e-6, None, ALU.add)),
            ("vector", lambda v: v.reciprocal(ssum, ssum)),
            ("vector", lambda v: v.tensor_tensor(ngc[:], em1, ssum, ALU.mult)),
        ]
        emit_seq(p, blk, steps)
    mscr.free()
    logits.free()

    h_fm = al("p4_hfm", [128, HH // 128, S], BF16, side="right")
    w1a = al("p4_w1a", [128, DB, HH], BF16, side="right")
    w1b = al("p4_w1b", [128, DB, HH], BF16, side="right")
    w2a = al("p4_w2a", [128, HH // 128, D], BF16, side="right")
    w2b = al("p4_w2b", [128, HH // 128, D], BF16, side="right")
    halves = [(e, hf) for e in range(E4) for hf in range(2)]
    half_ctr = lambda hi: p.lw if hi % 2 == 0 else p.lw2
    half_target = {}
    with p.block() as blk:
        @blk.gpsimd
        def _(g):
            emit_loadw2(p, g, half_ctr(0), w1a, W["l_w1"][0, :, 0:HH])
            emit_loadw2(p, g, half_ctr(0), w2a, W["l_w2"][0, 0:HH, :])
            half_ctr(0).inc(g.dma_start(out=b1c[:],
                                        in_=W["l_b1"][0, :].rearrange("(b q) -> q b", q=128),
                                        allow_slow_non_contiguous=True), 16)
            half_target[0] = half_ctr(0).n
    for hi, (e, hf) in enumerate(halves):
        w1 = w1a if hi % 2 == 0 else w1b
        w2 = w2a if hi % 2 == 0 else w2b
        w1n = w1b if hi % 2 == 0 else w1a
        w2n = w2b if hi % 2 == 0 else w2a
        hc = half_ctr(hi)
        with p.block() as blk:
            pe0, ev0, es0 = p.pe.n, p.ev.n, p.es.n
            hjobs = [(hb, tc) for hb in range(HH // 128) for tc in range(4)]
            yjobs = [(tb, ncx) for tb in range(TB) for ncx in range(2)]
            bank_free = {b: None for b in range(4)}
            plan = []
            es_cnt, ev_cnt = es0, ev0
            for ji in range(len(hjobs)):
                b = ji % 4
                plan.append((b, bank_free[b]))
                es_cnt += 1
                bank_free[b] = ("es", es_cnt)
            for yi in range(len(yjobs)):
                b = (len(hjobs) + yi) % 4
                plan.append((b, bank_free[b]))
                ev_cnt += 1
                bank_free[b] = ("ev", ev_cnt)

            @blk.gpsimd
            def _(g):
                if hi + 1 < len(halves):
                    en, hn = halves[hi + 1]
                    nctr = half_ctr(hi + 1)
                    g.wait_ge(p.pe.sem, pe0 + len(hjobs))
                    emit_loadw2(p, g, nctr, w1n, W["l_w1"][en, :, hn * HH:(hn + 1) * HH])
                    emit_loadw2(p, g, nctr, w2n, W["l_w2"][en, hn * HH:(hn + 1) * HH, :])
                    if hn == 0:
                        b1dst = b1c if en % 2 == 0 else b1c2
                        nctr.inc(g.dma_start(out=b1dst[:],
                                             in_=W["l_b1"][en, :].rearrange("(b q) -> q b", q=128),
                                             allow_slow_non_contiguous=True), 16)
                    half_target[hi + 1] = nctr.n

            def bwait(t, w):
                if w is not None:
                    t.wait_ge(p.es.sem if w[0] == "es" else p.ev.sem, w[1])

            @blk.tensor
            def _(t):
                t.wait_ge(hc.sem, half_target[hi])
                for ji, (hb, tc) in enumerate(hjobs):
                    b, w = plan[ji]
                    bwait(t, w)
                    for kb in range(DB):
                        ins = t.matmul(p.ps4[:, b, :], w1[:, kb, hb * 128:(hb + 1) * 128],
                                       xn_fm[:, kb, tc * 512:(tc + 1) * 512],
                                       start=(kb == 0), stop=(kb == DB - 1))
                    p.pe.inc(ins)
                for yi, (tb, ncx) in enumerate(yjobs):
                    b, w = plan[len(hjobs) + yi]
                    bwait(t, w)
                    for kb in range(HH // 128):
                        ins = t.matmul(p.ps4[:, b, :], h_fm[:, kb, tb * 128:(tb + 1) * 128],
                                       w2[:, kb, ncx * 512:(ncx + 1) * 512],
                                       start=(kb == 0), stop=(kb == HH // 128 - 1))
                    p.pe.inc(ins)

            @blk.scalar
            def _(s_):
                b1u = b1c if e % 2 == 0 else b1c2
                for ji, (hb, tc) in enumerate(hjobs):
                    b = plan[ji][0]
                    s_.wait_ge(p.pe.sem, pe0 + ji + 1)
                    p.es.inc(s_.activation(h_fm[:, hb, tc * 512:(tc + 1) * 512],
                                           p.ps4[:, b, :], AF.Gelu,
                                           bias=b1u[:, hf * 8 + hb:hf * 8 + hb + 1]))

            @blk.vector
            def _(v):
                for yi, (tb, ncx) in enumerate(yjobs):
                    b = plan[len(hjobs) + yi][0]
                    v.wait_ge(p.pe.sem, pe0 + len(hjobs) + yi + 1)
                    sl_ = TA[:, tb, ncx * 512:(ncx + 1) * 512]
                    if hi == 0:
                        p.ev.inc(v.tensor_tensor(sl_, p.ps4[:, b, :],
                                                 b2sb[:, ncx * 512:(ncx + 1) * 512], ALU.add))
                    else:
                        p.ev.inc(v.tensor_tensor(sl_, sl_, p.ps4[:, b, :], ALU.add))
    w2b.free()
    w2a.free()
    w1b.free()
    w1a.free()
    h_fm.free()

    with p.block() as blk:
        @blk.vector
        def _(v):
            for tb in range(TB):
                p.ev.inc(v.tensor_scalar(TA[:, tb, :], TA[:, tb, :],
                                         ngc[:, tb:tb + 1], None, ALU.mult))
    with p.block() as blk:
        emit_stats(p, blk, TA, TB, sq_scr, stats)
    with p.block() as blk:
        emit_ln_algebra(p, blk, stats, [LN_EPS])
    xfull = al("p4_xt", [128, TB, D], F32, side="right")
    otile = al("p4_ot", [128, 2, D], F32, side="right")
    with p.block() as blk:
        emit_epilogue(p, blk, TA, stats, xin_t, xout_t, xfull, otile, sq_scr)
    otile.free()
    xfull.free()

    for b in (xn_fm, b1c2, b1c, b2sb, rwb, gbb, gwT, ngc, TA):
        b.free()


def build_p4_program():
    nc = bass.Bass("TRN2", target_bir_lowering=False)
    xin = nc.dram_tensor("x", [S, D], F32, kind="ExternalInput").ap()
    xout = nc.dram_tensor("y", [S, D], F32, kind="ExternalOutput").ap()
    W = {}
    for k, shp in (("l_gw", [8, D]), ("l_gb", [8]), ("l_rw", [8]),
                   ("l_w1", [8, D, 2048]), ("l_b1", [8, 2048]),
                   ("l_w2", [8, 2048, D]), ("l_b2", [8, D])):
        W[k] = nc.dram_tensor(k, shp, F32, kind="ExternalInput").ap()
    Cnp = make_consts()
    W.update(declare_consts(nc, Cnp))
    with contextlib.ExitStack() as stack:
        p = Prog(nc, stack)
        p.emit_const_init()
        csb = load_consts_block(p, W)
        emit_p4(p, xin, xout, W, csb, standalone=True)
    return nc, Cnp



def load_consts_block(p, C_dram):
    d = {}
    d["ident"] = p.sb("cs_ident", [128, 128], BF16)
    d["ones1x128"] = p.sb("cs_ones", [1, 128], BF16)
    d["onesf32"] = p.sb("cs_onesf", [1, 128], F32)
    with p.block() as blk:
        @blk.sync
        def _(sy):
            p.ld.inc(sy.dma_start(out=d["ident"][:], in_=C_dram["c_ident"]), 16)
            sy.wait_ge(p.ld.sem, p.ld.n)

        @blk.gpsimd
        def _(g):
            g.memset(d["ones1x128"][:], 1.0)
            g.memset(d["onesf32"][:], 1.0)
    return d


def declare_consts(nc, Cnp):
    return {k: nc.dram_tensor(k, list(v.shape), CONST_DT[v.dtype], kind="ExternalInput").ap()
            for k, v in Cnp.items()}


def build_p1_program():
    nc = bass.Bass("TRN2", target_bir_lowering=False)
    xin = nc.dram_tensor("x", [S, D], F32, kind="ExternalInput").ap()
    xout = nc.dram_tensor("y", [S, D], F32, kind="ExternalOutput").ap()
    W = {k: nc.dram_tensor(k, shp, F32, kind="ExternalInput").ap()
         for k, shp in (("Wq", [D, D]), ("Wk", [D, D]), ("Wv", [D, D]), ("Wo", [D, D]),
                        ("gamma", [D]), ("beta", [D]))}
    Cnp = make_consts()
    C_dram = declare_consts(nc, Cnp)
    W.update(C_dram)
    with contextlib.ExitStack() as stack:
        p = Prog(nc, stack)
        p.emit_const_init()
        csb = load_consts_block(p, C_dram)
        emit_p1(p, xin, xout, W, csb)
    return nc, Cnp


STAGE_W = {
    1: lambda ins: {"Wq": ins["L0_attn_Wq"], "Wk": ins["L0_attn_Wk"],
                    "Wv": ins["L0_attn_Wv"], "Wo": ins["L0_attn_Wo"],
                    "gamma": ins["L0_attn_ln_gamma"], "beta": ins["L0_attn_ln_beta"]},
    2: lambda ins: {"f_wq": ins["L0_ffn_wq"], "f_wk": ins["L0_ffn_wk"],
                    "f_wv": ins["L0_ffn_wv"], "f_wo": ins["L0_ffn_wo"],
                    "f_g1": ins["L0_ffn_g1"], "f_g2": ins["L0_ffn_g2"],
                    "f_gate": ins["L0_ffn_gate"], "f_w1": ins["L0_ffn_w1"],
                    "f_w2": ins["L0_ffn_w2"], "f_w3": ins["L0_ffn_w3"]},
    3: lambda ins: {"s_Win": ins["L1_attn_W_in"], "s_Wx": ins["L1_attn_W_x"],
                    "s_Wdt": ins["L1_attn_W_dt"], "s_bdt": ins["L1_attn_b_dt"],
                    "s_Wout": ins["L1_attn_W_out"], "s_bout": ins["L1_attn_b_out"],
                    "s_Alog": ins["L1_attn_A_log"]},
    4: lambda ins: {"l_gw": ins["L1_ffn_gate_w"], "l_gb": ins["L1_ffn_gate_b"],
                    "l_rw": ins["L1_ffn_rms_w"], "l_w1": ins["L1_ffn_w1"],
                    "l_b1": ins["L1_ffn_b1"], "l_w2": ins["L1_ffn_w2"],
                    "l_b2": ins["L1_ffn_b2"]},
}
STAGE_BUILD = {}


def build_full_program():
    nc = bass.Bass("TRN2", target_bir_lowering=False)
    xin = nc.dram_tensor("x", [S, D], F32, kind="ExternalInput").ap()
    xout = nc.dram_tensor("y", [S, D], F32, kind="ExternalOutput").ap()
    r1 = nc.dram_tensor("fr1", [S, D], F32, kind="Internal").ap()
    r2 = nc.dram_tensor("fr2", [S, D], F32, kind="Internal").ap()
    r3 = nc.dram_tensor("fr3", [S, D], F32, kind="Internal").ap()
    W = {}
    for k, shp in (("Wq", [D, D]), ("Wk", [D, D]), ("Wv", [D, D]), ("Wo", [D, D]),
                   ("gamma", [D]), ("beta", [D]),
                   ("f_wq", [D, D]), ("f_wk", [D, 256]), ("f_wv", [D, 256]),
                   ("f_wo", [D, D]), ("f_g1", [D]), ("f_g2", [D]), ("f_gate", [D, 4]),
                   ("f_w1", [4, D, 3072]), ("f_w2", [4, 3072, D]), ("f_w3", [4, D, 3072]),
                   ("s_Win", [2048, D]), ("s_Wx", [128, D]), ("s_Wdt", [D, 64]),
                   ("s_bdt", [D]), ("s_Wout", [D, D]), ("s_bout", [D]), ("s_Alog", [16]),
                   ("l_gw", [8, D]), ("l_gb", [8]), ("l_rw", [8]),
                   ("l_w1", [8, D, 2048]), ("l_b1", [8, 2048]),
                   ("l_w2", [8, 2048, D]), ("l_b2", [8, D])):
        W[k] = nc.dram_tensor(k, shp, F32, kind="ExternalInput").ap()
    Cnp = make_consts()
    W.update(declare_consts(nc, Cnp))
    with contextlib.ExitStack() as stack:
        p = Prog(nc, stack)
        p.emit_const_init()
        csb = load_consts_block(p, W)
        emit_p1(p, xin, r1, W, csb)
        emit_p2(p, r1, r2, W, csb)
        emit_p3(p, r2, r3, W, csb)
        if DBG:
            p.dump("stats3", p.stats[:, 0:32])
        emit_p4(p, r3, xout, W, csb)
    return nc, Cnp


def kernel_fused(**inputs):
    nc, Cnp = build_full_program()
    x = np.ascontiguousarray(inputs["L0_attn_x"].astype(np.float32))
    base = {}
    for st in (1, 2, 3, 4):
        base.update(STAGE_W[st](inputs))
    base.update(Cnp)
    in_maps = []
    for b in range(8):
        m = dict(base)
        m["x"] = np.ascontiguousarray(x[b])
        in_maps.append(m)
    res = run_bass_kernel_spmd(nc, in_maps, core_ids=list(range(8)))
    return np.stack([r["y"] for r in res.results])


def kernel(**inputs):
    if os.environ.get("KFUSED", "1") == "1":
        return kernel_fused(**inputs)
    nstages = int(os.environ.get("KSTAGES", "4"))
    x = np.ascontiguousarray(inputs["L0_attn_x"].astype(np.float32))
    dbg = []
    for stage in range(1, nstages + 1):
        build = {1: build_p1_program, 2: build_p2_program, 3: build_p3_program,
                 4: build_p4_program}[stage]
        nc, Cnp = build()
        base = STAGE_W[stage](inputs)
        base.update(Cnp)
        in_maps = []
        for b in range(8):
            m = dict(base)
            m["x"] = np.ascontiguousarray(x[b])
            in_maps.append(m)
        res = run_bass_kernel_spmd(nc, in_maps, core_ids=list(range(8)))
        if DBG:
            dbg.append(res.results)
        x = np.stack([r["y"] for r in res.results])
    if DBG:
        kernel.dbg = dbg
    return x


# version 18
# speedup vs baseline: 1.1844x; 1.1844x over previous
import contextlib
import numpy as np
import ml_dtypes
import concourse.bass as bass
import concourse.mybir as mybir
from concourse.bass_utils import run_bass_kernel_spmd

F32 = mybir.dt.float32
BF16 = mybir.dt.bfloat16
F16 = mybir.dt.float16
AF = mybir.ActivationFunctionType
ALU = mybir.AluOpType
AX = mybir.AxisListType

S = 2048
D = 1024
TB = S // 128
DB = D // 128
LN_EPS = 1e-5
RMS_EPS = 1.1920928955078125e-07
ALIBI_SLOPES = [2.0 ** (-8.0 / n) for n in range(16, 0, -1)]

C_SUM, C_SSQ, C_NMU, C_R, C_T4, C_T5, C_T6, C_T7, C_R1 = (i * 16 for i in range(9))


class Ctr:
    def __init__(self, sem):
        self.sem = sem
        self.n = 0
        self.base = 0

    def inc(self, instr, k=1):
        self.n += k
        return instr.then_inc(self.sem, k)


class Prog:
    def __init__(self, nc, stack):
        self.nc = nc
        self.stack = stack
        ent = stack.enter_context
        self.ld = Ctr(ent(nc.semaphore("c_ld")))
        self.lw = Ctr(ent(nc.semaphore("c_lw")))
        self.pe = Ctr(ent(nc.semaphore("c_pe")))
        self.ev = Ctr(ent(nc.semaphore("c_ev")))
        self.es = Ctr(ent(nc.semaphore("c_es")))
        self.eg = Ctr(ent(nc.semaphore("c_eg")))
        self.st = Ctr(ent(nc.semaphore("c_st")))
        self.se = Ctr(ent(nc.semaphore("c_se")))
        self.so = Ctr(ent(nc.semaphore("c_so")))
        self.e2 = Ctr(ent(nc.semaphore("c_e2")))
        self.s2 = Ctr(ent(nc.semaphore("c_s2")))
        self.le = Ctr(ent(nc.semaphore("c_le")))
        self.lo = Ctr(ent(nc.semaphore("c_lo")))
        self.lw2 = Ctr(ent(nc.semaphore("c_lw2")))
        self.lw3 = Ctr(ent(nc.semaphore("c_lw3")))
        self.ps4 = ent(nc.psum_tensor("ps4", [128, 4, 512], F32))
        self.psq = ent(nc.psum_tensor("psq", [128, 4, 512], F32))
        self.stats = ent(nc.sbuf_tensor("g_stats", [128, 160], F32))
        self.sqscr = ent(nc.sbuf_tensor("g_sqscr", [128, 2, D], F32))
        self.cvals = [LN_EPS, RMS_EPS, -0.5, -1.0, 0.125, 2.0, 1.0, 0.0, 1e-6,
                      0.5, -2.0, 1.0 / D, -1.0 / D]
        self.cpool = ent(nc.sbuf_tensor("cpool", [128, len(self.cvals)], F32))

    def cval(self, v):
        i = self.cvals.index(v)
        return self.cpool[:, i:i + 1]

    def emit_const_init(self):
        with self.block() as blk:
            @blk.gpsimd
            def _(g):
                for i, v in enumerate(self.cvals):
                    g.memset(self.cpool[:, i:i + 1], float(v))

    def dump(self, name, ap):
        shape = list(ap.shape)
        d = self.nc.dram_tensor("dbg_" + name, shape, ap.dtype, kind="ExternalOutput").ap()
        with self.block() as blk:
            @blk.sync
            def _(sy):
                self.st.inc(sy.dma_start(out=d, in_=ap), 16)
                sy.wait_ge(self.st.sem, self.st.n)

    def sb(self, name, shape, dtype):
        return self.stack.enter_context(self.nc.sbuf_tensor(name, shape, dtype))

    def alloc(self, name, shape, dtype, side="left"):
        g = self.nc.sbuf_tensor(name, shape, dtype, side=side)
        t = g.__enter__()
        b = _Buf(t, g, self, side)
        if not hasattr(self, "_live"):
            self._live = {"left": [], "right": []}
        self._live[side].append(b)
        return b

    def free_live(self):
        for side in ("right", "left"):
            for b in reversed(self._live.get(side, []) if hasattr(self, "_live") else []):
                b._g.__exit__(None, None, None)
            if hasattr(self, "_live"):
                self._live[side] = []

    @contextlib.contextmanager
    def block(self):
        with self.nc.Block() as blk:
            yield blk


class _Buf:
    def __init__(self, t, g, p, side):
        self.t = t
        self._g = g
        self._p = p
        self._side = side

    def __getitem__(self, k):
        return self.t[k]

    def free(self):
        self._g.__exit__(None, None, None)
        self._p._live[self._side].remove(self)



def make_consts():
    c = {}
    c["c_ident"] = np.eye(128, dtype=ml_dtypes.bfloat16)
    pp = np.arange(128)[:, None]
    ff = np.arange(512)[None, :]
    offs = np.arange(-15, 13) * 128
    c["c_alibi"] = np.stack([np.abs(o + ff - pp) for o in offs]).astype(np.float16)
    c["c_causal"] = np.stack([((ff - pp - 128 * k) >= 0) for k in range(4)]).astype(ml_dtypes.bfloat16)
    HEAD_DIM = 64
    inv_freq = 1.0 / 10000.0 ** (np.arange(0, HEAD_DIM, 2, dtype=np.float32) / HEAD_DIM)
    freqs = np.arange(S, dtype=np.float32)[:, None] * inv_freq[None, :]
    emb = np.concatenate([freqs, freqs], axis=-1)
    c["c_cos"] = np.cos(emb).astype(np.float32)
    sin = np.sin(emb)
    sin_s = sin.copy()
    sin_s[:, :32] *= -1.0
    c["c_sin"] = sin_s.astype(np.float32)
    sel = np.zeros((64, 64, 128), dtype=ml_dtypes.bfloat16)
    for r in range(64):
        sel[r, r, :] = 1
    c["c_sel64"] = sel.reshape(64, 64 * 128)
    return c


CONST_DT = {np.dtype(np.float32): F32, np.dtype(np.float16): F16,
            np.dtype(ml_dtypes.bfloat16): BF16}



def emit_loadw(p, g, dst, src):
    s3 = src.rearrange("(a p) n -> p a n", p=128)
    for a in range(s3.shape[1]):
        p.lw.inc(g.dma_start(out=dst[:, a, :], in_=s3[:, a, :]), 16)


def emit_loadw2(p, g, ctr, dst, src):
    s3 = src.rearrange("(a p) n -> p a n", p=128)
    for a in range(s3.shape[1]):
        ctr.inc(g.dma_start(out=dst[:, a, :], in_=s3[:, a, :]), 16)


def emit_loadcol(p, g, dst, src):
    p.lw.inc(g.dma_start(out=dst[:], in_=src.rearrange("(b p) -> p b", p=128),
                         allow_slow_non_contiguous=True), 16)


def emit_transposes(p, blk, src3, dst3, ntb, ndb, ident, src_off=0):
    pe0, ev0, es0 = p.pe.n, p.ev.n, p.es.n
    tiles = [(db, tb) for db in range(ndb) for tb in range(ntb)]
    pairs = [tiles[i:i + 2] for i in range(0, len(tiles), 2)]
    bview = p.ps4[:].bitcast(BF16)

    @blk.tensor
    def _(t):
        for pi, pr in enumerate(pairs):
            if pi >= 2:
                who, cnt = divmod(pi - 2, 2)
                if (pi - 2) % 2 == 0:
                    t.wait_ge(p.ev.sem, ev0 + (pi - 2) // 2 + 1)
                else:
                    t.wait_ge(p.es.sem, es0 + (pi - 2) // 2 + 1)
            for j, (db, tb) in enumerate(pr):
                ins = t.transpose(bview[:, (2 * pi + j) % 4, 0:128],
                                  src3[:, tb, src_off + db * 128: src_off + (db + 1) * 128],
                                  ident[:])
            p.pe.inc(ins)

    @blk.vector
    def _(v):
        for pi, pr in enumerate(pairs):
            if pi % 2 != 0:
                continue
            v.wait_ge(p.pe.sem, pe0 + pi + 1)
            db0, tb0 = pr[0]
            half = (2 * pi) % 4
            p.ev.inc(v.tensor_copy(
                dst3[:, db0, tb0 * 128:(tb0 + 2) * 128].rearrange("q (a f) -> q a f", f=128),
                bview[:, half:half + 2, 0:128]))

    @blk.scalar
    def _(s_):
        for pi, pr in enumerate(pairs):
            if pi % 2 != 1:
                continue
            s_.wait_ge(p.pe.sem, pe0 + pi + 1)
            db0, tb0 = pr[0]
            half = (2 * pi) % 4
            p.es.inc(s_.copy(
                dst3[:, db0, tb0 * 128:(tb0 + 2) * 128].rearrange("q (a f) -> q a f", f=128),
                bview[:, half:half + 2, 0:128]))


def emit_seq(p, blk, steps):
    cnt = {"vector": p.ev, "scalar": p.es, "gpsimd": p.eg}
    base = {k: c.n for k, c in cnt.items()}
    counts = dict(base)
    pres = []
    for en, fn in steps:
        pres.append(dict(counts))
        counts[en] += 1

    for en in ("vector", "scalar", "gpsimd"):
        if not any(e == en for e, _ in steps):
            continue

        def body(eng, en=en):
            for i, (sen, fn) in enumerate(steps):
                if sen != en:
                    continue
                for oen in ("vector", "scalar", "gpsimd"):
                    if pres[i][oen] > base[oen]:
                        eng.wait_ge(cnt[oen].sem, pres[i][oen])
                cnt[en].inc(fn(eng))
        getattr(blk, en)(body)


def emit_stats(p, blk, src3, n_tiles, sq_scr, stats):
    es0, ev0 = p.es.n, p.ev.n

    @blk.scalar
    def _(s_):
        for tb in range(n_tiles):
            if tb >= 2:
                s_.wait_ge(p.ev.sem, ev0 + tb - 1)
            p.es.inc(s_.activation(sq_scr[:, tb % 2, :], src3[:, tb, :], AF.Square))

    @blk.vector
    def _(v):
        for tb in range(n_tiles):
            v.reduce_sum(stats[:, C_SUM + tb:C_SUM + tb + 1], src3[:, tb, :], AX.X)
            v.wait_ge(p.es.sem, es0 + tb + 1)
            p.ev.inc(v.reduce_sum(stats[:, C_SSQ + tb:C_SSQ + tb + 1], sq_scr[:, tb % 2, :], AX.X))


def emit_ln_algebra(p, blk, stats, eps_list, nt=TB, store_r1=False, rms=False):
    sl = lambda c: stats[:, c:c + nt]
    if rms:
        steps = [
            ("vector", lambda v: v.tensor_scalar(sl(C_NMU), sl(C_SUM), 0.0, None, ALU.mult)),
            ("vector", lambda v: v.tensor_scalar(sl(C_T4), sl(C_SSQ), 1.0 / D, None, ALU.mult)),
        ]
    else:
        steps = [
            ("vector", lambda v: v.tensor_scalar(sl(C_NMU), sl(C_SUM), -1.0 / D, None, ALU.mult)),
            ("vector", lambda v: v.tensor_scalar(sl(C_T4), sl(C_SSQ), 1.0 / D, None, ALU.mult)),
            ("vector", lambda v: v.tensor_tensor(sl(C_T5), sl(C_NMU), sl(C_NMU), ALU.mult)),
            ("vector", lambda v: v.tensor_tensor(sl(C_T4), sl(C_T4), sl(C_T5), ALU.subtract)),
        ]
    for i, eps in enumerate(eps_list):
        tgt = C_R if i == 0 else C_T6
        steps.append(("scalar", lambda s_, eps=eps: s_.activation(sl(C_T5), sl(C_T4), AF.Ln, bias=p.cval(eps))))
        steps.append(("scalar", lambda s_, tgt=tgt: s_.activation(sl(tgt), sl(C_T5), AF.Exp, scale=p.cval(-0.5))))
        if i == 0 and store_r1:
            steps.append(("scalar", lambda s_: s_.copy(sl(C_R1), sl(C_R))))
        if i > 0:
            steps.append(("vector", lambda v: v.tensor_tensor(sl(C_R), sl(C_R), sl(C_T6), ALU.mult)))
        if i < len(eps_list) - 1:
            steps.append(("scalar", lambda s_: s_.activation(sl(C_T7), sl(C_T5), AF.Exp, scale=p.cval(-1.0))))
            steps.append(("vector", lambda v: v.tensor_tensor(sl(C_T4), sl(C_T4), sl(C_T7), ALU.mult)))
    emit_seq(p, blk, steps)


def emit_epilogue(p, blk, TA, stats, xin_t, xout_t, xfull, otile, scr, out_stats=False):
    ld0, ev0, es0 = p.ld.n, p.ev.n, p.es.n
    se0, so0 = p.se.n, p.so.n
    VP = 4 if out_stats else 2

    @blk.sync
    def _(sy):
        for tb in range(TB):
            p.ld.inc(sy.dma_start(out=xfull[:, tb, :], in_=xin_t[:, tb, :]), 16)
        for tb in range(TB):
            sy.wait_ge(p.ev.sem, ev0 + VP * tb + 2)
            ctr = p.se if tb % 2 == 0 else p.so
            ctr.inc(sy.dma_start(out=xout_t[:, tb, :], in_=otile[:, tb % 2, :]), 16)
        sy.wait_ge(p.se.sem, p.se.n)
        sy.wait_ge(p.so.sem, p.so.n)

    if out_stats:
        @blk.scalar
        def _(s_):
            for tb in range(TB):
                s_.wait_ge(p.ev.sem, ev0 + VP * tb + 2)
                p.es.inc(s_.activation(scr[:, tb % 2, :], otile[:, tb % 2, :], AF.Square))

    @blk.vector
    def _(v):
        v.wait_ge(p.ld.sem, ld0 + 16 * TB)
        for tb in range(TB):
            if tb >= 2:
                ctr0, base = (p.se, se0) if tb % 2 == 0 else (p.so, so0)
                v.wait_ge(ctr0.sem, base + 16 * (tb // 2))
            if out_stats and tb >= 2:
                v.wait_ge(p.es.sem, es0 + tb - 1)
            p.ev.inc(v.tensor_scalar(scr[:, tb % 2, :], TA[:, tb, :],
                                     stats[:, C_NMU + tb:C_NMU + tb + 1],
                                     stats[:, C_R + tb:C_R + tb + 1], ALU.add, ALU.mult))
            NOEDGE or v.wait_ge(p.ev.sem, p.ev.n)
            p.ev.inc(v.tensor_tensor(otile[:, tb % 2, :], xfull[:, tb, :],
                                     scr[:, tb % 2, :], ALU.add))
            if out_stats:
                NOEDGE or v.wait_ge(p.ev.sem, p.ev.n)
                p.ev.inc(v.reduce_sum(stats[:, C_SUM + tb:C_SUM + tb + 1],
                                      otile[:, tb % 2, :], AX.X))
                v.wait_ge(p.es.sem, es0 + tb + 1)
                p.ev.inc(v.reduce_sum(stats[:, C_SSQ + tb:C_SSQ + tb + 1],
                                      scr[:, tb % 2, :], AX.X))


def emit_mm_grid(p, blk, jobs, evac, psum=None, nbank=4, lag=None):
    psum = psum or p.ps4
    nb = nbank
    pe0, ev0 = p.pe.n, p.ev.n

    @blk.tensor
    def _(t):
        for ji, job in enumerate(jobs):
            if ji >= nb:
                t.wait_ge(p.ev.sem, ev0 + ji - nb + 1)
            p.pe.inc(job(t, psum[:, ji % nb, :]))

    @blk.vector
    def _(v):
        for ji in range(len(jobs)):
            v.wait_ge(p.pe.sem, pe0 + ji + 1)
            p.ev.inc(evac(v, ji, psum[:, ji % nb, :]))



def emit_ln_stream(p, xin_t, stats, sq_scr, XR, eps_list, dsts):
    with p.block() as blk:
        ld0 = p.ld.n

        @blk.sync
        def _(sy):
            for tb in range(TB):
                p.ld.inc(sy.dma_start(out=XR[:, tb, :], in_=xin_t[:, tb, :]), 16)

        es0, ev0 = p.es.n, p.ev.n

        @blk.scalar
        def _(s_):
            s_.wait_ge(p.ld.sem, ld0 + 16 * TB)
            for tb in range(TB):
                if tb >= 2:
                    s_.wait_ge(p.ev.sem, ev0 + tb - 1)
                p.es.inc(s_.activation(sq_scr[:, tb % 2, :], XR[:, tb, :], AF.Square))

        @blk.vector
        def _(v):
            v.wait_ge(p.ld.sem, ld0 + 16 * TB)
            for tb in range(TB):
                v.reduce_sum(stats[:, C_SUM + tb:C_SUM + tb + 1], XR[:, tb, :], AX.X)
                v.wait_ge(p.es.sem, es0 + tb + 1)
                p.ev.inc(v.reduce_sum(stats[:, C_SSQ + tb:C_SSQ + tb + 1],
                                      sq_scr[:, tb % 2, :], AX.X))

    with p.block() as blk:
        emit_ln_algebra(p, blk, stats, eps_list)

    with p.block() as blk:
        @blk.vector
        def _(v):
            for dst in dsts:
                for tb in range(TB):
                    p.ev.inc(v.tensor_scalar(dst[:, tb, :], XR[:, tb, :],
                                             stats[:, C_NMU + tb:C_NMU + tb + 1],
                                             stats[:, C_R + tb:C_R + tb + 1],
                                             ALU.add, ALU.mult))


import os
P1_STOP = int(os.environ.get("P1_STOP", "99"))
P1_NONORM = int(os.environ.get("P1_NONORM", "0"))
DBG = int(os.environ.get("P1_DBG", "0"))
NOEDGE = int(os.environ.get("NOEDGE", "0"))
SCANEDGE_OFF = int(os.environ.get("SCANEDGE_OFF", "1"))


def emit_p1(p, xin, xout, W, csb):
    nc = p.nc
    xin_t = xin.rearrange("(tb p) d -> p tb d", p=128)
    xout_t = xout.rearrange("(tb p) d -> p tb d", p=128)
    ident = csb["ident"]

    al = p.alloc
    gcol = al("p1_gcol", [128, DB], F32)
    bcol = al("p1_bcol", [128, DB], F32)
    bcol_bf = al("p1_bcolb", [128, DB], BF16)
    bWq = al("p1_bWq", [128, DB], F32)
    bWk = al("p1_bWk", [128, DB], F32)
    stats = p.stats
    sq_scr = p.sqscr
    rowscr = sq_scr[:].bitcast(BF16)[0:1, 0, 0:D]
    bWv_b = sq_scr[:].bitcast(BF16)[:, 1, 0:D]

    wq = al("p1_wq", [128, DB, D], BF16, side="right")
    wk = al("p1_wk", [128, DB, D], BF16, side="right")
    wv = al("p1_wv", [128, DB, D], BF16, side="right")
    xn_fm = al("p1_xn_fm", [128, DB, S], BF16, side="right")
    xn_tm = al("p1_xn_tm", [128, TB, D], BF16, side="right")
    XR = al("p1_XR", [128, TB, D], F32, side="right")

    with p.block() as blk:
        @blk.gpsimd
        def _(g):
            emit_loadw(p, g, wq, W["Wq"])
            emit_loadw(p, g, wk, W["Wk"])
            emit_loadw(p, g, wv, W["Wv"])
            emit_loadcol(p, g, gcol, W["gamma"])
            emit_loadcol(p, g, bcol, W["beta"])
            g.wait_ge(p.lw.sem, p.lw.n)

    emit_ln_stream(p, xin_t, stats, sq_scr, XR, [LN_EPS, LN_EPS, LN_EPS], [xn_tm])
    XR.free()

    if P1_STOP <= 1:
        return p.free_live()
    with p.block() as blk:
        emit_transposes(p, blk, xn_tm, xn_fm, TB, DB, ident)

        @blk.scalar
        def _(s_):
            s_.copy(bcol_bf[:], bcol[:])
    xn_tm.free()

    if DBG:
        p.dump("xn_tm", xn_tm[:])
    if P1_STOP <= 2:
        return p.free_live()
    with p.block() as blk:
        pe0, ev0 = p.pe.n, p.ev.n

        @blk.tensor
        def _(t):
            ji = 0
            for w in (wq, wk):
                for db in range(DB):
                    bank = p.ps4[:, ji % 4, :]
                    if ji >= 4:
                        t.wait_ge(p.ev.sem, ev0 + ji - 3)
                    for kb in range(DB):
                        ins = t.matmul(bank[:, 0:1], w[:, kb, db * 128:(db + 1) * 128],
                                       bcol_bf[:, kb:kb + 1], start=(kb == 0), stop=(kb == DB - 1))
                    p.pe.inc(ins)
                    ji += 1
            for ncx in range(2):
                bank = p.ps4[:, ji % 4, :]
                t.wait_ge(p.ev.sem, ev0 + ji - 3)
                for kb in range(DB):
                    ins = t.matmul(bank[0:1, :], bcol_bf[:, kb:kb + 1],
                                   wv[:, kb, ncx * 512:(ncx + 1) * 512],
                                   start=(kb == 0), stop=(kb == DB - 1))
                p.pe.inc(ins)
                ji += 1

        @blk.vector
        def _(v):
            ji = 0
            for dst in (bWq, bWk):
                for db in range(DB):
                    v.wait_ge(p.pe.sem, pe0 + ji + 1)
                    p.ev.inc(v.tensor_copy(dst[:, db:db + 1], p.ps4[:, ji % 4, 0:1]))
                    ji += 1
            for ncx in range(2):
                v.wait_ge(p.pe.sem, pe0 + ji + 1)
                p.ev.inc(v.tensor_copy(rowscr[:, ncx * 512:(ncx + 1) * 512],
                                       p.ps4[0:1, ji % 4, :]))
                ji += 1

    with p.block() as blk:
        pe0, ev0 = p.pe.n, p.ev.n

        @blk.vector
        def _(v):
            for w in (wq, wk, wv):
                for kb in range(DB):
                    v.tensor_scalar(w[:, kb, :], w[:, kb, :], gcol[:, kb:kb + 1],
                                    None, ALU.mult)
            for ncx in range(2):
                v.wait_ge(p.pe.sem, pe0 + ncx + 1)
                p.ev.inc(v.tensor_copy(bWv_b[:, ncx * 512:(ncx + 1) * 512],
                                       p.ps4[:, ncx, :]))

        @blk.tensor
        def _(t):
            for ncx in range(2):
                p.pe.inc(t.matmul(p.ps4[:, ncx, :], csb["ones1x128"][:],
                                  rowscr[:, ncx * 512:(ncx + 1) * 512],
                                  start=True, stop=True))

    if DBG:
        p.dump("xn_fm", xn_fm[:])
        p.dump("stats", stats[:])
    if P1_STOP <= 3:
        return p.free_live()
    qf = al("p1_qf", [128, DB, S], BF16)
    kf = al("p1_kf", [128, DB, S], BF16)
    vte = al("p1_vte", [128, TB, 8, 65], BF16)
    vto = al("p1_vto", [128, TB, 8, 96], BF16)
    with p.block() as blk:
        pe0, ev0, es0 = p.pe.n, p.ev.n, p.es.n
        jobs = ([("q", db, tc) for db in range(DB) for tc in range(4)]
                + [("k", db, tc) for db in range(DB) for tc in range(4)]
                + [("v", tb, ncx) for tb in range(TB) for ncx in range(2)])

        @blk.gpsimd
        def _(g):
            g.memset(vte[:, :, :, 64], 1.0)
            g.memset(vto[:, :, :, 0], 1.0)
            g.memset(vto[:, :, :, 1:32], 0.0)

        @blk.tensor
        def _(t):
            for ji, job in enumerate(jobs):
                bank = p.ps4[:, ji % 4, :]
                if ji >= 4:
                    prev = ji - 4
                    if prev < 64:
                        t.wait_ge(p.es.sem, es0 + prev + 1)
                    else:
                        t.wait_ge(p.ev.sem, ev0 + prev - 64 + 1)
                kind = job[0]
                if kind in ("q", "k"):
                    _, db, tc = job
                    w = wq if kind == "q" else wk
                    for kb in range(DB):
                        ins = t.matmul(bank, w[:, kb, db * 128:(db + 1) * 128],
                                       xn_fm[:, kb, tc * 512:(tc + 1) * 512],
                                       start=(kb == 0), stop=(kb == DB - 1))
                else:
                    _, tb, ncx = job
                    for kb in range(DB):
                        ins = t.matmul(bank, xn_fm[:, kb, tb * 128:(tb + 1) * 128],
                                       wv[:, kb, ncx * 512:(ncx + 1) * 512],
                                       start=(kb == 0), stop=(kb == DB - 1))
                p.pe.inc(ins)

        @blk.scalar
        def _(s_):
            for ji, job in enumerate(jobs[:64]):
                _, db, tc = job
                s_.wait_ge(p.pe.sem, pe0 + ji + 1)
                dst = qf if job[0] == "q" else kf
                bias = bWq if job[0] == "q" else bWk
                p.es.inc(s_.activation(dst[:, db, tc * 512:(tc + 1) * 512],
                                       p.ps4[:, ji % 4, :], AF.Identity, bias=bias[:, db:db + 1]))

        @blk.vector
        def _(v):
            for ji, job in enumerate(jobs):
                if job[0] != "v":
                    continue
                _, tb, ncx = job
                v.wait_ge(p.pe.sem, pe0 + ji + 1)
                src = p.ps4[:, ji % 4, :].rearrange("p (h e) -> p h e", h=8)
                bsl = bWv_b[:, ncx * 512:(ncx + 1) * 512].rearrange("p (h e) -> p h e", h=8)
                p.e2.inc(v.tensor_tensor(vte[:, tb, ncx * 4:(ncx + 1) * 4, 0:64],
                                         src[:, 0::2, :], bsl[:, 0::2, :], ALU.add))
                NOEDGE or v.wait_ge(p.e2.sem, p.e2.n)
                p.ev.inc(v.tensor_tensor(vto[:, tb, ncx * 4:(ncx + 1) * 4, 32:96],
                                         src[:, 1::2, :], bsl[:, 1::2, :], ALU.add))

    xn_fm.free()
    wv.free()
    wk.free()
    wq.free()

    if DBG:
        p.dump("qf", qf[:])
        p.dump("kf", kf[:])
        p.dump("bWq", bWq[:])
        p.dump("bWv_b", bWv_b[:])
    if P1_STOP <= 4:
        return p.free_live()
    PT = al("p1_PT", [128, 2, TB, 512], BF16, side="right")
    sT = al("p1_sT", [128, 4, 512], F32, side="right")
    rzrow = al("p1_rzrow", [128, 2, 512], BF16, side="right")
    rzc = al("p1_rzc", [128, 2, 512], F32, side="right")
    alibi = al("p1_alibi", [128, 28, 512], F16, side="right")
    attn_fm = qf
    with p.block() as blk:
        @blk.sync
        def _(sy):
            asrc = W["c_alibi"].rearrange("a p f -> p a f")
            for a in range(0, 28, 4):
                p.ld.inc(sy.dma_start(out=alibi[:, a:a + 4, :], in_=asrc[:, a:a + 4, :]), 16)
            sy.wait_ge(p.ld.sem, p.ld.n)
    with p.block() as blk:
        pe0, ev0, es0, s20 = p.pe.n, p.ev.n, p.es.n, p.s2.n
        NH = 16
        NIT = NH * 4
        pe_sc, pe_pv, pe_bc = {}, {}, {}
        pe_cnt = pe0
        pe_order = [("S", 0)]
        for it in range(NIT):
            if it + 1 < NIT:
                pe_order.append(("S", it + 1))
            pe_order.append(("P", it))
            if it >= 1:
                pe_order.append(("B", it - 1))
        pe_order.append(("B", NIT - 1))
        for kind, it in pe_order:
            if kind == "S":
                for skb in range(TB):
                    pe_cnt += 1
                    pe_sc[(it, skb)] = pe_cnt
            elif kind == "P":
                pe_cnt += 1
                pe_pv[it] = pe_cnt
            else:
                pe_cnt += 1
                pe_bc[it] = pe_cnt
        v_order = []
        for it in range(NIT):
            v_order.append(("T", it))
            if it >= 1:
                v_order.append(("R", it - 1))
            if it >= 2:
                v_order.append(("E", it - 2))
        v_order += [("R", NIT - 1), ("E", NIT - 2), ("E", NIT - 1)]
        ev_cnt = ev0
        ev_stt_d, ev_recip, ev_evac = {}, {}, {}
        for kind, it in v_order:
            if kind == "T":
                for skb in range(TB):
                    ev_cnt += 1
                    ev_stt_d[it * 16 + skb] = ev_cnt
            elif kind == "R":
                ev_cnt += 1
                ev_recip[it] = ev_cnt
            else:
                ev_cnt += 1
                ev_evac[it] = ev_cnt
        s2_rzc = {}
        s2_cnt = s20
        s_order = []
        for it in range(NIT):
            s_order.append(("X", it))
            if it >= 2:
                s_order.append(("Z", it - 2))
        s_order += [("Z", NIT - 2), ("Z", NIT - 1)]
        for kind, it in s_order:
            if kind == "Z":
                s2_cnt += 1
                s2_rzc[it] = s2_cnt
        S_PER = 8

        def es_exp(it, skb):
            return es0 + S_PER * it + skb // 2 + 1

        @blk.tensor
        def _(t):
            for kind, it in pe_order:
                h, sqc = it // 4, it % 4
                if kind == "S":
                    dbh, off = h // 2, 64 * (h % 2)
                    for skb in range(TB):
                        g = it * 16 + skb
                        if g >= 4:
                            t.wait_ge(p.ev.sem, ev_stt_d[g - 4])
                        t.matmul(p.ps4[:, g % 4, :],
                                 kf[off:off + 64, dbh, skb * 128:(skb + 1) * 128],
                                 qf[off:off + 64, dbh, sqc * 512:(sqc + 1) * 512],
                                 start=True, stop=True).then_inc(p.pe.sem, 1)
                elif kind == "P":
                    if it >= 2:
                        t.wait_ge(p.ev.sem, ev_evac[it - 2])
                    if h % 2 == 0:
                        for skb in range(TB):
                            t.wait_ge(p.es.sem, es_exp(it, skb))
                            ins = t.matmul(p.psq[0:65, it % 2, :],
                                           vte[:, skb, h // 2, :],
                                           PT[:, it % 2, skb, :],
                                           start=(skb == 0), stop=(skb == TB - 1))
                    else:
                        for skb in range(TB):
                            t.wait_ge(p.es.sem, es_exp(it, skb))
                            t.matmul(p.psq[64:128, it % 2, :],
                                     vto[:, skb, h // 2, 32:96],
                                     PT[:, it % 2, skb, :],
                                     start=(skb == 0), stop=(skb == TB - 1))
                        for skb in range(TB):
                            ins = t.matmul(p.psq[32:33, it % 2, :],
                                           vto[:, skb, h // 2, 0:1],
                                           PT[:, it % 2, skb, :],
                                           start=(skb == 0), stop=(skb == TB - 1))
                    ins.then_inc(p.pe.sem, 1)
                else:
                    if P1_NONORM:
                        t.nop().then_inc(p.pe.sem, 1)
                        continue
                    t.wait_ge(p.ev.sem, ev_recip[it])
                    if it >= 2:
                        t.wait_ge(p.s2.sem, s2_rzc[it - 2])
                    if h % 2 == 0:
                        t.matmul(p.psq[0:64, 2 + it % 2, :], csb["onespb"][64:65, 0:64],
                                 rzrow[64:65, it % 2, :],
                                 start=True, stop=True).then_inc(p.pe.sem, 1)
                    else:
                        t.matmul(p.psq[64:128, 2 + it % 2, :], csb["onespb"][32:33, 0:64],
                                 rzrow[32:33, it % 2, :],
                                 start=True, stop=True).then_inc(p.pe.sem, 1)
        p.pe.n = pe_cnt

        @blk.vector
        def _(v):
            for kind, it in v_order:
                h, sqc = it // 4, it % 4
                if kind == "T":
                    for skb in range(TB):
                        g = it * 16 + skb
                        v.wait_ge(p.pe.sem, pe_sc[(it, skb)])
                        if g >= 4:
                            v.wait_ge(p.es.sem, es_exp((g - 4) // 16, (g - 4) % 16))
                        dix = 4 * sqc - skb + 15
                        v.scalar_tensor_tensor(
                            sT[:, g % 4, :], alibi[:, dix, :], -8.0 * ALIBI_SLOPES[h],
                            p.ps4[:, g % 4, :], ALU.mult, ALU.add).then_inc(p.ev.sem, 1)
                elif kind == "R":
                    v.wait_ge(p.pe.sem, pe_pv[it])
                    if P1_NONORM:
                        v.memset(rzrow[0:1, 0, 0:1], 0.0).then_inc(p.ev.sem, 1)
                        continue
                    if it >= 2:
                        v.wait_ge(p.pe.sem, pe_bc[it - 2])
                    zp = 64 if h % 2 == 0 else 32
                    with p.nc.allow_low_precision(reason="softmax 1/Z bcast tile"):
                        v.reciprocal(rzrow[zp:zp + 1, it % 2, :],
                                     p.psq[zp:zp + 1, it % 2, :]).then_inc(p.ev.sem, 1)
                else:
                    dbh, off = h // 2, 64 * (h % 2)
                    if P1_NONORM:
                        v.tensor_copy(attn_fm[off:off + 64, dbh, sqc * 512:(sqc + 1) * 512],
                                      p.psq[off:off + 64, it % 2, :]).then_inc(p.ev.sem, 1)
                        continue
                    v.wait_ge(p.s2.sem, s2_rzc[it])
                    v.tensor_tensor(attn_fm[off:off + 64, dbh, sqc * 512:(sqc + 1) * 512],
                                    p.psq[off:off + 64, it % 2, :],
                                    rzc[off:off + 64, it % 2, :],
                                    ALU.mult).then_inc(p.ev.sem, 1)
        p.ev.n = ev_cnt

        @blk.scalar
        def _(s_):
            for kind, it in s_order:
                if kind == "Z":
                    if P1_NONORM:
                        s_.nop().then_inc(p.s2.sem, 1)
                        continue
                    s_.wait_ge(p.pe.sem, pe_bc[it])
                    if it >= 2:
                        s_.wait_ge(p.ev.sem, ev_evac[it - 2])
                    off = 64 * ((it // 4) % 2)
                    s_.copy(rzc[off:off + 64, it % 2, :],
                            p.psq[off:off + 64, 2 + it % 2, :]).then_inc(p.s2.sem, 1)
                else:
                    if it >= 2:
                        s_.wait_ge(p.pe.sem, pe_pv[it - 2])
                    for sk2 in range(8):
                        g = it * 16 + 2 * sk2
                        s_.wait_ge(p.ev.sem, ev_stt_d[g + 1])
                        s_.activation(PT[:, it % 2, 2 * sk2:2 * sk2 + 2, :],
                                      sT[:, g % 4:g % 4 + 2, :],
                                      AF.Exp, scale=p.cval(0.125)).then_inc(p.es.sem, 1)
        p.es.n = es0 + S_PER * NIT
        p.s2.n = s2_cnt

    alibi.free()
    rzc.free()
    rzrow.free()
    sT.free()
    PT.free()
    vto.free()
    vte.free()
    kf.free()

    if P1_STOP <= 5:
        return p.free_live()
    wo = al("p1_wo", [128, DB, D], BF16)
    with p.block() as blk:
        @blk.gpsimd
        def _(g):
            emit_loadw(p, g, wo, W["Wo"])
            g.wait_ge(p.lw.sem, p.lw.n)

    TA = al("p1_TA", [128, TB, D], F32)
    with p.block() as blk:
        pe0, ev0 = p.pe.n, p.ev.n

        @blk.tensor
        def _(t):
            ji = 0
            for tb in range(TB):
                for ncx in range(2):
                    bank = p.ps4[:, ji % 4, :]
                    if ji >= 4:
                        t.wait_ge(p.ev.sem, ev0 + ji - 3)
                    for kb in range(DB):
                        ins = t.matmul(bank, attn_fm[:, kb, tb * 128:(tb + 1) * 128],
                                       wo[:, kb, ncx * 512:(ncx + 1) * 512],
                                       start=(kb == 0), stop=(kb == DB - 1))
                    p.pe.inc(ins)
                    ji += 1

        @blk.vector
        def _(v):
            ji = 0
            for tb in range(TB):
                for ncx in range(2):
                    v.wait_ge(p.pe.sem, pe0 + ji + 1)
                    p.ev.inc(v.tensor_copy(TA[:, tb, ncx * 512:(ncx + 1) * 512],
                                           p.ps4[:, ji % 4, :]))
                    ji += 1

    with p.block() as blk:
        emit_stats(p, blk, TA, TB, sq_scr, stats)
    with p.block() as blk:
        emit_ln_algebra(p, blk, stats, [LN_EPS])

    if DBG:
        p.dump("TA", TA[:])
        p.dump("stats2", stats[:])
    xfull = al("p1_xt", [128, TB, D], F32)
    otile = al("p1_ot", [128, 2, D], F32)
    with p.block() as blk:
        emit_epilogue(p, blk, TA, stats, xin_t, xout_t, xfull, otile, sq_scr,
                      out_stats=True)

    for b in (otile, xfull, TA, wo, qf, bWk, bWq,
              bcol_bf, bcol, gcol):
        b.free()



def emit_bcast_f32(p, csb, row, dst, n):
    with p.block() as blk:
        pe0 = p.pe.n
        nmm = ceil_div2(n, 512)

        @blk.tensor
        def _(t):
            for i in range(nmm):
                w = min(512, n - 512 * i)
                p.pe.inc(t.matmul(p.ps4[:, i % 4, 0:w], csb["onesf32"][:],
                                  row[0:1, 512 * i:512 * i + w], start=True, stop=True))

        @blk.vector
        def _(v):
            for i in range(nmm):
                w = min(512, n - 512 * i)
                v.wait_ge(p.pe.sem, pe0 + i + 1)
                p.ev.inc(v.tensor_copy(dst[:, 512 * i:512 * i + w], p.ps4[:, i % 4, 0:w]))


def ceil_div2(a, b):
    return -(-a // b)


def emit_ln_stats_stream(p, blk, xin_t, stats, sq_scr, xtile):
    ev0, es0 = p.ev.n, p.es.n
    le0, lo0 = p.le.n, p.lo.n

    def ldval(tb):
        return (le0 if tb % 2 == 0 else lo0) + 16 * (tb // 2 + 1)

    @blk.sync
    def _(sy):
        for tb in range(TB):
            if tb >= 2:
                sy.wait_ge(p.ev.sem, ev0 + tb - 1)
            ctr = p.le if tb % 2 == 0 else p.lo
            ctr.inc(sy.dma_start(out=xtile[:, tb % 2, :], in_=xin_t[:, tb, :]), 16)

    @blk.scalar
    def _(s_):
        for tb in range(TB):
            s_.wait_ge((p.le if tb % 2 == 0 else p.lo).sem, ldval(tb))
            if tb >= 2:
                s_.wait_ge(p.ev.sem, ev0 + tb - 1)
            p.es.inc(s_.activation(sq_scr[:, tb % 2, :], xtile[:, tb % 2, :], AF.Square))

    @blk.vector
    def _(v):
        for tb in range(TB):
            v.wait_ge((p.le if tb % 2 == 0 else p.lo).sem, ldval(tb))
            v.reduce_sum(stats[:, C_SUM + tb:C_SUM + tb + 1], xtile[:, tb % 2, :], AX.X)
            v.wait_ge(p.es.sem, es0 + tb + 1)
            p.ev.inc(v.reduce_sum(stats[:, C_SSQ + tb:C_SSQ + tb + 1],
                                  sq_scr[:, tb % 2, :], AX.X))


def emit_apply_stream(p, blk, xin_t, xtile, fn):
    ev0 = p.ev.n
    le0, lo0 = p.le.n, p.lo.n

    def ldval(tb):
        return (le0 if tb % 2 == 0 else lo0) + 16 * (tb // 2 + 1)

    @blk.sync
    def _(sy):
        for tb in range(TB):
            if tb >= 2:
                sy.wait_ge(p.ev.sem, ev0 + tb - 1)
            ctr = p.le if tb % 2 == 0 else p.lo
            ctr.inc(sy.dma_start(out=xtile[:, tb % 2, :], in_=xin_t[:, tb, :]), 16)

    @blk.vector
    def _(v):
        for tb in range(TB):
            v.wait_ge((p.le if tb % 2 == 0 else p.lo).sem, ldval(tb))
            fn(v, tb, xtile[:, tb % 2, :])



def emit_p2(p, xin, xout, W, csb, standalone=False):
    nc = p.nc
    xin_t = xin.rearrange("(tb p) d -> p tb d", p=128)
    xout_t = xout.rearrange("(tb p) d -> p tb d", p=128)
    ident = csb["ident"]
    al = p.alloc
    NKV = 4
    KVD = 256

    stats = p.stats
    sq_scr = p.sqscr
    TA = al("p2_TA", [128, TB, D], F32)
    comb = al("p2_comb", [128, TB, 4], F32)
    g1b = al("p2_g1b", [128, D], F32)
    g2b = al("p2_g2b", [128, D], F32)

    g1r = al("p2_g1r", [1, D], BF16)
    g2r = al("p2_g2r", [1, D], BF16)

    qk_tm = al("p2_qktm", [128, TB, D + KVD], BF16, side="right")
    xn1_fm = al("p2_xn1fm", [128, DB, S], BF16, side="right")
    xn1_tm = al("p2_xn1tm", [128, TB, D], BF16, side="right")
    xtile = al("p2_xtile", [128, 2, D], F32, side="right")

    with p.block() as blk:
        @blk.gpsimd
        def _(g):
            p.lw.inc(g.dma_start(out=g1r[:], in_=W["f_g1"].rearrange("(a d) -> a d", a=1)), 16)
            p.lw.inc(g.dma_start(out=g2r[:], in_=W["f_g2"].rearrange("(a d) -> a d", a=1)), 16)
            g.wait_ge(p.lw.sem, p.lw.n)



    with p.block() as blk:
        pe0 = p.pe.n

        @blk.tensor
        def _(t):
            for gi, grow in enumerate((g1r, g2r)):
                for ncx in range(2):
                    p.pe.inc(t.matmul(p.ps4[:, (2 * gi + ncx) % 4, :], csb["ones1x128"][:],
                                      grow[0:1, ncx * 512:(ncx + 1) * 512], start=True, stop=True))

        @blk.vector
        def _(v):
            for gi, gb in enumerate((g1b, g2b)):
                for ncx in range(2):
                    v.wait_ge(p.pe.sem, pe0 + 2 * gi + ncx + 1)
                    p.ev.inc(v.tensor_copy(gb[:, ncx * 512:(ncx + 1) * 512],
                                           p.ps4[:, (2 * gi + ncx) % 4, :]))

    if standalone:
        with p.block() as blk:
            emit_ln_stats_stream(p, blk, xin_t, stats, sq_scr, xtile)
    with p.block() as blk:
        emit_ln_algebra(p, blk, stats, [LN_EPS, RMS_EPS], store_r1=True)

    with p.block() as blk:
        def apply1(v, tb, xs):
            p.e2.inc(v.tensor_scalar(TA[:, tb, :], xs,
                                     stats[:, C_NMU + tb:C_NMU + tb + 1],
                                     stats[:, C_R1 + tb:C_R1 + tb + 1], ALU.add, ALU.mult))
            p.e2.inc(v.tensor_scalar(sq_scr[:, tb % 2, :], xs,
                                     stats[:, C_NMU + tb:C_NMU + tb + 1],
                                     stats[:, C_R + tb:C_R + tb + 1], ALU.add, ALU.mult))
            NOEDGE or v.wait_ge(p.e2.sem, p.e2.n)
            p.ev.inc(v.tensor_tensor(xn1_tm[:, tb, :], sq_scr[:, tb % 2, :], g1b[:], ALU.mult))
        emit_apply_stream(p, blk, xin_t, xtile, apply1)
    xtile.free()

    with p.block() as blk:
        emit_transposes(p, blk, xn1_tm, xn1_fm, TB, DB, ident)
    xn1_tm.free()
    wq2 = al("p2_wq", [128, DB, D], BF16, side="right")
    wk2 = al("p2_wk", [128, DB, KVD], BF16, side="right")
    wv2 = al("p2_wv", [128, DB, KVD], BF16, side="right")
    with p.block() as blk:
        @blk.gpsimd
        def _(g):
            emit_loadw(p, g, wq2, W["f_wq"])
            emit_loadw(p, g, wk2, W["f_wk"])
            emit_loadw(p, g, wv2, W["f_wv"])
            g.wait_ge(p.lw.sem, p.lw.n)

    cosb = al("p2_cos", [128, TB, 64], F32)
    sinb = al("p2_sin", [128, TB, 64], F32)
    vt = al("p2_vt", [128, TB, NKV, 65], BF16)
    with p.block() as blk:
        @blk.gpsimd
        def _(g):
            g.memset(vt[:, :, :, 64], 1.0)

        @blk.sync
        def _(sy):
            p.ld.inc(sy.dma_start(out=cosb[:], in_=W["c_cos"].rearrange("(tb p) e -> p tb e", p=128)), 16)
            p.ld.inc(sy.dma_start(out=sinb[:], in_=W["c_sin"].rearrange("(tb p) e -> p tb e", p=128)), 16)
            sy.wait_ge(p.ld.sem, p.ld.n)
    with p.block() as blk:
        pe0, ev0 = p.pe.n, p.ev.n
        jobs = []
        for tb in range(TB):
            jobs.append(("q", tb, 0))
            jobs.append(("q", tb, 1))
            jobs.append(("k", tb, 0))
            jobs.append(("v", tb, 0))

        @blk.tensor
        def _(t):
            for ji, (kind, tb, ncx) in enumerate(jobs):
                bank = p.ps4[:, ji % 4, :]
                if ji >= 4:
                    t.wait_ge(p.ev.sem, ev0 + ji - 3)
                w = {"q": wq2, "k": wk2, "v": wv2}[kind]
                nn = 512 if kind == "q" else KVD
                for kb in range(DB):
                    ins = t.matmul(bank[:, 0:nn], xn1_fm[:, kb, tb * 128:(tb + 1) * 128],
                                   w[:, kb, ncx * 512:ncx * 512 + nn],
                                   start=(kb == 0), stop=(kb == DB - 1))
                p.pe.inc(ins)

        @blk.vector
        def _(v):
            for ji, (kind, tb, ncx) in enumerate(jobs):
                v.wait_ge(p.pe.sem, pe0 + ji + 1)
                if ji >= 1:
                    v.wait_ge(p.ev.sem, ev0 + ji)
                bank = p.ps4[:, ji % 4, :]
                if kind == "v":
                    dst = vt[:, tb, :, 0:64]
                    src = bank[:, 0:KVD].rearrange("p (h e) -> p h e", h=NKV)
                    p.ev.inc(v.tensor_copy(dst, src))
                    continue
                nh = 8 if kind == "q" else NKV
                v3 = bank[:, 0:64 * nh].rearrange("p (h e) -> p h e", h=nh)
                cs = cosb[:, tb, :].unsqueeze(1).broadcast_to([128, nh, 64])
                sn_lo = sinb[:, tb, 0:32].unsqueeze(1).broadcast_to([128, nh, 32])
                sn_hi = sinb[:, tb, 32:64].unsqueeze(1).broadcast_to([128, nh, 32])
                tmp = sq_scr[:, 0, 0:64 * nh].rearrange("p (h e) -> p h e", h=nh)
                qc = sq_scr[:, 1, 0:64 * nh].rearrange("p (h e) -> p h e", h=nh)
                p.e2.inc(v.tensor_tensor(tmp[:, :, 0:32], v3[:, :, 32:64], sn_lo, ALU.mult))
                p.e2.inc(v.tensor_tensor(tmp[:, :, 32:64], v3[:, :, 0:32], sn_hi, ALU.mult))
                p.e2.inc(v.tensor_tensor(qc, v3, cs, ALU.mult))
                NOEDGE or v.wait_ge(p.e2.sem, p.e2.n)
                if kind == "q":
                    dst = qk_tm[:, tb, ncx * 512:(ncx + 1) * 512].rearrange(
                        "p (a s e) -> p s a e", a=4, s=2)
                    srcq = qc.rearrange("p (s a) e -> p s a e", s=2)
                    srct = tmp.rearrange("p (s a) e -> p s a e", s=2)
                    p.ev.inc(v.tensor_tensor(dst, srcq, srct, ALU.add))
                else:
                    dst = qk_tm[:, tb, D:D + KVD].rearrange("p (h e) -> p h e", h=nh)
                    p.ev.inc(v.tensor_tensor(dst, qc, tmp, ALU.add))

    wv2.free()
    wk2.free()
    wq2.free()
    xn1_fm.free()

    kfm = al("p2_kfm", [128, 2, S], BF16)
    qfm = al("p2_qfm", [128, DB, S], BF16)
    with p.block() as blk:
        emit_transposes(p, blk, qk_tm, qfm, TB, DB, ident)
    with p.block() as blk:
        emit_transposes(p, blk, qk_tm, kfm, TB, 2, ident, src_off=D)
    qk_tm.free()

    causal = al("p2_causal", [128, 4, 512], BF16)
    with p.block() as blk:
        @blk.sync
        def _(sy):
            p.ld.inc(sy.dma_start(out=causal[:], in_=W["c_causal"].rearrange("a p f -> p a f")), 16)
            sy.wait_ge(p.ld.sem, p.ld.n)
    attn_tm = al("p2_attn_tm", [128, TB, D], BF16, side="right")
    PT = al("p2_PT", [128, 1, TB, 512], BF16, side="right")
    rzq = al("p2_rzq", [128, 4], F32, side="right")
    with p.block() as blk:
        pe0, ev0, es0, e20 = p.pe.n, p.ev.n, p.es.n, p.e2.n
        NH = 16
        its = [(h, sqc) for h in range(NH) for sqc in range(4)]
        nsk_l = [4 * sqc + 4 for (h, sqc) in its]
        pe_sc = []
        pe_pv = []
        es_xp = []
        ev_msk = []
        ev_evc = []
        cpe = pe0
        ces = es0
        cev = ev0
        for i, (h, sqc) in enumerate(its):
            nsk = nsk_l[i]
            pe_sc.append([cpe + j + 1 for j in range(nsk)])
            cpe += nsk
            pe_pv.append([cpe + j + 1 for j in range(4)])
            cpe += 4
            es_xp.append([ces + j + 1 for j in range(nsk)])
            ces += nsk
            ev_msk.append([cev + j + 1 for j in range(4)])
            cev += 4
            ev_evc.append([cev + j + 1 for j in range(4)])
            cev += 4
        sc_flat = [(i, j) for i, (h, sqc) in enumerate(its) for j in range(nsk_l[i])]

        @blk.tensor
        def _(t):
            gsc = 0
            for i, (h, sqc) in enumerate(its):
                kh = h // 4
                dbh, off = kh // 2, 64 * (kh % 2)
                nsk = nsk_l[i]
                for skb in range(nsk):
                    if gsc >= 4:
                        pi, pj = sc_flat[gsc - 4]
                        t.wait_ge(p.es.sem, es_xp[pi][pj])
                    qdb = (h % 4) + 4 * (h // 8)
                    p.pe.inc(t.matmul(p.ps4[:, gsc % 4, :],
                                      kfm[off:off + 64, dbh, skb * 128:(skb + 1) * 128],
                                      qfm[off:off + 64, qdb, sqc * 512:(sqc + 1) * 512],
                                      start=True, stop=True))
                    gsc += 1
                for sqb in range(4):
                    gq = i * 4 + sqb
                    if gq >= 2:
                        pi, pj = (gq - 2) // 4, (gq - 2) % 4
                        t.wait_ge(p.ev.sem, ev_evc[pi][pj])
                    t.wait_ge(p.ev.sem, ev_msk[i][3])
                    for skb in range(nsk):
                        ins = t.matmul(p.psq[:, gq % 2, 0:65],
                                       PT[:, 0, skb, sqb * 128:(sqb + 1) * 128],
                                       vt[:, skb, kh, :],
                                       start=(skb == 0), stop=(skb == nsk - 1))
                    p.pe.inc(ins)

        @blk.scalar
        def _(s_):
            for i, (h, sqc) in enumerate(its):
                nsk = nsk_l[i]
                if i >= 1:
                    s_.wait_ge(p.pe.sem, pe_pv[i - 1][3])
                for skb in range(nsk):
                    s_.wait_ge(p.pe.sem, pe_sc[i][skb])
                    p.es.inc(s_.activation(PT[:, 0, skb, :], p.ps4[:, (sum(nsk_l[:i]) + skb) % 4, :],
                                           AF.Exp, scale=p.cval(0.125)))

        @blk.vector
        def _(v):
            for i, (h, sqc) in enumerate(its):
                nsk = nsk_l[i]
                kh = h // 4
                for j in range(4):
                    skb = 4 * sqc + j
                    v.wait_ge(p.es.sem, es_xp[i][skb])
                    p.ev.inc(v.tensor_tensor(PT[:, 0, skb, :], PT[:, 0, skb, :],
                                             causal[:, j, :], ALU.mult))
                for sqb in range(4):
                    gq = i * 4 + sqb
                    tb = sqc * 4 + sqb
                    v.wait_ge(p.pe.sem, pe_pv[i][sqb])
                    p.e2.inc(v.reciprocal(rzq[:, sqb:sqb + 1], p.psq[:, gq % 2, 64:65]))
                    NOEDGE or v.wait_ge(p.e2.sem, p.e2.n)
                    p.ev.inc(v.tensor_scalar(attn_tm[:, tb, h * 64:(h + 1) * 64],
                                             p.psq[:, gq % 2, 0:64], rzq[:, sqb:sqb + 1],
                                             None, ALU.mult))

    rzq.free()
    PT.free()

    attn_fm = qfm
    wo2 = al("p2_wo", [128, DB, D], BF16, side="right")
    with p.block() as blk:
        @blk.gpsimd
        def _(g):
            emit_loadw(p, g, wo2, W["f_wo"])
            g.wait_ge(p.lw.sem, p.lw.n)
    with p.block() as blk:
        emit_transposes(p, blk, attn_tm, attn_fm, TB, DB, ident)

    with p.block() as blk:
        pe0, ev0 = p.pe.n, p.ev.n

        @blk.tensor
        def _(t):
            ji = 0
            for tb in range(TB):
                for ncx in range(2):
                    bank = p.ps4[:, ji % 4, :]
                    if ji >= 4:
                        t.wait_ge(p.ev.sem, ev0 + ji - 3)
                    for kb in range(DB):
                        ins = t.matmul(bank, attn_fm[:, kb, tb * 128:(tb + 1) * 128],
                                       wo2[:, kb, ncx * 512:(ncx + 1) * 512],
                                       start=(kb == 0), stop=(kb == DB - 1))
                    p.pe.inc(ins)
                    ji += 1

        @blk.vector
        def _(v):
            ji = 0
            for tb in range(TB):
                for ncx in range(2):
                    v.wait_ge(p.pe.sem, pe0 + ji + 1)
                    sl_ = TA[:, tb, ncx * 512:(ncx + 1) * 512]
                    p.ev.inc(v.tensor_tensor(sl_, sl_, p.ps4[:, ji % 4, :], ALU.add))
                    ji += 1
    wo2.free()
    attn_tm.free()
    causal.free()
    qfm.free()
    kfm.free()
    vt.free()
    sinb.free()
    cosb.free()

    xn2_fm = al("p2_xn2fm", [128, DB, S], BF16, side="right")
    xn2_tm = al("p2_xn2tm", [128, TB, D], BF16, side="right")
    with p.block() as blk:
        emit_stats(p, blk, TA, TB, sq_scr, stats)
    with p.block() as blk:
        emit_ln_algebra(p, blk, stats, [RMS_EPS], rms=True)
    with p.block() as blk:
        @blk.vector
        def _(v):
            for tb in range(TB):
                p.e2.inc(v.tensor_scalar(sq_scr[:, tb % 2, :], TA[:, tb, :],
                                         stats[:, C_NMU + tb:C_NMU + tb + 1],
                                         stats[:, C_R + tb:C_R + tb + 1], ALU.add, ALU.mult))
                NOEDGE or v.wait_ge(p.e2.sem, p.e2.n)
                p.e2.inc(v.tensor_tensor(xn2_tm[:, tb, :], sq_scr[:, tb % 2, :], g2b[:], ALU.mult))
                NOEDGE or v.wait_ge(p.e2.sem, p.e2.n)
    with p.block() as blk:
        emit_transposes(p, blk, xn2_tm, xn2_fm, TB, DB, ident)
    xn2_tm.free()

    gate_sb = al("p2_gate", [128, DB, 4], BF16, side="right")
    logits = al("p2_logits", [128, TB, 4], F32, side="right")
    with p.block() as blk:
        @blk.gpsimd
        def _(g):
            p.lw.inc(g.dma_start(out=gate_sb[:], in_=W["f_gate"].rearrange("(a p) n -> p a n", p=128)), 16)
            g.wait_ge(p.lw.sem, p.lw.n)
    with p.block() as blk:
        pe0, ev0g = p.pe.n, p.ev.n

        @blk.tensor
        def _(t):
            for tb in range(TB):
                bank = p.ps4[:, tb % 2, :]
                if tb >= 2:
                    t.wait_ge(p.ev.sem, ev0g + tb - 1)
                for kb in range(DB):
                    ins = t.matmul(bank[:, 0:4], xn2_fm[:, kb, tb * 128:(tb + 1) * 128],
                                   gate_sb[:, kb, :], start=(kb == 0), stop=(kb == DB - 1))
                p.pe.inc(ins)

        @blk.vector
        def _(v):
            for tb in range(TB):
                v.wait_ge(p.pe.sem, pe0 + tb + 1)
                p.ev.inc(v.tensor_copy(logits[:, tb, :], p.ps4[:, tb % 2, 0:4]))
    mscr = al("p2_mscr", [128, 6, TB, 4], F32, side="right")
    with p.block() as blk:
        m1 = stats[:, C_T4:C_T4 + TB]
        em2 = stats[:, C_T5:C_T5 + TB]
        den = stats[:, C_T6:C_T6 + TB]
        e_ = mscr[:, 0]
        d0 = mscr[:, 1]
        msk = mscr[:, 2]
        e2m = mscr[:, 3]
        ge2 = mscr[:, 4]
        em1 = stats[:, C_T7:C_T7 + TB]

        def b3(ap2d):
            return ap2d.unsqueeze(2).broadcast_to([128, TB, 4])

        steps = [
            ("vector", lambda v: v.tensor_reduce(m1.unsqueeze(2), logits[:], AX.X, ALU.max)),
            ("vector", lambda v: v.tensor_tensor(d0, logits[:], b3(m1), ALU.subtract)),
            ("scalar", lambda s_: s_.activation(e_, d0, AF.Exp)),
            ("vector", lambda v: v.tensor_single_scalar(msk, d0, 0.0, ALU.is_ge)),
            ("vector", lambda v: v.scalar_tensor_tensor(e2m, msk, -2.0, e_, ALU.mult, ALU.add)),
            ("vector", lambda v: v.tensor_reduce(em2.unsqueeze(2), e2m, AX.X, ALU.max)),
            ("vector", lambda v: v.tensor_reduce(em1.unsqueeze(2), e_, AX.X, ALU.max)),
            ("vector", lambda v: v.tensor_tensor(ge2, e_, b3(em2), ALU.is_ge)),
            ("vector", lambda v: v.tensor_tensor(den, em1, em2, ALU.add)),
            ("vector", lambda v: v.reciprocal(den, den)),
            ("vector", lambda v: v.tensor_tensor(e_, e_, ge2, ALU.mult)),
            ("vector", lambda v: v.tensor_tensor(comb[:], e_, b3(den), ALU.mult)),
        ]
        emit_seq(p, blk, steps)
    mscr.free()
    logits.free()
    gate_sb.free()

    g2r.free()
    g1r.free()
    g2b.free()
    g1b.free()
    HID = 3072
    HC = 1024
    NHC = HID // HC
    chunks = [(e, hc) for e in range(4) for hc in range(NHC)]
    h_fm = al("p2_hfm", [128, HC // 128, S], BF16, side="right")
    w1p = al("p2_w1p", [128, DB, HC], BF16, side="right")
    w3p = al("p2_w3p", [128, DB, HC], BF16, side="right")
    w2p = al("p2_w2p", [128, HC // 128, D], BF16, side="right")
    ch_ctr = lambda ci: p.lw if ci % 2 == 0 else p.lw2
    ch_target = {}
    w2_target = {}
    with p.block() as blk:
        @blk.gpsimd
        def _(g):
            e0, h0 = chunks[0]
            emit_loadw2(p, g, ch_ctr(0), w1p, W["f_w1"][e0, :, h0 * HC:(h0 + 1) * HC])
            emit_loadw2(p, g, ch_ctr(0), w3p, W["f_w3"][e0, :, h0 * HC:(h0 + 1) * HC])
            ch_target[0] = ch_ctr(0).n

    combf = comb[:].rearrange("p a b -> p (a b)")
    for ci, (e, hc) in enumerate(chunks):
        w1, w3 = w1p, w3p
        w1n, w3n = w1p, w3p
        hc_ctr = ch_ctr(ci)
        with p.block() as blk:
            pe0, ev0, es0 = p.pe.n, p.ev.n, p.es.n
            hjobs = [(hb, tc) for hb in range(HC // 128) for tc in range(4)]
            yjobs = [(tb, ncx) for tb in range(TB) for ncx in range(2)]
            bank_free = {0: None, 1: None, 2: None, 3: None}
            bank_plan = []
            cur = 0
            ev_cnt = ev0
            for _ in hjobs:
                ba, bb = cur % 4, (cur + 1) % 4
                bank_plan.append((ba, bb, [bank_free[ba], bank_free[bb]]))
                ev_cnt += 1
                bank_free[ba] = ev_cnt
                bank_free[bb] = ev_cnt
                cur += 2
            for _ in yjobs:
                b = cur % 4
                bank_plan.append((b, None, [bank_free[b]]))
                ev_cnt += 1
                bank_free[b] = ev_cnt
                cur += 1

            @blk.gpsimd
            def _(g):
                emit_loadw2(p, g, p.lw3, w2p, W["f_w2"][e, hc * HC:(hc + 1) * HC, :])
                w2_target[ci] = p.lw3.n
                if ci + 1 < len(chunks):
                    g.wait_ge(p.pe.sem, pe0 + len(hjobs))
                    en, hn = chunks[ci + 1]
                    nctr = ch_ctr(ci + 1)
                    emit_loadw2(p, g, nctr, w1n, W["f_w1"][en, :, hn * HC:(hn + 1) * HC])
                    emit_loadw2(p, g, nctr, w3n, W["f_w3"][en, :, hn * HC:(hn + 1) * HC])
                    ch_target[ci + 1] = nctr.n

            @blk.tensor
            def _(t):
                t.wait_ge(hc_ctr.sem, ch_target[ci])
                for ji, (hb, tc) in enumerate(hjobs):
                    ba, bb, waits = bank_plan[ji]
                    for wv in waits:
                        if wv is not None:
                            t.wait_ge(p.ev.sem, wv)
                    for kb in range(DB):
                        t.matmul(p.ps4[:, ba, :], w1[:, kb, hb * 128:(hb + 1) * 128],
                                 xn2_fm[:, kb, tc * 512:(tc + 1) * 512],
                                 start=(kb == 0), stop=(kb == DB - 1))
                    for kb in range(DB):
                        ins = t.matmul(p.ps4[:, bb, :], w3[:, kb, hb * 128:(hb + 1) * 128],
                                       xn2_fm[:, kb, tc * 512:(tc + 1) * 512],
                                       start=(kb == 0), stop=(kb == DB - 1))
                    p.pe.inc(ins)
                t.wait_ge(p.lw3.sem, w2_target[ci])
                for yi, (tb, ncx) in enumerate(yjobs):
                    ji = len(hjobs) + yi
                    b, _, waits = bank_plan[ji]
                    for wv in waits:
                        if wv is not None:
                            t.wait_ge(p.ev.sem, wv)
                    for kb in range(HC // 128):
                        ins = t.matmul(p.ps4[:, b, :], h_fm[:, kb, tb * 128:(tb + 1) * 128],
                                       w2p[:, kb, ncx * 512:(ncx + 1) * 512],
                                       start=(kb == 0), stop=(kb == HC // 128 - 1))
                    p.pe.inc(ins)

            @blk.scalar
            def _(s_):
                for ji, (hb, tc) in enumerate(hjobs):
                    ba = bank_plan[ji][0]
                    s_.wait_ge(p.pe.sem, pe0 + ji + 1)
                    if ji >= 1:
                        s_.wait_ge(p.ev.sem, ev0 + ji)
                    p.es.inc(s_.activation(sq_scr[:, ji % 2, 0:512], p.ps4[:, ba, :], AF.Silu))

            @blk.vector
            def _(v):
                for ji, (hb, tc) in enumerate(hjobs):
                    bb = bank_plan[ji][1]
                    v.wait_ge(p.es.sem, es0 + ji + 1)
                    p.ev.inc(v.tensor_tensor(h_fm[:, hb, tc * 512:(tc + 1) * 512],
                                             sq_scr[:, ji % 2, 0:512],
                                             p.ps4[:, bb, :], ALU.mult))
                for yi, (tb, ncx) in enumerate(yjobs):
                    ji = len(hjobs) + yi
                    b = bank_plan[ji][0]
                    v.wait_ge(p.pe.sem, pe0 + ji + 1)
                    sl_ = TA[:, tb, ncx * 512:(ncx + 1) * 512]
                    p.ev.inc(v.scalar_tensor_tensor(sl_, p.ps4[:, b, :],
                                                    combf[:, 4 * tb + e:4 * tb + e + 1], sl_,
                                                    ALU.mult, ALU.add))

    w2p.free()
    w3p.free()
    w1p.free()
    h_fm.free()
    xn2_fm.free()

    with p.block() as blk:
        emit_stats(p, blk, TA, TB, sq_scr, stats)
    with p.block() as blk:
        emit_ln_algebra(p, blk, stats, [LN_EPS])
    xfull = al("p2_xt", [128, TB, D], F32, side="right")
    otile = al("p2_ot", [128, 2, D], F32, side="right")
    with p.block() as blk:
        emit_epilogue(p, blk, TA, stats, xin_t, xout_t, xfull, otile, sq_scr,
                      out_stats=True)
    otile.free()
    xfull.free()

    for b in (comb, TA):
        b.free()


def build_p2_program():
    nc = bass.Bass("TRN2", target_bir_lowering=False)
    xin = nc.dram_tensor("x", [S, D], F32, kind="ExternalInput").ap()
    xout = nc.dram_tensor("y", [S, D], F32, kind="ExternalOutput").ap()
    W = {}
    for k, shp in (("f_wq", [D, D]), ("f_wk", [D, 256]), ("f_wv", [D, 256]),
                   ("f_wo", [D, D]), ("f_g1", [D]), ("f_g2", [D]), ("f_gate", [D, 4]),
                   ("f_w1", [4, D, 3072]), ("f_w2", [4, 3072, D]), ("f_w3", [4, D, 3072])):
        W[k] = nc.dram_tensor(k, shp, F32, kind="ExternalInput").ap()
    Cnp = make_consts()
    W.update(declare_consts(nc, Cnp))
    with contextlib.ExitStack() as stack:
        p = Prog(nc, stack)
        p.emit_const_init()
        csb = load_consts_block(p, W)
        emit_p2(p, xin, xout, W, csb, standalone=True)
    return nc, Cnp




def emit_p3(p, xin, xout, W, csb, standalone=False):
    nc = p.nc
    xin_t = xin.rearrange("(tb p) d -> p tb d", p=128)
    xout_t = xout.rearrange("(tb p) d -> p tb d", p=128)
    ident = csb["ident"]
    al = p.alloc
    NS = 16
    RR = 64

    xin_d = nc.dram_tensor("p3_xin_s", [D, S], BF16, kind="Internal").ap()
    zsil_d = nc.dram_tensor("p3_zsil_s", [D, S], BF16, kind="Internal").ap()
    dt_d = nc.dram_tensor("p3_dt_s", [D, S], F32, kind="Internal").ap()
    gx_d = nc.dram_tensor("p3_gx_s", [D, S], F32, kind="Internal").ap()
    xp_bc_d = nc.dram_tensor("p3_xpbc_s", [64, S], BF16, kind="Internal").ap()

    stats = p.stats
    sq_scr = p.sqscr
    Ab = al("p3_Ab", [128, NS], F32)
    xp_bf = al("p3_xpbf", [128, S], BF16)
    wxT = al("p3_wxT", [128, DB, 128], BF16)
    wdtT = al("p3_wdtT", [128, DB, 128], BF16)
    bdt = al("p3_bdt", [128, DB], F32)
    tfm = al("p3_tfm", [128, DB, S], BF16)

    t_tm = al("p3_ttm", [128, TB, D], BF16, side="right")
    xtile = al("p3_xtile", [128, 2, D], F32, side="right")
    if standalone:
        with p.block() as blk:
            emit_ln_stats_stream(p, blk, xin_t, stats, sq_scr, xtile)
    with p.block() as blk:
        emit_ln_algebra(p, blk, stats, [LN_EPS])
    with p.block() as blk:
        def applyA(v, tb, xs):
            p.ev.inc(v.tensor_scalar(t_tm[:, tb, :], xs,
                                     stats[:, C_NMU + tb:C_NMU + tb + 1],
                                     stats[:, C_R + tb:C_R + tb + 1], ALU.add, ALU.mult))
        emit_apply_stream(p, blk, xin_t, xtile, applyA)
    xtile.free()
    with p.block() as blk:
        emit_transposes(p, blk, t_tm, tfm, TB, DB, ident)
    t_tm.free()

    winT = al("p3_winT", [128, DB, 2048], BF16, side="right")
    win_nat = al("p3_winn", [128, 16, D], BF16, side="right")
    wx_nat = al("p3_wxn", [128, 1, D], BF16, side="right")
    wdt_nat = al("p3_wdtn", [128, DB, 64], BF16, side="right")
    with p.block() as blk:
        @blk.gpsimd
        def _(g):
            emit_loadw(p, g, win_nat, W["s_Win"])
            emit_loadw(p, g, wx_nat, W["s_Wx"])
            emit_loadw(p, g, wdt_nat, W["s_Wdt"])
            emit_loadcol(p, g, bdt, W["s_bdt"])
            g.wait_ge(p.lw.sem, p.lw.n)

        @blk.sync
        def _(sy):
            p.ld.inc(sy.dma_start(out=Ab[0:1, :], in_=W["s_Alog"].rearrange("(a d) -> a d", a=1)), 16)
            sy.wait_ge(p.ld.sem, p.ld.n)
    with p.block() as blk:
        emit_seq(p, blk, [
            ("scalar", lambda s_: s_.activation(Ab[0:1, :], Ab[0:1, :], AF.Exp)),
            ("vector", lambda v: v.tensor_scalar(Ab[0:1, :], Ab[0:1, :], -1.0, None, ALU.mult)),
        ])
    emit_bcast_f32(p, csb, Ab[0:1, :], Ab[:], NS)
    with p.block() as blk:
        emit_transposes(p, blk, win_nat, winT, 16, DB, ident)
    with p.block() as blk:
        pe0, ev0 = p.pe.n, p.ev.n
        tjobs = [("wx", kb) for kb in range(DB)] + [("wdt", kb) for kb in range(DB)]

        @blk.tensor
        def _(t):
            for ti, (kind, kb) in enumerate(tjobs):
                if ti >= 4:
                    t.wait_ge(p.ev.sem, ev0 + ti - 3)
                bank = p.ps4[:, ti % 4, :].bitcast(BF16)[:, 0:128]
                if kind == "wx":
                    p.pe.inc(t.transpose(bank, wx_nat[:, 0, kb * 128:(kb + 1) * 128], ident[:]))
                else:
                    p.pe.inc(t.transpose(bank[0:64, :], wdt_nat[:, kb, :], ident[:]))

        @blk.vector
        def _(v):
            for ti, (kind, kb) in enumerate(tjobs):
                v.wait_ge(p.pe.sem, pe0 + ti + 1)
                bank = p.ps4[:, ti % 4, :].bitcast(BF16)[:, 0:128]
                if kind == "wx":
                    p.ev.inc(v.tensor_copy(wxT[:, kb, :], bank))
                else:
                    p.ev.inc(v.tensor_copy(wdtT[0:64, kb, :], bank[0:64, :]))
    wdt_nat.free()
    wx_nat.free()
    win_nat.free()

    stg = al("p3_stg", [128, 4, 512], BF16, side="right")
    with p.block() as blk:
        pe0, ev0, es0, st0 = p.pe.n, p.ev.n, p.es.n, p.st.n
        jobs = [(ob, tc) for ob in range(16) for tc in range(4)]

        @blk.tensor
        def _(t):
            for ji, (ob, tc) in enumerate(jobs):
                bank = p.ps4[:, ji % 4, :]
                if ji >= 4:
                    t.wait_ge(p.es.sem, es0 + ji - 3)
                for kb in range(DB):
                    ins = t.matmul(bank, winT[:, kb, ob * 128:(ob + 1) * 128],
                                   tfm[:, kb, tc * 512:(tc + 1) * 512],
                                   start=(kb == 0), stop=(kb == DB - 1))
                p.pe.inc(ins)

        se0, so0 = p.se.n, p.so.n

        @blk.scalar
        def _(s_):
            for ji, (ob, tc) in enumerate(jobs):
                s_.wait_ge(p.pe.sem, pe0 + ji + 1)
                if ji >= 4:
                    base = se0 if ji % 2 == 0 else so0
                    ctr = p.se if ji % 2 == 0 else p.so
                    s_.wait_ge(ctr.sem, base + 16 * (ji // 2 - 1))
                p.es.inc(s_.activation(stg[:, ji % 4, :], p.ps4[:, ji % 4, :], AF.Silu))

        @blk.sync
        def _(sy):
            for ji, (ob, tc) in enumerate(jobs):
                sy.wait_ge(p.es.sem, es0 + ji + 1)
                ctr = p.se if ji % 2 == 0 else p.so
                base = se0 if ji % 2 == 0 else so0
                if ji >= 2:
                    ctr.wait = None
                    sy.wait_ge(ctr.sem, base + 16 * (ji // 2))
                dst = xin_d if ob < 8 else zsil_d
                o2 = ob if ob < 8 else ob - 8
                ctr.inc(sy.dma_start(
                    out=dst.rearrange("(a q) s -> q a s", q=128)[:, o2, tc * 512:(tc + 1) * 512],
                    in_=stg[:, ji % 4, :]), 16)
            sy.wait_ge(p.se.sem, p.se.n)
            sy.wait_ge(p.so.sem, p.so.n)
    stg.free()
    winT.free()

    xk = al("p3_xk", [128, 2, 2048], BF16, side="right")
    with p.block() as blk:
        pe0, ev0, es0, ld0 = p.pe.n, p.ev.n, p.es.n, p.ld.n
        le0, lo0 = p.le.n, p.lo.n
        xin3 = xin_d.rearrange("(a q) s -> q a s", q=128)

        @blk.sync
        def _(sy):
            for kb in range(DB):
                if kb >= 2:
                    sy.wait_ge(p.pe.sem, pe0 + 4 * (kb - 1))
                ctr = p.le if kb % 2 == 0 else p.lo
                ctr.inc(sy.dma_start(out=xk[:, kb % 2, :], in_=xin3[:, kb, :]), 16)

        @blk.tensor
        def _(t):
            for kb in range(DB):
                t.wait_ge((p.le if kb % 2 == 0 else p.lo).sem,
                          (le0 if kb % 2 == 0 else lo0) + 16 * (kb // 2 + 1))
                for tc in range(4):
                    p.pe.inc(t.matmul(p.ps4[:, tc, :], wxT[:, kb, :],
                                      xk[:, kb % 2, tc * 512:(tc + 1) * 512],
                                      start=(kb == 0), stop=(kb == DB - 1)))

        @blk.vector
        def _(v):
            for tc in range(4):
                v.wait_ge(p.pe.sem, pe0 + 4 * DB)
                p.ev.inc(v.tensor_copy(xp_bf[:, tc * 512:(tc + 1) * 512], p.ps4[:, tc, :]))
    with p.block() as blk:
        @blk.sync
        def _(sy):
            p.st.inc(sy.dma_start(out=xp_bc_d, in_=xp_bf[64:128, :]), 16)
            sy.wait_ge(p.st.sem, p.st.n)

    dtb = al("p3_dtb", [128, 2048], F32, side="right")
    scr1 = al("p3_scr1", [128, 2048], F32, side="right")
    scr2 = al("p3_scr2", [128, 2048], F32, side="right")
    with p.block() as blk:
        pe0, ev0, es0, s20, e20, st0 = p.pe.n, p.ev.n, p.es.n, p.s2.n, p.e2.n, p.st.n
        le0, lo0 = p.le.n, p.lo.n
        xin3 = xin_d.rearrange("(a q) s -> q a s", q=128)

        @blk.sync
        def _(sy):
            for db in range(DB):
                if db >= 2:
                    sy.wait_ge(p.e2.sem, e20 + 2 * (db - 1))
                ctr = p.le if db % 2 == 0 else p.lo
                ctr.inc(sy.dma_start(out=xk[:, db % 2, :], in_=xin3[:, db, :]), 16)
                sy.wait_ge(p.e2.sem, e20 + 2 * db + 2)
                d3 = dt_d.rearrange("(a q) s -> q a s", q=128)
                g3 = gx_d.rearrange("(a q) s -> q a s", q=128)
                p.st.inc(sy.dma_start(out=d3[:, db, :], in_=dtb[:]), 16)
                p.st.inc(sy.dma_start(out=g3[:, db, :], in_=scr1[:]), 16)
            sy.wait_ge(p.st.sem, p.st.n)

        @blk.tensor
        def _(t):
            for db in range(DB):
                for tc in range(4):
                    if db >= 1:
                        t.wait_ge(p.es.sem, es0 + 4 * db)
                    p.pe.inc(t.matmul(p.ps4[:, tc, :], wdtT[0:64, db, :],
                                      xp_bf[0:64, tc * 512:(tc + 1) * 512],
                                      start=True, stop=True))

        @blk.scalar
        def _(s_):
            for db in range(DB):
                for tc in range(4):
                    ji = 4 * db + tc
                    s_.wait_ge(p.pe.sem, pe0 + ji + 1)
                    if db >= 1:
                        s_.wait_ge(p.e2.sem, e20 + 2 * db)
                        s_.wait_ge(p.st.sem, st0 + 32 * db)
                    sl1 = scr2[:, tc * 512:(tc + 1) * 512]
                    sld = dtb[:, tc * 512:(tc + 1) * 512]
                    p.s2.inc(s_.activation(sl1, p.ps4[:, tc, :], AF.Abs, bias=bdt[:, db:db + 1]))
                    NOEDGE or s_.wait_ge(p.s2.sem, p.s2.n)
                    p.s2.inc(s_.activation(sl1, sl1, AF.Exp, scale=p.cval(-1.0)))
                    NOEDGE or s_.wait_ge(p.s2.sem, p.s2.n)
                    p.s2.inc(s_.activation(sl1, sl1, AF.Ln, bias=p.cval(1.0)))
                    NOEDGE or s_.wait_ge(p.s2.sem, p.s2.n)
                    p.es.inc(s_.activation(sld, p.ps4[:, tc, :], AF.Relu, bias=bdt[:, db:db + 1]))

        @blk.vector
        def _(v):
            for db in range(DB):
                v.wait_ge(p.es.sem, es0 + 4 * (db + 1))
                v.wait_ge((p.le if db % 2 == 0 else p.lo).sem,
                          (le0 if db % 2 == 0 else lo0) + 16 * (db // 2 + 1))
                if db >= 1:
                    v.wait_ge(p.st.sem, st0 + 32 * db)
                p.e2.inc(v.tensor_tensor(dtb[:], dtb[:], scr2[:], ALU.add))
                NOEDGE or v.wait_ge(p.e2.sem, p.e2.n)
                p.e2.inc(v.tensor_tensor(scr1[:], dtb[:], xk[:, db % 2, :], ALU.mult))
                NOEDGE or v.wait_ge(p.e2.sem, p.e2.n)
    scr2.free()
    scr1.free()
    dtb.free()
    xk.free()
    tfm.free()
    bdt.free()
    wdtT.free()
    wxT.free()
    xp_bf.free()

    u4 = al("p3_u4", [128, 2, 4, S], BF16, side="right")
    p4 = al("p3_p4", [128, 4, S], BF16, side="right")
    alpha4 = al("p3_alpha", [128, 2, 4, S], BF16, side="right")
    BCC = al("p3_bcc", [128, 2, 8, S], BF16, side="right")
    gx2 = al("p3_gx2", [128, S], F32, side="right")
    gx2b = al("p3_gx2b", [128, S], BF16, side="right")
    dt2 = al("p3_dt2", [128, S], F32, side="right")
    zs2 = al("p3_zs2", [128, S], BF16, side="right")
    y2stg = al("p3_y2s", [128, S], BF16, side="right")
    brow = sq_scr[:].bitcast(BF16)[0:1, :, :]
    ybf_d = nc.dram_tensor("p3_ybf_s", [D, S], BF16, kind="Internal").ap()

    with p.block() as blk:
        NG = 64
        e20, eg0, es0, s20, pe0, ld0 = p.e2.n, p.eg.n, p.es.n, p.s2.n, p.pe.n, p.ld.n
        le0, lo0 = p.le.n, p.lo.n
        st0, se0 = p.st.n, p.se.n
        d3 = dt_d.rearrange("(a q) s -> q a s", q=128)
        g3 = gx_d.rearrange("(a q) s -> q a s", q=128)
        z3 = zsil_d.rearrange("(a q) s -> q a s", q=128)
        y3 = ybf_d.rearrange("(a q) s -> q a s", q=128)
        yccp = p.psq[:].rearrange("p b f -> p (b f)")

        def adds_seq(gg):
            out = []
            if gg % 8 == 0:
                out.append(("add01", gg))
                cr = (2, 3)
            else:
                cr = (0, 1, 2, 3)
            for c in cr:
                out.append(("add", gg, c))
            if gg % 8 == 7:
                out.append(("y2", gg // 8))
            return out

        vseq = [("memset",)]
        for g in range(NG):
            vseq.append(("scan", g))
            if g >= 1:
                vseq += adds_seq(g - 1)
            for c in range(4):
                vseq.append(("prod", g, c))
        vseq += adds_seq(NG - 1)

        ev_scan, ev_prodlast, ev_y2 = {}, {}, {}
        for i, op in enumerate(vseq):
            n = e20 + i + 1
            if op[0] == "scan":
                ev_scan[op[1]] = n
            elif op[0] == "prod" and op[2] == 3:
                ev_prodlast[op[1]] = n
            elif op[0] == "y2":
                ev_y2[op[1]] = n

        eg_u = {g: eg0 + 4 * (g + 1) for g in range(NG)}
        es_evac = {g: es0 + 16 * (g + 1) for g in range(NG)}
        s2_alpha, s2_copy = {}, {}
        cs = s20
        for g in range(NG):
            if g % 8 == 0:
                cs += 1
                s2_copy[g // 8] = cs
            cs += 2
            s2_alpha[g] = cs

        def rowlist(g):
            ng = g % 8
            return ([(4 * ng + j, False) for j in range(4)] +
                    [(32 + 4 * ng + j, j % 2 == 1) for j in range(4)])

        @blk.sync
        def _(sy):
            for db in range(DB):
                if db >= 1:
                    sy.wait_ge(p.s2.sem, s2_alpha[8 * db - 1])
                p.ld.inc(sy.dma_start(out=dt2[:], in_=d3[:, db, :]), 16)
                p.ld.inc(sy.dma_start(out=gx2[:], in_=g3[:, db, :]), 16)
                for ng in range(8):
                    g = 8 * db + ng
                    if ng == 3:
                        if db >= 1:
                            sy.wait_ge(p.e2.sem, ev_y2[db - 1])
                        p.se.inc(sy.dma_start(out=zs2[:], in_=z3[:, db, :]), 16)
                    for r in range(8):
                        q = 8 * g + r
                        if q >= 2:
                            sy.wait_ge(p.pe.sem, pe0 + 2 * q - 2)
                        rabs, _neg = rowlist(g)[r]
                        ctr = p.le if q % 2 == 0 else p.lo
                        ctr.inc(sy.dma_start(out=brow[0:1, q % 2, :],
                                             in_=xp_bc_d[rabs:rabs + 1, :]), 16)
            sy.wait_ge(p.st.sem, st0 + 16 * 8)

        @blk.tensor
        def _(t):
            for g in range(NG):
                for r in range(8):
                    q = 8 * g + r
                    t.wait_ge((p.le if q % 2 == 0 else p.lo).sem,
                              (le0 if q % 2 == 0 else lo0) + 16 * (q // 2 + 1))
                    for h in range(2):
                        m = 2 * q + h
                        if m >= 2:
                            t.wait_ge(p.es.sem, es0 + m - 1)
                        for tc in (2 * h, 2 * h + 1):
                            ins = t.matmul(p.ps4[:, 2 * (m % 2) + tc % 2, :],
                                           csb["ones1x128"][:],
                                           brow[0:1, q % 2, tc * 512:(tc + 1) * 512],
                                           start=True, stop=True)
                        p.pe.inc(ins)

        @blk.scalar
        def _(s_):
            for g in range(NG):
                db, ng = g // 8, g % 8
                if ng == 0:
                    s_.wait_ge(p.ld.sem, ld0 + 32 * db + 32)
                    if db >= 1:
                        s_.wait_ge(p.eg.sem, eg_u[8 * db - 1])
                    p.s2.inc(s_.copy(gx2b[:], gx2[:]))
                for r in range(8):
                    for h in range(2):
                        m = 2 * (8 * g + r) + h
                        s_.wait_ge(p.pe.sem, pe0 + m + 1)
                        if r == 0 and h == 0 and g >= 2:
                            s_.wait_ge(p.e2.sem, ev_prodlast[g - 2])
                        src = p.ps4[:, 2 * (m % 2):2 * (m % 2) + 2, :]
                        src = src.rearrange("p b f -> p (b f)")
                        _rabs, neg = rowlist(g)[r]
                        dst = BCC[:, g % 2, r, h * 1024:(h + 1) * 1024]
                        if neg:
                            p.es.inc(s_.activation(dst, src, AF.Copy, scale=-1.0))
                        else:
                            p.es.inc(s_.activation(dst, src, AF.Copy))
                if g >= 2:
                    s_.wait_ge(p.e2.sem, ev_scan[g - 2])
                for j in range(2):
                    n = 2 * ng + j
                    p.s2.inc(s_.activation(
                        alpha4[:, g % 2, 2 * j:2 * j + 2, 1:S],
                        dt2[:, 1:S].unsqueeze(1).broadcast_to([128, 2, S - 1]),
                        AF.Exp, scale=Ab[:, n:n + 1]))
                if ng == 0 and db >= 1:
                    s_.wait_ge(p.e2.sem, ev_y2[db - 1])
                    p.st.inc(s_.dma_start(out=y3[:, db - 1, :], in_=y2stg[:]), 16)
            s_.wait_ge(p.e2.sem, ev_y2[7])
            p.st.inc(s_.dma_start(out=y3[:, 7, :], in_=y2stg[:]), 16)

        @blk.gpsimd
        def _(g_):
            for g in range(NG):
                db = g // 8
                g_.wait_ge(p.es.sem, es0 + 16 * g + 8)
                g_.wait_ge(p.s2.sem, s2_copy[db])
                if g >= 2:
                    g_.wait_ge(p.e2.sem, ev_prodlast[g - 2])
                for c in range(4):
                    p.eg.inc(g_.tensor_tensor(u4[:, g % 2, c, :], gx2b[:],
                                              BCC[:, g % 2, c, :], ALU.mult))

        @blk.vector
        def _(v):
            for op in vseq:
                if op[0] == "memset":
                    p.e2.inc(v.memset(alpha4[:, :, :, 0:1], 0.0))
                elif op[0] == "scan":
                    g = op[1]
                    u4f = u4[:, g % 2, :, :].rearrange("p c s -> p (c s)")
                    alph_f = alpha4[:, g % 2, :, :].rearrange("p c s -> p (c s)")
                    v.wait_ge(p.s2.sem, s2_alpha[g])
                    v.wait_ge(p.eg.sem, eg_u[g])
                    p.e2.inc(v.tensor_tensor_scan(u4f, alph_f, u4f, 0.0,
                                                  ALU.mult, ALU.add))
                elif op[0] == "prod":
                    g, c = op[1], op[2]
                    p.e2.inc(v.tensor_tensor(p4[:, c, :], u4[:, g % 2, c, :],
                                             BCC[:, g % 2, 4 + c, :], ALU.mult))
                elif op[0] == "add01":
                    p.e2.inc(v.tensor_tensor(yccp, p4[:, 0, :], p4[:, 1, :],
                                             ALU.add))
                elif op[0] == "add":
                    c = op[2]
                    p.e2.inc(v.tensor_tensor(yccp, yccp, p4[:, c, :], ALU.add))
                elif op[0] == "y2":
                    db = op[1]
                    v.wait_ge(p.se.sem, se0 + 16 * (db + 1))
                    if db >= 1:
                        v.wait_ge(p.st.sem, st0 + 16 * db)
                    p.e2.inc(v.tensor_tensor(y2stg[:], yccp, zs2[:], ALU.mult))

        assert p.e2.n == ev_y2[7], (p.e2.n, ev_y2[7])
        assert p.eg.n == eg_u[NG - 1], (p.eg.n, eg_u[NG - 1])
    y2stg.free()
    zs2.free()
    dt2.free()
    gx2b.free()
    gx2.free()
    BCC.free()
    alpha4.free()
    p4.free()
    u4.free()

    ybf = al("p3_ybf", [128, DB, S], BF16, side="right")
    woT = al("p3_woT", [128, DB, D], BF16, side="right")
    wo_nat = al("p3_won", [128, DB, D], BF16, side="right")
    with p.block() as blk:
        @blk.gpsimd
        def _(g):
            emit_loadw(p, g, wo_nat, W["s_Wout"])
            g.wait_ge(p.lw.sem, p.lw.n)

        @blk.sync
        def _(sy):
            y3 = ybf_d.rearrange("(a q) s -> q a s", q=128)
            for kb in range(DB):
                p.ld.inc(sy.dma_start(out=ybf[:, kb, :], in_=y3[:, kb, :]), 16)
            sy.wait_ge(p.ld.sem, p.ld.n)
    with p.block() as blk:
        emit_transposes(p, blk, wo_nat, woT, DB, DB, ident)
    wo_nat.free()

    TA = al("p3_TA", [128, TB, D], F32, side="right")
    bob = al("p3_bob", [128, D], F32, side="right")
    with p.block() as blk:
        @blk.sync
        def _(sy):
            p.ld.inc(sy.dma_start(out=bob[0:1, :], in_=W["s_bout"].rearrange("(a d) -> a d", a=1)), 16)
            sy.wait_ge(p.ld.sem, p.ld.n)
    emit_bcast_f32(p, csb, bob[0:1, :], bob[:], D)
    with p.block() as blk:
        pe0, ev0 = p.pe.n, p.ev.n

        @blk.tensor
        def _(t):
            ji = 0
            for tb in range(TB):
                for ncx in range(2):
                    bank = p.ps4[:, ji % 4, :]
                    if ji >= 4:
                        t.wait_ge(p.ev.sem, ev0 + ji - 3)
                    for kb in range(DB):
                        ins = t.matmul(bank, ybf[:, kb, tb * 128:(tb + 1) * 128],
                                       woT[:, kb, ncx * 512:(ncx + 1) * 512],
                                       start=(kb == 0), stop=(kb == DB - 1))
                    p.pe.inc(ins)
                    ji += 1

        @blk.vector
        def _(v):
            ji = 0
            for tb in range(TB):
                for ncx in range(2):
                    v.wait_ge(p.pe.sem, pe0 + ji + 1)
                    p.ev.inc(v.tensor_tensor(TA[:, tb, ncx * 512:(ncx + 1) * 512],
                                             p.ps4[:, ji % 4, :],
                                             bob[:, ncx * 512:(ncx + 1) * 512], ALU.add))
                    ji += 1

    with p.block() as blk:
        emit_stats(p, blk, TA, TB, sq_scr, stats)
    with p.block() as blk:
        emit_ln_algebra(p, blk, stats, [LN_EPS])
    xfull = al("p3_xt", [128, TB, D], F32, side="right")
    otile = al("p3_ot", [128, 2, D], F32, side="right")
    with p.block() as blk:
        emit_epilogue(p, blk, TA, stats, xin_t, xout_t, xfull, otile, sq_scr,
                      out_stats=True)
    otile.free()
    xfull.free()
    bob.free()
    TA.free()
    woT.free()
    ybf.free()

    Ab.free()


def build_p3_program():
    nc = bass.Bass("TRN2", target_bir_lowering=False)
    xin = nc.dram_tensor("x", [S, D], F32, kind="ExternalInput").ap()
    xout = nc.dram_tensor("y", [S, D], F32, kind="ExternalOutput").ap()
    W = {}
    for k, shp in (("s_Win", [2048, D]), ("s_Wx", [128, D]), ("s_Wdt", [D, 64]),
                   ("s_bdt", [D]), ("s_Wout", [D, D]), ("s_bout", [D]), ("s_Alog", [16])):
        W[k] = nc.dram_tensor(k, shp, F32, kind="ExternalInput").ap()
    Cnp = make_consts()
    W.update(declare_consts(nc, Cnp))
    with contextlib.ExitStack() as stack:
        p = Prog(nc, stack)
        p.emit_const_init()
        csb = load_consts_block(p, W)
        emit_p3(p, xin, xout, W, csb, standalone=True)
    return nc, Cnp




def emit_p4(p, xin, xout, W, csb, standalone=False):
    nc = p.nc
    xin_t = xin.rearrange("(tb p) d -> p tb d", p=128)
    xout_t = xout.rearrange("(tb p) d -> p tb d", p=128)
    ident = csb["ident"]
    al = p.alloc
    E4 = 8
    HID = 2048
    HH = 1024

    b2s_d = nc.dram_tensor("p4_b2s", [D], F32, kind="Internal").ap()

    stats = p.stats
    sq_scr = p.sqscr
    TA = al("p4_TA", [128, TB, D], F32)
    ngc = al("p4_ng", [128, TB], F32)
    gwT = al("p4_gwT", [128, DB, E4], BF16)
    gbb = al("p4_gbb", [128, E4], F32)
    rwb = al("p4_rwb", [128, E4], F32)
    b2sb = al("p4_b2sb", [128, D], F32)
    b1c = al("p4_b1c", [128, 16], F32)
    b1c2 = al("p4_b1c2", [128, 16], F32)
    xn_fm = al("p4_xnfm", [128, DB, S], BF16)

    xn_tm = al("p4_xntm", [128, TB, D], BF16, side="right")
    xtile = al("p4_xtile", [128, 2, D], F32, side="right")
    if standalone:
        with p.block() as blk:
            emit_ln_stats_stream(p, blk, xin_t, stats, sq_scr, xtile)
    with p.block() as blk:
        emit_ln_algebra(p, blk, stats, [LN_EPS])
    with p.block() as blk:
        def applyA(v, tb, xs):
            p.ev.inc(v.tensor_scalar(xn_tm[:, tb, :], xs,
                                     stats[:, C_NMU + tb:C_NMU + tb + 1],
                                     stats[:, C_R + tb:C_R + tb + 1], ALU.add, ALU.mult))
        emit_apply_stream(p, blk, xin_t, xtile, applyA)
    xtile.free()
    with p.block() as blk:
        emit_transposes(p, blk, xn_tm, xn_fm, TB, DB, ident)
    xn_tm.free()

    b2row = al("p4_b2row", [1, E4, D], F32, side="right")
    b2srow = al("p4_b2srow", [1, D], F32, side="right")
    with p.block() as blk:
        @blk.gpsimd
        def _(g):
            gsrc = W["l_gw"].rearrange("e (kb q) -> q kb e", q=128)
            for kb in range(DB):
                p.lw.inc(g.dma_start(out=gwT[:, kb, :], in_=gsrc[:, kb, :],
                                     allow_slow_non_contiguous=True), 16)
            g.wait_ge(p.lw.sem, p.lw.n)

        @blk.sync
        def _(sy):
            p.ld.inc(sy.dma_start(out=gbb[0:1, :], in_=W["l_gb"].rearrange("(a d) -> a d", a=1)), 16)
            p.ld.inc(sy.dma_start(out=rwb[0:1, :], in_=W["l_rw"].rearrange("(a d) -> a d", a=1)), 16)
            p.ld.inc(sy.dma_start(out=b2row[:], in_=W["l_b2"].unsqueeze(0),
                                  allow_slow_non_contiguous=True), 16)
            sy.wait_ge(p.ld.sem, p.ld.n)
    emit_bcast_f32(p, csb, gbb[0:1, :], gbb[:], E4)
    emit_bcast_f32(p, csb, rwb[0:1, :], rwb[:], E4)
    with p.block() as blk:
        emit_seq(p, blk, [
            ("vector", lambda v: v.tensor_reduce(
                b2srow[0:1, :].unsqueeze(2),
                b2row[0:1].transpose([0, 2, 1]), AX.X, ALU.add)),
        ])
    emit_bcast_f32(p, csb, b2srow[0:1, :], b2sb[:], D)
    b2srow.free()
    b2row.free()

    logits = al("p4_logits", [128, TB, E4], F32, side="right")
    mscr = al("p4_mscr", [128, 3, TB, E4], F32, side="right")
    with p.block() as blk:
        pe0, ev0 = p.pe.n, p.ev.n

        @blk.tensor
        def _(t):
            for tb in range(TB):
                bank = p.ps4[:, tb % 2, :]
                if tb >= 2:
                    t.wait_ge(p.ev.sem, ev0 + tb - 1)
                for kb in range(DB):
                    ins = t.matmul(bank[:, 0:E4], xn_fm[:, kb, tb * 128:(tb + 1) * 128],
                                   gwT[:, kb, :], start=(kb == 0), stop=(kb == DB - 1))
                p.pe.inc(ins)

        @blk.vector
        def _(v):
            gb3 = gbb[:].unsqueeze(1)
            for tb in range(TB):
                v.wait_ge(p.pe.sem, pe0 + tb + 1)
                p.ev.inc(v.tensor_tensor(logits[:, tb:tb + 1, :], p.ps4[:, tb % 2, 0:E4].unsqueeze(1),
                                         gb3, ALU.add))

    with p.block() as blk:
        ms = stats[:, C_T4:C_T4 + TB]
        m1 = stats[:, C_T5:C_T5 + TB]
        em1 = stats[:, C_T6:C_T6 + TB]
        ssum = stats[:, C_T7:C_T7 + TB]
        sq3 = mscr[:, 0]
        l2 = mscr[:, 1]
        e_ = mscr[:, 2]

        def b3(ap2d):
            return ap2d.unsqueeze(2).broadcast_to([128, TB, E4])

        steps = [
            ("scalar", lambda s_: s_.activation(sq3, logits[:], AF.Square)),
            ("vector", lambda v: v.tensor_reduce(ms.unsqueeze(2), sq3, AX.X, ALU.add)),
            ("vector", lambda v: v.tensor_scalar(ms, ms, 1.0 / E4, None, ALU.mult)),
            ("scalar", lambda s_: s_.activation(ms, ms, AF.Ln, bias=p.cval(RMS_EPS))),
            ("scalar", lambda s_: s_.activation(ms, ms, AF.Exp, scale=p.cval(-0.5))),
            ("vector", lambda v: v.tensor_tensor(l2, logits[:], b3(ms), ALU.mult)),
            ("vector", lambda v: v.tensor_tensor(l2, l2, rwb[:].unsqueeze(1).broadcast_to([128, TB, E4]), ALU.mult)),
            ("vector", lambda v: v.tensor_reduce(m1.unsqueeze(2), l2, AX.X, ALU.max)),
            ("vector", lambda v: v.tensor_tensor(l2, l2, b3(m1), ALU.subtract)),
            ("scalar", lambda s_: s_.activation(e_, l2, AF.Exp, scale=p.cval(2.0))),
            ("vector", lambda v: v.tensor_reduce(em1.unsqueeze(2), e_, AX.X, ALU.max)),
            ("vector", lambda v: v.tensor_reduce(ssum.unsqueeze(2), e_, AX.X, ALU.add)),
            ("vector", lambda v: v.reciprocal(ssum, ssum)),
            ("vector", lambda v: v.tensor_tensor(em1, em1, ssum, ALU.mult)),
            ("vector", lambda v: v.tensor_scalar(ssum, em1, 1e-6, None, ALU.add)),
            ("vector", lambda v: v.reciprocal(ssum, ssum)),
            ("vector", lambda v: v.tensor_tensor(ngc[:], em1, ssum, ALU.mult)),
        ]
        emit_seq(p, blk, steps)
    mscr.free()
    logits.free()

    h_fm = al("p4_hfm", [128, HH // 128, S], BF16, side="right")
    w1a = al("p4_w1a", [128, DB, HH], BF16, side="right")
    w1b = al("p4_w1b", [128, DB, HH], BF16, side="right")
    w2a = al("p4_w2a", [128, HH // 128, D], BF16, side="right")
    w2b = al("p4_w2b", [128, HH // 128, D], BF16, side="right")
    halves = [(e, hf) for e in range(E4) for hf in range(2)]
    half_ctr = lambda hi: p.lw if hi % 2 == 0 else p.lw2
    half_target = {}
    with p.block() as blk:
        @blk.gpsimd
        def _(g):
            emit_loadw2(p, g, half_ctr(0), w1a, W["l_w1"][0, :, 0:HH])
            emit_loadw2(p, g, half_ctr(0), w2a, W["l_w2"][0, 0:HH, :])
            half_ctr(0).inc(g.dma_start(out=b1c[:],
                                        in_=W["l_b1"][0, :].rearrange("(b q) -> q b", q=128),
                                        allow_slow_non_contiguous=True), 16)
            half_target[0] = half_ctr(0).n
    for hi, (e, hf) in enumerate(halves):
        w1 = w1a if hi % 2 == 0 else w1b
        w2 = w2a if hi % 2 == 0 else w2b
        w1n = w1b if hi % 2 == 0 else w1a
        w2n = w2b if hi % 2 == 0 else w2a
        hc = half_ctr(hi)
        with p.block() as blk:
            pe0, ev0, es0 = p.pe.n, p.ev.n, p.es.n
            hjobs = [(hb, tc) for hb in range(HH // 128) for tc in range(4)]
            yjobs = [(tb, ncx) for tb in range(TB) for ncx in range(2)]
            bank_free = {b: None for b in range(4)}
            plan = []
            es_cnt, ev_cnt = es0, ev0
            for ji in range(len(hjobs)):
                b = ji % 4
                plan.append((b, bank_free[b]))
                es_cnt += 1
                bank_free[b] = ("es", es_cnt)
            for yi in range(len(yjobs)):
                b = (len(hjobs) + yi) % 4
                plan.append((b, bank_free[b]))
                ev_cnt += 1
                bank_free[b] = ("ev", ev_cnt)

            @blk.gpsimd
            def _(g):
                if hi + 1 < len(halves):
                    en, hn = halves[hi + 1]
                    nctr = half_ctr(hi + 1)
                    g.wait_ge(p.pe.sem, pe0 + len(hjobs))
                    emit_loadw2(p, g, nctr, w1n, W["l_w1"][en, :, hn * HH:(hn + 1) * HH])
                    emit_loadw2(p, g, nctr, w2n, W["l_w2"][en, hn * HH:(hn + 1) * HH, :])
                    if hn == 0:
                        b1dst = b1c if en % 2 == 0 else b1c2
                        nctr.inc(g.dma_start(out=b1dst[:],
                                             in_=W["l_b1"][en, :].rearrange("(b q) -> q b", q=128),
                                             allow_slow_non_contiguous=True), 16)
                    half_target[hi + 1] = nctr.n

            def bwait(t, w):
                if w is not None:
                    t.wait_ge(p.es.sem if w[0] == "es" else p.ev.sem, w[1])

            @blk.tensor
            def _(t):
                t.wait_ge(hc.sem, half_target[hi])
                for ji, (hb, tc) in enumerate(hjobs):
                    b, w = plan[ji]
                    bwait(t, w)
                    for kb in range(DB):
                        ins = t.matmul(p.ps4[:, b, :], w1[:, kb, hb * 128:(hb + 1) * 128],
                                       xn_fm[:, kb, tc * 512:(tc + 1) * 512],
                                       start=(kb == 0), stop=(kb == DB - 1))
                    p.pe.inc(ins)
                for yi, (tb, ncx) in enumerate(yjobs):
                    b, w = plan[len(hjobs) + yi]
                    bwait(t, w)
                    for kb in range(HH // 128):
                        ins = t.matmul(p.ps4[:, b, :], h_fm[:, kb, tb * 128:(tb + 1) * 128],
                                       w2[:, kb, ncx * 512:(ncx + 1) * 512],
                                       start=(kb == 0), stop=(kb == HH // 128 - 1))
                    p.pe.inc(ins)

            @blk.scalar
            def _(s_):
                b1u = b1c if e % 2 == 0 else b1c2
                for ji, (hb, tc) in enumerate(hjobs):
                    b = plan[ji][0]
                    s_.wait_ge(p.pe.sem, pe0 + ji + 1)
                    p.es.inc(s_.activation(h_fm[:, hb, tc * 512:(tc + 1) * 512],
                                           p.ps4[:, b, :], AF.Gelu,
                                           bias=b1u[:, hf * 8 + hb:hf * 8 + hb + 1]))

            @blk.vector
            def _(v):
                for yi, (tb, ncx) in enumerate(yjobs):
                    b = plan[len(hjobs) + yi][0]
                    v.wait_ge(p.pe.sem, pe0 + len(hjobs) + yi + 1)
                    sl_ = TA[:, tb, ncx * 512:(ncx + 1) * 512]
                    if hi == 0:
                        p.ev.inc(v.tensor_tensor(sl_, p.ps4[:, b, :],
                                                 b2sb[:, ncx * 512:(ncx + 1) * 512], ALU.add))
                    else:
                        p.ev.inc(v.tensor_tensor(sl_, sl_, p.ps4[:, b, :], ALU.add))
    w2b.free()
    w2a.free()
    w1b.free()
    w1a.free()
    h_fm.free()

    with p.block() as blk:
        @blk.vector
        def _(v):
            for tb in range(TB):
                p.ev.inc(v.tensor_scalar(TA[:, tb, :], TA[:, tb, :],
                                         ngc[:, tb:tb + 1], None, ALU.mult))
    with p.block() as blk:
        emit_stats(p, blk, TA, TB, sq_scr, stats)
    with p.block() as blk:
        emit_ln_algebra(p, blk, stats, [LN_EPS])
    xfull = al("p4_xt", [128, TB, D], F32, side="right")
    otile = al("p4_ot", [128, 2, D], F32, side="right")
    with p.block() as blk:
        emit_epilogue(p, blk, TA, stats, xin_t, xout_t, xfull, otile, sq_scr)
    otile.free()
    xfull.free()

    for b in (xn_fm, b1c2, b1c, b2sb, rwb, gbb, gwT, ngc, TA):
        b.free()


def build_p4_program():
    nc = bass.Bass("TRN2", target_bir_lowering=False)
    xin = nc.dram_tensor("x", [S, D], F32, kind="ExternalInput").ap()
    xout = nc.dram_tensor("y", [S, D], F32, kind="ExternalOutput").ap()
    W = {}
    for k, shp in (("l_gw", [8, D]), ("l_gb", [8]), ("l_rw", [8]),
                   ("l_w1", [8, D, 2048]), ("l_b1", [8, 2048]),
                   ("l_w2", [8, 2048, D]), ("l_b2", [8, D])):
        W[k] = nc.dram_tensor(k, shp, F32, kind="ExternalInput").ap()
    Cnp = make_consts()
    W.update(declare_consts(nc, Cnp))
    with contextlib.ExitStack() as stack:
        p = Prog(nc, stack)
        p.emit_const_init()
        csb = load_consts_block(p, W)
        emit_p4(p, xin, xout, W, csb, standalone=True)
    return nc, Cnp



def load_consts_block(p, C_dram):
    d = {}
    d["ident"] = p.sb("cs_ident", [128, 128], BF16)
    d["ones1x128"] = p.sb("cs_ones", [1, 128], BF16)
    d["onesf32"] = p.sb("cs_onesf", [1, 128], F32)
    d["onesp"] = p.sb("cs_onesp", [128, 64], F32)
    d["onespb"] = p.sb("cs_onespb", [128, 64], BF16)
    with p.block() as blk:
        @blk.sync
        def _(sy):
            p.ld.inc(sy.dma_start(out=d["ident"][:], in_=C_dram["c_ident"]), 16)
            sy.wait_ge(p.ld.sem, p.ld.n)

        @blk.gpsimd
        def _(g):
            g.memset(d["ones1x128"][:], 1.0)
            g.memset(d["onesf32"][:], 1.0)
            g.memset(d["onesp"][:], 1.0)
            g.memset(d["onespb"][:], 1.0)
    return d


def declare_consts(nc, Cnp):
    return {k: nc.dram_tensor(k, list(v.shape), CONST_DT[v.dtype], kind="ExternalInput").ap()
            for k, v in Cnp.items()}


def build_p1_program():
    nc = bass.Bass("TRN2", target_bir_lowering=False)
    xin = nc.dram_tensor("x", [S, D], F32, kind="ExternalInput").ap()
    xout = nc.dram_tensor("y", [S, D], F32, kind="ExternalOutput").ap()
    W = {k: nc.dram_tensor(k, shp, F32, kind="ExternalInput").ap()
         for k, shp in (("Wq", [D, D]), ("Wk", [D, D]), ("Wv", [D, D]), ("Wo", [D, D]),
                        ("gamma", [D]), ("beta", [D]))}
    Cnp = make_consts()
    C_dram = declare_consts(nc, Cnp)
    W.update(C_dram)
    with contextlib.ExitStack() as stack:
        p = Prog(nc, stack)
        p.emit_const_init()
        csb = load_consts_block(p, C_dram)
        emit_p1(p, xin, xout, W, csb)
    return nc, Cnp


STAGE_W = {
    1: lambda ins: {"Wq": ins["L0_attn_Wq"], "Wk": ins["L0_attn_Wk"],
                    "Wv": ins["L0_attn_Wv"], "Wo": ins["L0_attn_Wo"],
                    "gamma": ins["L0_attn_ln_gamma"], "beta": ins["L0_attn_ln_beta"]},
    2: lambda ins: {"f_wq": ins["L0_ffn_wq"], "f_wk": ins["L0_ffn_wk"],
                    "f_wv": ins["L0_ffn_wv"], "f_wo": ins["L0_ffn_wo"],
                    "f_g1": ins["L0_ffn_g1"], "f_g2": ins["L0_ffn_g2"],
                    "f_gate": ins["L0_ffn_gate"], "f_w1": ins["L0_ffn_w1"],
                    "f_w2": ins["L0_ffn_w2"], "f_w3": ins["L0_ffn_w3"]},
    3: lambda ins: {"s_Win": ins["L1_attn_W_in"], "s_Wx": ins["L1_attn_W_x"],
                    "s_Wdt": ins["L1_attn_W_dt"], "s_bdt": ins["L1_attn_b_dt"],
                    "s_Wout": ins["L1_attn_W_out"], "s_bout": ins["L1_attn_b_out"],
                    "s_Alog": ins["L1_attn_A_log"]},
    4: lambda ins: {"l_gw": ins["L1_ffn_gate_w"], "l_gb": ins["L1_ffn_gate_b"],
                    "l_rw": ins["L1_ffn_rms_w"], "l_w1": ins["L1_ffn_w1"],
                    "l_b1": ins["L1_ffn_b1"], "l_w2": ins["L1_ffn_w2"],
                    "l_b2": ins["L1_ffn_b2"]},
}
STAGE_BUILD = {}


def build_full_program():
    nc = bass.Bass("TRN2", target_bir_lowering=False)
    xin = nc.dram_tensor("x", [S, D], F32, kind="ExternalInput").ap()
    xout = nc.dram_tensor("y", [S, D], F32, kind="ExternalOutput").ap()
    r1 = nc.dram_tensor("fr1", [S, D], F32, kind="Internal").ap()
    r2 = nc.dram_tensor("fr2", [S, D], F32, kind="Internal").ap()
    r3 = nc.dram_tensor("fr3", [S, D], F32, kind="Internal").ap()
    W = {}
    for k, shp in (("Wq", [D, D]), ("Wk", [D, D]), ("Wv", [D, D]), ("Wo", [D, D]),
                   ("gamma", [D]), ("beta", [D]),
                   ("f_wq", [D, D]), ("f_wk", [D, 256]), ("f_wv", [D, 256]),
                   ("f_wo", [D, D]), ("f_g1", [D]), ("f_g2", [D]), ("f_gate", [D, 4]),
                   ("f_w1", [4, D, 3072]), ("f_w2", [4, 3072, D]), ("f_w3", [4, D, 3072]),
                   ("s_Win", [2048, D]), ("s_Wx", [128, D]), ("s_Wdt", [D, 64]),
                   ("s_bdt", [D]), ("s_Wout", [D, D]), ("s_bout", [D]), ("s_Alog", [16]),
                   ("l_gw", [8, D]), ("l_gb", [8]), ("l_rw", [8]),
                   ("l_w1", [8, D, 2048]), ("l_b1", [8, 2048]),
                   ("l_w2", [8, 2048, D]), ("l_b2", [8, D])):
        W[k] = nc.dram_tensor(k, shp, F32, kind="ExternalInput").ap()
    Cnp = make_consts()
    W.update(declare_consts(nc, Cnp))
    with contextlib.ExitStack() as stack:
        p = Prog(nc, stack)
        p.emit_const_init()
        csb = load_consts_block(p, W)
        emit_p1(p, xin, r1, W, csb)
        emit_p2(p, r1, r2, W, csb)
        emit_p3(p, r2, r3, W, csb)
        if DBG:
            p.dump("stats3", p.stats[:, 0:32])
        emit_p4(p, r3, xout, W, csb)
    return nc, Cnp


def kernel_fused(**inputs):
    nc, Cnp = build_full_program()
    x = np.ascontiguousarray(inputs["L0_attn_x"].astype(np.float32))
    base = {}
    for st in (1, 2, 3, 4):
        base.update(STAGE_W[st](inputs))
    base.update(Cnp)
    in_maps = []
    for b in range(8):
        m = dict(base)
        m["x"] = np.ascontiguousarray(x[b])
        in_maps.append(m)
    res = run_bass_kernel_spmd(nc, in_maps, core_ids=list(range(8)))
    return np.stack([r["y"] for r in res.results])


def kernel(**inputs):
    if os.environ.get("KFUSED", "1") == "1":
        return kernel_fused(**inputs)
    nstages = int(os.environ.get("KSTAGES", "4"))
    x = np.ascontiguousarray(inputs["L0_attn_x"].astype(np.float32))
    dbg = []
    for stage in range(1, nstages + 1):
        build = {1: build_p1_program, 2: build_p2_program, 3: build_p3_program,
                 4: build_p4_program}[stage]
        nc, Cnp = build()
        base = STAGE_W[stage](inputs)
        base.update(Cnp)
        in_maps = []
        for b in range(8):
            m = dict(base)
            m["x"] = np.ascontiguousarray(x[b])
            in_maps.append(m)
        res = run_bass_kernel_spmd(nc, in_maps, core_ids=list(range(8)))
        if DBG:
            dbg.append(res.results)
        x = np.stack([r["y"] for r in res.results])
    if DBG:
        kernel.dbg = dbg
    return x
